# version 01
# baseline (speedup 1.0000x reference)
import numpy as np
import ml_dtypes
import concourse.bass as bass
import concourse.mybir as mybir
from concourse.bass_utils import run_bass_kernel_spmd
from contextlib import ExitStack

F32 = mybir.dt.float32
BF16 = mybir.dt.bfloat16
I32 = mybir.dt.int32
AF = mybir.ActivationFunctionType
ALU = mybir.AluOpType
AXX = mybir.AxisListType.X

L, D, NT, ND = 2048, 2048, 16, 16
H, HD, KVH = 16, 128, 4
ISQ = float(128.0 ** -0.5)
ESH = -12.0
EPS = 1e-5
INV_D = 1.0 / 2048.0
TWO_PI_HI = float(np.float32(2 * np.pi))
TWO_PI_LO = float(np.float32(2 * np.pi - np.float64(np.float32(2 * np.pi))))
PI2_HI = float(np.float32(np.pi / 2))
PI2_LO = float(np.float32(np.pi / 2 - np.float64(np.float32(np.pi / 2))))
PI_CLAMP = 3.1415925
PI_F32 = float(np.float32(np.pi))
INV_2PI = float(np.float32(1.0 / (2 * np.pi)))
EXPSC = float(np.float32(-2.0 * np.log(10000.0) / 128.0))


class Sems:

    def __init__(self):
        self.h, self.c = {}, {}
        self.dh, self.dc, self.di = {}, {}, {}

    def add(self, name, handle):
        self.h[name], self.c[name] = handle, 0

    def add_dma(self, name, h0, h1):
        self.dh[name] = [h0, h1]
        self.dc[name] = [0, 0]
        self.di[name] = 0

    def inc(self, inst, name, n=1):
        self.c[name] += n
        inst.then_inc(self.h[name], n)
        return self.c[name]

    def wait(self, eng, name, value=None):
        v = self.c[name] if value is None else value
        if v > 0:
            eng.wait_ge(self.h[name], v)

    def val(self, name):
        return self.c[name]

    def dma(self, eng, name, **kw):
        p = self.di[name] % 2
        self.di[name] += 1
        if self.dc[name][p] > 0:
            eng.wait_ge(self.dh[name][p], self.dc[name][p])
        tr = kw.pop("transpose_call", False)
        if tr:
            inst = eng.dma_start_transpose(kw["out"], kw["in_"])
        else:
            inst = eng.dma_start(**kw)
        self.dc[name][p] += 16
        inst.then_inc(self.dh[name][p], 16)
        return (name, p, self.dc[name][p])

    def dwait(self, eng, mark):
        name, p, v = mark
        eng.wait_ge(self.dh[name][p], v)

    def dwait_all(self, eng, name):
        for p in (0, 1):
            if self.dc[name][p] > 0:
                eng.wait_ge(self.dh[name][p], self.dc[name][p])


def build_nc(upto=4):
    nc = bass.Bass("TRN2", target_bir_lowering=False)

    x_in = nc.dram_tensor("x", [L, D], F32, kind="ExternalInput").ap()
    out_t = nc.dram_tensor("out", [L, D], F32, kind="ExternalOutput").ap()

    wts = {}
    for nm, shp in [
        ("aWq", [D, D]), ("aWk", [D, D]), ("aWv", [D, D]), ("aWo", [D, D]),
        ("abq", [D]), ("abk", [D]), ("abv", [D]), ("abo", [D]),
        ("aBq", [16, D]), ("aBk", [16, D]), ("aBv", [16, D]), ("aBo", [16, D]),
        ("aAq", [D, 16]), ("aAk", [D, 16]), ("aAv", [D, 16]), ("aAo", [D, 16]),
        ("bWq", [D, D]), ("bWk", [512, D]), ("bWv", [512, D]), ("bWo", [D, D]),
        ("bbq", [D]), ("bbk", [512]), ("bbv", [512]), ("bbo", [D]),
        ("fW", [D, D]), ("fb", [D]),
    ]:
        wts[nm] = nc.dram_tensor(nm, shp, F32, kind="ExternalInput").ap()
    dmask_in = nc.dram_tensor("dmaski", [128, 128], F32, kind="ExternalInput").ap()

    scr = {}
    for nm, shp, dt in [
        ("WTqa", [128, ND * D], BF16), ("WTka", [128, ND * D], BF16),
        ("WTva", [128, ND * D], BF16), ("WToa", [128, ND * D], BF16),
        ("WTqb", [128, ND * D], BF16), ("WTob", [128, ND * D], BF16),
        ("WTkb", [128, ND * 512], BF16), ("WTvb", [128, ND * 512], BF16),
        ("WTf", [128, ND * D], BF16),
        ("ATsc", [16, 4 * D], BF16),
        ("Vspa", [NT, 128, H * 132], BF16),
        ("Vspb", [NT, 128, KVH * 132], BF16),
        ("KTbs", [KVH, 128, L], BF16),
        ("OTsc", [H, 128, L], BF16),
        ("x2sp", [NT, 128, D], BF16),
    ]:
        scr[nm] = nc.dram_tensor(nm, shp, dt, kind="Internal").ap()

    ctx = ExitStack()
    with ctx:
        def sbt(name, shape, dtype):
            return ctx.enter_context(nc.sbuf_tensor(name, shape, dtype))

        bigA = sbt("bigA", [128, 32768], BF16)
        bigB = sbt("bigB", [128, 32768], BF16)
        stage32 = sbt("stage32", [128, 4096], F32)
        xnt = sbt("xnt", [128, 4096], BF16)
        vstg = sbt("vstg", [128, 1056], BF16)
        otld = sbt("otld", [128, 4096], BF16)
        k1t = sbt("k1t", [128, 1024], I32)
        aux32 = sbt("aux32", [128, 2048], F32)
        zq = sbt("zq", [16, 2048], BF16)
        zk = sbt("zk", [16, 2048], BF16)
        zv = sbt("zv", [16, 2048], BF16)
        zob = sbt("zob", [16, 256], BF16)
        BTq = sbt("BTq", [128, ND * 16], BF16)
        BTk = sbt("BTk", [128, ND * 16], BF16)
        BTv = sbt("BTv", [128, ND * 16], BF16)
        BTo = sbt("BTo", [128, ND * 16], BF16)
        ATs = sbt("ATs", [16, 2048], BF16)
        rowsB = sbt("rowsB", [3, 2048], BF16)
        rowsA = sbt("rowsA", [3, 128], BF16)
        bcolq = sbt("bcolq", [128, 16], F32)
        bcolk = sbt("bcolk", [128, 16], F32)
        bqbc = sbt("bqbc", [128, 16], F32)
        bkbc = sbt("bkbc", [128, 4], F32)
        fbcol = sbt("fbcol", [128, 16], F32)
        wscol = sbt("wscol", [128, 16], F32)
        wsumc = sbt("wsumc", [128, 16], F32)
        betac = sbt("betac", [128, 16], F32)
        c2hc = sbt("c2hc", [128, 16], BF16)
        c2lc = sbt("c2lc", [128, 16], BF16)
        alc = sbt("alc", [128, 4], BF16)
        sl = sbt("sl", [128, 16], F32)
        cst = sbt("cst", [128, 8], F32)
        dmask = sbt("dmask", [128, 128], BF16)
        identb = sbt("identb", [128, 128], BF16)
        identf = sbt("identf", [128, 128], F32)
        onesf = sbt("onesf", [1, 128], F32)
        onesr = sbt("onesr", [1, 128], BF16)
        iotp = sbt("iotp", [128, 1], F32)
        invd = sbt("invd", [128, 4], F32)
        wnat = sbt("wnat", [128, 2048], BF16)
        wtt = sbt("wtt", [128, 2048], BF16)
        Bnat = otld[0:16, 0:2048]
        brow = rowsB[0:1, :]

        ps_a = ctx.enter_context(nc.psum_tensor("ps_a", [128, 1536], F32))
        ps_b = ctx.enter_context(nc.psum_tensor("ps_b", [128, 1536], F32))
        ps_oa = ctx.enter_context(nc.psum_tensor("ps_oa", [128, 512], F32))
        ps_ob = ctx.enter_context(nc.psum_tensor("ps_ob", [128, 512], F32))
        PG = [ps_a[:, 0:512], ps_a[:, 512:1024], ps_a[:, 1024:1536], ps_b[:, 0:512]]
        KGB = [ps_b[:, 512:1024], ps_b[:, 1024:1536], ps_oa[:, 0:512], ps_ob[:, 0:512]]
        PZ = ps_b[:, 512:1024]
        STRIPLE = [ps_a, ps_b]

        S = Sems()
        for snm in ["te", "ve", "se", "gp"]:
            S.add(snm, ctx.enter_context(nc.semaphore(snm)))
        for snm in ["gld", "sld", "tr", "st", "glx", "grs"]:
            S.add_dma(snm, ctx.enter_context(nc.semaphore(snm + "0")),
                      ctx.enter_context(nc.semaphore(snm + "1")))
        block = ctx.enter_context(nc.Block())

        ET = [bigB[:, 0:8192], bigB[:, 8192:16384]]
        QTh = bigB[:, 16384:18432]
        KTh = bigB[:, 18432:20480]
        Vh = bigB[:, 20480:22592]
        OSg = bigB[:, 22592:24640]
        OTt = bigB[:, 24640:26688]
        WS = [bigB[:, 26688:28736], bigB[:, 28736:30784]]

        def b3(ap, w):
            return ap.rearrange("p (n w) -> p n w", w=w)

        def emit(engname, fn):
            getattr(block, engname)(fn)

        p0_marks = []

        def p0a_g(g):
            g.iota(iotp[:, :], pattern=[[0, 1]], base=0, channel_multiplier=1,
                   allow_small_or_imprecise_dtypes=True)
            S.inc(g.iota(stage32[:, 0:128], pattern=[[1, 128]], base=0,
                         channel_multiplier=0,
                         allow_small_or_imprecise_dtypes=True), "gp", 1)
            p0_marks.append(S.dma(g, "gld", out=dmask[:, :], in_=dmask_in))
            for dst, src, w in [(bcolq, wts["abq"], 16), (bcolk, wts["abk"], 16),
                                (bqbc, wts["bbq"], 16), (bkbc, wts["bbk"], 4),
                                (fbcol, wts["fb"], 16)]:
                p0_marks.append(S.dma(g, "gld", out=dst[:, 0:w],
                                      in_=src.rearrange("(a b) -> b a", b=128),
                                      allow_slow_non_contiguous=True))
        emit("gpsimd", p0a_g)

        def p0a_v(v):
            v.memset(cst[:, 0:1], ESH)
            v.memset(cst[:, 1:2], 128.0)
            v.memset(cst[:, 2:3], EPS)
            v.memset(cst[:, 3:4], -127.5)
            v.memset(cst[:, 4:5], 0.0)
            v.memset(onesr[0:1, :], 1.0)
            v.memset(rowsA[0:3, :], 1.0)
            v.memset(vstg[:, :], 1.0)
            for mk in p0_marks:
                S.dwait(v, mk)
            S.wait(v, "gp", 1)
            v.memset(onesf[0:1, :], 1.0)
            v.tensor_scalar(identf[:, :], stage32[:, 0:128], iotp[:, 0:1],
                            None, ALU.is_equal)
            S.inc(v.tensor_scalar(identb[:, :], stage32[:, 0:128], iotp[:, 0:1],
                                  None, ALU.is_equal), "ve", 1)
        emit("vector", p0a_v)
        ve_consts = S.val("ve")

        at_store_marks = []
        for i, nm in enumerate(["aAq", "aAk", "aAv", "aAo"]):
            ga_mark = []

            def g_a(g, i=i, nm=nm):
                if i > 0:
                    S.dwait(g, at_store_marks[-1])
                ga_mark.append(S.dma(
                    g, "gld",
                    out=b3(wnat[:, 0:256], 16),
                    in_=wts[nm].rearrange("(ot p) r -> p ot r", p=128)))
            emit("gpsimd", g_a)

            for ot in range(16):
                def t_a(t, ot=ot, vp=S.val("ve")):
                    if ot == 0:
                        S.dwait(t, ga_mark[0])
                        S.wait(t, "ve", ve_consts)
                    else:
                        S.wait(t, "ve", vp)
                    S.inc(t.matmul(PZ[0:16, 0:128],
                                   wnat[:, ot * 16:(ot + 1) * 16],
                                   identb[:, :], start=True, stop=True), "te", 1)
                emit("tensor", t_a)
                te_a = S.val("te")

                def v_a(v, ot=ot, te_a=te_a):
                    S.wait(v, "te", te_a)
                    S.inc(v.tensor_scalar(ATs[:, ot * 128:(ot + 1) * 128],
                                          PZ[0:16, 0:128], 0.5, None, ALU.mult),
                          "ve", 1)
                emit("vector", v_a)
            ve_a = S.val("ve")

            def s_at(sy, i=i, ve_a=ve_a):
                S.wait(sy, "ve", ve_a)
                at_store_marks.append(S.dma(
                    sy, "st", out=scr["ATsc"][:, i * D:(i + 1) * D],
                    in_=ATs[:, :]))
            emit("sync", s_at)
        at_done = list(at_store_marks)

        bld_marks, btr_marks = [], []
        for i, bt in enumerate([BTq, BTk, BTv, BTo]):
            def g_b(g, i=i):
                if i > 0:
                    S.dwait(g, btr_marks[-1])
                bld_marks.append(S.dma(
                    g, "gld", out=Bnat,
                    in_=wts[["aBq", "aBk", "aBv", "aBo"][i]]))
            emit("gpsimd", g_b)

            def s_b(sy, i=i, bt=bt):
                S.dwait(sy, bld_marks[-1])
                btr_marks.append(S.dma(sy, "tr", transpose_call=True,
                                       out=b3(bt, 16), in_=Bnat))
            emit("sync", s_b)
        bt_done = list(btr_marks)

        pw_ld, pw_tr, pw_st = [], [], []
        wprep_st = {}

        def prep_weight(wname, dramT, nrows):
            nt = nrows // 128
            for i in range(nt):
                def g_fn(g, i=i, wname=wname):
                    if pw_tr:
                        S.dwait(g, pw_tr[-1])
                    pw_ld.append(S.dma(g, "gld", out=wnat[:, :],
                                       in_=wts[wname][i * 128:(i + 1) * 128, :]))
                emit("gpsimd", g_fn)

                def s_fn(sq, i=i):
                    S.dwait(sq, pw_ld[-1])
                    if pw_st:
                        S.dwait(sq, pw_st[-1])
                    pw_tr.append(S.dma(sq, "tr", transpose_call=True,
                                       out=b3(wtt, 128), in_=wnat[:, :]))
                emit("scalar", s_fn)

                def q_fn(sq, i=i, dramT=dramT, nrows=nrows):
                    S.dwait(sq, pw_tr[-1])
                    pw_st.append(S.dma(
                        sq, "st",
                        out=dramT.rearrange("p (dc o) -> p dc o", o=nrows)[
                            :, :, i * 128:(i + 1) * 128],
                        in_=b3(wtt, 128)))
                emit("scalar", q_fn)
            wprep_st[wname] = pw_st[-1]

        prep_weight("aWv", scr["WTva"], D)
        prep_weight("aWq", scr["WTqa"], D)
        prep_weight("aWk", scr["WTka"], D)
        prep_weight("aWo", scr["WToa"], D)
        if upto >= 3:
            prep_weight("bWk", scr["WTkb"], 512)
            prep_weight("bWv", scr["WTvb"], 512)
            prep_weight("bWq", scr["WTqb"], D)
            prep_weight("bWo", scr["WTob"], D)

        def emit_ln_tile(src_ap, dst_ap, sq_ap, pre_mark=None, dst_free=None,
                         pre_ve=None, dst_free_ve=None):
            def v1(v):
                if pre_mark:
                    for mk in pre_mark:
                        S.dwait(v, mk)
                if pre_ve:
                    S.wait(v, "ve", pre_ve)
                rs = S.inc(v.reduce_sum(sl[:, 0:1], src_ap, axis=AXX), "ve", 1)
                S.wait(v, "ve", rs)
                S.inc(v.tensor_scalar(sl[:, 2:3], sl[:, 0:1], INV_D, None, ALU.mult),
                      "ve", 1)
            emit("vector", v1)
            ve_mean = S.val("ve")

            def s1(se):
                if pre_mark:
                    for mk in pre_mark:
                        S.dwait(se, mk)
                if pre_ve:
                    S.wait(se, "ve", pre_ve)
                S.inc(se.activation(sq_ap, src_ap, AF.Square,
                                    accum_out=sl[:, 1:2]), "se", 1)
            emit("scalar", s1)
            se_sq = S.val("se")

            def v2(v):
                S.wait(v, "se", se_sq)
                S.wait(v, "ve", ve_mean)
                m2 = S.inc(v.tensor_tensor(sl[:, 4:5], sl[:, 2:3], sl[:, 2:3],
                                           ALU.mult), "ve", 1)
                S.wait(v, "ve", m2)
                S.inc(v.scalar_tensor_tensor(sl[:, 5:6], sl[:, 1:2], INV_D,
                                             sl[:, 4:5], ALU.mult, ALU.subtract),
                      "ve", 1)
            emit("vector", v2)
            ve_var = S.val("ve")

            def s2(se):
                S.wait(se, "ve", ve_var)
                S.inc(se.activation(sl[:, 6:7], sl[:, 5:6], AF.Sqrt,
                                    bias=cst[:, 2:3], scale=1.0), "se", 1)
            emit("scalar", s2)
            se_sd = S.val("se")

            def v3(v):
                S.wait(v, "se", se_sd)
                rc = S.inc(v.reciprocal(sl[:, 7:8], sl[:, 6:7]), "ve", 1)
                S.wait(v, "ve", rc)
                S.inc(v.tensor_scalar(sl[:, 8:9], sl[:, 2:3], sl[:, 7:8], -1.0,
                                      ALU.mult, ALU.mult), "ve", 1)
            emit("vector", v3)
            ve_nm = S.val("ve")

            def s3(se):
                S.wait(se, "ve", ve_nm)
                if dst_free is not None:
                    S.dwait(se, dst_free)
                if dst_free_ve:
                    S.wait(se, "ve", dst_free_ve)
                S.inc(se.activation(dst_ap, src_ap, AF.Identity,
                                    bias=sl[:, 8:9], scale=sl[:, 7:8]), "se", 1)
            emit("scalar", s3)
            return S.val("se")

        def emit_ln_to_xnt(src_kind, xnt_dst, sq_ap, ld_halves,
                           src_ready=None, sq_free=None, dst_free_marks=None):
            prev_se = [0, 0]
            tr_marks = []
            for tt in range(NT):
                half = tt % 2
                ldt, ldw = ld_halves
                ld_buf = ldt[:, half * ldw:(half + 1) * ldw]
                src = (x_in[tt * 128:(tt + 1) * 128, :] if src_kind == "x0"
                       else scr["x2sp"][tt, :, :])

                lm = []

                def g_ld(g, ld_buf=ld_buf, src=src, tt=tt, pse=prev_se[half]):
                    if tt < 2:
                        S.wait(g, "ve", ve_consts)
                    else:
                        S.wait(g, "se", pse)
                    if src_ready is not None:
                        S.dwait(g, src_ready[tt])
                    lm.append(S.dma(g, "glx", out=ld_buf, in_=src))
                emit("gpsimd", g_ld)

                se_done = emit_ln_tile(
                    ld_buf, xnt[:, half * 2048:(half + 1) * 2048], sq_ap,
                    pre_mark=(lm + (sq_free or []) if tt < 2 else lm),
                    dst_free=(tr_marks[tt - 2] if tt >= 2 else None))
                prev_se[half] = se_done

                def s_tr(sy, tt=tt, half=half, se_done=se_done):
                    if tt == 0 and dst_free_marks:
                        for mk in dst_free_marks:
                            S.dwait(sy, mk)
                    S.wait(sy, "se", se_done)
                    tr_marks.append(S.dma(
                        sy, "tr", transpose_call=True,
                        out=b3(xnt_dst, 2048)[:, :, tt * 128:(tt + 1) * 128],
                        in_=xnt[:, half * 2048:(half + 1) * 2048]))
                emit("sync", s_tr)
            return tr_marks[-2:]

        def emit_z_all(xnT, tr_xnt_all):
            for i, (bt, zdst) in enumerate(zip([BTq, BTk, BTv], [zq, zk, zv])):
                for tc in range(4):
                    ve_prev = S.val("ve")

                    def t_fn(t, i=i, tc=tc, bt=bt, ve_prev=ve_prev):
                        if i == 0 and tc == 0:
                            for mk in tr_xnt_all:
                                S.dwait(t, mk)
                            for mk in bt_done:
                                S.dwait(t, mk)
                            S.wait(t, "ve", ve_consts)
                        else:
                            S.wait(t, "ve", ve_prev)
                        inst = None
                        for dc in range(ND):
                            inst = t.matmul(
                                PZ[0:16, :], b3(bt, 16)[:, dc, :],
                                xnT[:, dc * 2048 + tc * 512:dc * 2048 + (tc + 1) * 512],
                                start=(dc == 0), stop=(dc == ND - 1))
                        S.inc(inst, "te", 1)
                    emit("tensor", t_fn)
                    te_z = S.val("te")

                    def v_fn(v, zdst=zdst, tc=tc, te_z=te_z):
                        S.wait(v, "te", te_z)
                        S.inc(v.tensor_copy(zdst[:, tc * 512:(tc + 1) * 512],
                                            PZ[0:16, :]), "ve", 1)
                    emit("vector", v_fn)
            return S.val("ve")

        def emit_v_all(xnT, wvT_sb, nheads, vsp, tr_xnt_all, wv_ld_mark,
                       first_ve_wait, brow_gld, zvbuf, av_ready_sld):
            noc = (nheads * 128) // 512
            vst_marks = {}
            vhalf_free = {}
            for tt in range(NT):
                for oc in range(noc):
                    gidx = tt * noc + oc
                    half = gidx % 2

                    def t_fn(t, tt=tt, oc=oc, gidx=gidx):
                        if gidx == 0:
                            for mk in tr_xnt_all:
                                S.dwait(t, mk)
                            S.dwait(t, wv_ld_mark)
                            S.wait(t, "ve", first_ve_wait)
                            S.dwait(t, brow_gld)
                            if av_ready_sld:
                                S.dwait(t, av_ready_sld)
                        if gidx >= 4:
                            S.wait(t, "ve", vhalf_free[gidx - 4])
                        last = None
                        for dc in range(ND):
                            last = t.matmul(
                                PG[gidx % 4],
                                b3(xnT, 2048)[:, dc, tt * 128:(tt + 1) * 128],
                                wvT_sb[:, dc * (nheads * 128) + oc * 512:
                                       dc * (nheads * 128) + (oc + 1) * 512],
                                start=(dc == 0), stop=False)
                        if zvbuf is not None:
                            last = t.matmul(
                                PG[gidx % 4],
                                zvbuf[0:16, tt * 128:(tt + 1) * 128],
                                ATs[0:16, oc * 512:(oc + 1) * 512],
                                start=False, stop=False)
                        last = t.matmul(PG[gidx % 4], onesr[0:1, :],
                                        brow[0:1, oc * 512:(oc + 1) * 512],
                                        start=False, stop=True)
                        S.inc(last, "te", 1)
                    emit("tensor", t_fn)
                    te_v = S.val("te")

                    def v_fn(v, tt=tt, oc=oc, gidx=gidx, half=half, te_v=te_v):
                        S.wait(v, "te", te_v)
                        if gidx >= 2:
                            S.dwait(v, vst_marks[gidx - 2])
                        S.inc(v.tensor_copy(
                            b3(vstg[:, half * 528:(half + 1) * 528], 132)[:, :, 0:128],
                            b3(PG[gidx % 4], 128)), "ve", 1)
                    emit("vector", v_fn)
                    vhalf_free[gidx] = S.val("ve")

                    def s_sp(sy, tt=tt, oc=oc, half=half, vm=vhalf_free[gidx],
                             gidx2=gidx):
                        S.wait(sy, "ve", vm)
                        vst_marks[gidx2] = S.dma(
                            sy, "st",
                            out=vsp[tt, :, oc * 528:(oc + 1) * 528],
                            in_=vstg[:, half * 528:(half + 1) * 528])
                    emit("sync", s_sp)
            return vst_marks

        def emit_projT(xnT, ws_buf, dst, bias_col_ap, lora_at, zbuf,
                       sld_mark, groups, state, rope=False, z_ve=0):
            def t_fn(t):
                for mk in sld_mark:
                    S.dwait(t, mk)
                if z_ve:
                    S.wait(t, "ve", z_ve)
                S.wait(t, "se", state["proj_evac_prev"])
                S.wait(t, "se", max(state["trip_last_exp"]))
                S.wait(t, "ve", max(state["ob_last_evac"]))
                last = None
                for dc in range(ND):
                    for tc in range(4):
                        last = t.matmul(
                            groups[tc], b3(ws_buf, 128)[:, dc, :],
                            xnT[:, dc * 2048 + tc * 512:dc * 2048 + (tc + 1) * 512],
                            start=(dc == 0),
                            stop=(dc == ND - 1 and lora_at is None))
                if lora_at is not None:
                    for tc in range(4):
                        last = t.matmul(
                            groups[tc], ATs[0:16, lora_at:lora_at + 128],
                            zbuf[0:16, tc * 512:(tc + 1) * 512],
                            start=False, stop=True)
                S.inc(last, "te", 1)
            emit("tensor", t_fn)
            te_p = S.val("te")

            def s_fn(se):
                S.wait(se, "te", te_p)
                last = None
                for tc in range(4):
                    last = se.activation(dst[:, tc * 512:(tc + 1) * 512],
                                         groups[tc], AF.Identity,
                                         bias=bias_col_ap, scale=1.0)
                S.inc(last, "se", 1)
            emit("scalar", s_fn)
            state["proj_evac_prev"] = S.val("se")
            if rope:
                def v_r(v, se_p=S.val("se")):
                    S.wait(v, "se", se_p)
                    S.inc(v.tensor_tensor(dst[:, :], dst[:, :], aux32[:, :],
                                          ALU.mult), "ve", 1)
                emit("vector", v_r)
                state["rope_ve"] = S.val("ve")
            return S.val("se")

        def emit_attention_head(qT, kT, vhbuf, etbufs, causal, osg,
                                se_ready, ve_ready, vh_mark, state,
                                kt_mark=None):
            for qc in range(4):
                kt_hi = (qc * 4 + 4) if causal else 16
                ngrp = (kt_hi + 2) // 3
                exp_marks = []
                for ktg in range(ngrp):
                    kts = list(range(3 * ktg, min(3 * ktg + 3, kt_hi)))
                    gsz = len(kts)
                    par = state["sparity"] % 2
                    state["sparity"] += 1
                    trip = STRIPLE[par]
                    need_exp = state["trip_last_exp"][par]

                    def t_s(t, kts=kts, trip=trip, need_exp=need_exp, qc=qc,
                            first=(ktg == 0 and qc == 0)):
                        if first:
                            S.wait(t, "se", se_ready)
                            if ve_ready:
                                S.wait(t, "ve", ve_ready)
                            if kt_mark is not None:
                                S.dwait(t, kt_mark)
                        S.wait(t, "se", need_exp)
                        inst = None
                        for j, kt in enumerate(kts):
                            inst = t.matmul(
                                trip[:, j * 512:(j + 1) * 512],
                                kT[:, kt * 128:(kt + 1) * 128],
                                qT[:, qc * 512:(qc + 1) * 512],
                                start=True, stop=True)
                        S.inc(inst, "te", 1)
                    emit("tensor", t_s)
                    te_s = S.val("te")

                    etdst = etbufs[qc % 2]
                    oread = state["et_last_read"][qc % 2]

                    def s_e(se, te_s=te_s, trip=trip, gsz=gsz, etdst=etdst,
                            ktg=ktg, oread=oread):
                        S.wait(se, "te", te_s)
                        S.wait(se, "te", oread)
                        S.inc(se.activation(
                            etdst[:, (3 * ktg) * 512:(3 * ktg + gsz) * 512],
                            trip[:, 0:gsz * 512], AF.Exp,
                            bias=cst[:, 0:1], scale=ISQ), "se", 1)
                    emit("scalar", s_e)
                    exp_marks.append(S.val("se"))
                    state["trip_last_exp"][par] = S.val("se")

                mask_marks = {}
                if causal:
                    for qt_l in range(4):
                        qt = qc * 4 + qt_l
                        kt = qt
                        gi = kt - qc * 4 * 0
                        gi = kt // 3
                        off = kt * 512 + qt_l * 128

                        def v_m(v, off=off, qc=qc, em=exp_marks[gi]):
                            S.wait(v, "se", em)
                            S.inc(v.tensor_tensor(
                                etbufs[qc % 2][:, off:off + 128],
                                etbufs[qc % 2][:, off:off + 128],
                                dmask[:, :], ALU.mult), "ve", 1)
                        emit("vector", v_m)
                        mask_marks[qt_l] = S.val("ve")

                for qt_l in range(4):
                    qt = qc * 4 + qt_l
                    kts = list(range(qt + 1)) if causal else list(range(16))
                    opar = state["oparity"] % 2
                    state["oparity"] += 1
                    ob = [ps_oa, ps_ob][opar]
                    ev_prev = state["ob_last_evac"][opar]
                    mm = mask_marks.get(qt_l, 0)

                    def t_o(t, qt_l=qt_l, kts=kts, ob=ob, qc=qc, ev_prev=ev_prev,
                            em=exp_marks[-1], mm=mm,
                            first=(qc == 0 and qt_l == 0)):
                        if first and vh_mark is not None:
                            S.dwait(t, vh_mark)
                        S.wait(t, "se", em)
                        if mm:
                            S.wait(t, "ve", mm)
                        S.wait(t, "ve", ev_prev)
                        inst = None
                        for idx, kt in enumerate(kts):
                            inst = t.matmul(
                                ob[:, 0:129],
                                etbufs[qc % 2][:, kt * 512 + qt_l * 128:
                                               kt * 512 + qt_l * 128 + 128],
                                b3(vhbuf, 132)[:, kt, 0:129],
                                start=(idx == 0), stop=(idx == len(kts) - 1))
                        S.inc(inst, "te", 1)
                    emit("tensor", t_o)
                    te_o = S.val("te")
                    state["et_last_read"][qc % 2] = te_o

                    def v_o(v, qt=qt, ob=ob, te_o=te_o):
                        S.wait(v, "te", te_o)
                        rc = S.inc(v.reciprocal(invd[:, qt % 4:qt % 4 + 1],
                                                ob[:, 128:129]), "ve", 1)
                        S.wait(v, "ve", rc)
                        S.inc(v.tensor_scalar(
                            osg[:, qt * 128:(qt + 1) * 128], ob[:, 0:128],
                            invd[:, qt % 4:qt % 4 + 1], None, ALU.mult), "ve", 1)
                    emit("vector", v_o)
                    state["ob_last_evac"][opar] = S.val("ve")
            return S.val("ve")

        def emit_oproj(wtname, wt_st_mark, attn_done_st, bias_gld, lora,
                       resid_kind, resid_buf, dst_bigA, write_x2sp):
            pre_se = S.val("se")
            pre_trm = [("tr", p, S.dc["tr"][p]) for p in (0, 1)
                       if S.dc["tr"][p] > 0]
            wo_marks = []

            def s_wo(sy):
                S.dwait(sy, wt_st_mark)
                S.dwait(sy, attn_done_st)
                wo_marks.append(S.dma(sy, "sld", out=bigB[:, :], in_=scr[wtname]))
                if lora:
                    S.dwait(sy, at_done[3])
                    wo_marks.append(S.dma(sy, "sld", out=ATs[:, :],
                                          in_=scr["ATsc"][:, 3 * D:4 * D]))
            emit("sync", s_wo)

            zo_ve = {}
            prev_otld_reader = [0, 0]
            prev_y_evac = [0]
            for tt in range(NT):
                half = tt % 2

                otm_l = []

                def s_otl(sy, tt=tt, half=half, pe=prev_otld_reader[half]):
                    if tt == 0:
                        for mk in wo_marks:
                            S.dwait(sy, mk)
                    if tt >= 2:
                        S.wait(sy, "te", pe)
                    otm_l.append(S.dma(
                        sy, "sld",
                        out=otld[:, half * 2048:(half + 1) * 2048],
                        in_=scr["OTsc"].rearrange("h p t -> p h t")[
                            :, :, tt * 128:(tt + 1) * 128]))
                emit("sync", s_otl)

                res_l = []

                def g_res(g, tt=tt, half=half, py=prev_y_evac[0]):
                    if tt < 2:
                        S.wait(g, "se", pre_se)
                        for mk in pre_trm:
                            S.dwait(g, mk)
                    else:
                        S.wait(g, "ve", py)
                    src = (x_in[tt * 128:(tt + 1) * 128, :] if resid_kind == "x0"
                           else scr["x2sp"][tt, :, :])
                    res_l.append(S.dma(
                        g, "grs",
                        out=resid_buf[:, half * 2048:(half + 1) * 2048],
                        in_=src))
                emit("gpsimd", g_res)

                if lora:
                    def t_zo(t, tt=tt, half=half,
                             pz=zo_ve.get(tt - 2, 0)):
                        S.dwait(t, otm_l[-1])
                        S.wait(t, "ve", pz)
                        inst = None
                        for oc in range(16):
                            inst = t.matmul(
                                PZ[0:16, 0:128], b3(BTo, 16)[:, oc, :],
                                otld[:, half * 2048 + oc * 128:
                                     half * 2048 + (oc + 1) * 128],
                                start=(oc == 0), stop=(oc == 15))
                        S.inc(inst, "te", 1)
                    emit("tensor", t_zo)
                    te_zo = S.val("te")

                    def v_zo(v, half=half, te_zo=te_zo):
                        S.wait(v, "te", te_zo)
                        S.inc(v.tensor_copy(zob[:, half * 128:(half + 1) * 128],
                                            PZ[0:16, 0:128]), "ve", 1)
                    emit("vector", v_zo)
                    zo_ve[tt] = S.val("ve")

                def t_y(t, tt=tt, half=half,
                        zv_=zo_ve.get(tt, 0), py=prev_y_evac[0]):
                    S.dwait(t, otm_l[-1])
                    if lora:
                        S.wait(t, "ve", zv_)
                    S.wait(t, "ve", py)
                    S.dwait(t, bias_gld)
                    last = None
                    for dch in range(4):
                        for oc in range(16):
                            last = t.matmul(
                                PG[dch],
                                otld[:, half * 2048 + oc * 128:
                                     half * 2048 + (oc + 1) * 128],
                                bigB[:, oc * 2048 + dch * 512:
                                     oc * 2048 + (dch + 1) * 512],
                                start=(oc == 0), stop=False)
                        if lora:
                            last = t.matmul(
                                PG[dch], zob[0:16, half * 128:(half + 1) * 128],
                                ATs[0:16, dch * 512:(dch + 1) * 512],
                                start=False, stop=False)
                        last = t.matmul(
                            PG[dch], onesr[0:1, :],
                            brow[0:1, dch * 512:(dch + 1) * 512],
                            start=False, stop=True)
                    S.inc(last, "te", 1)
                emit("tensor", t_y)
                te_y = S.val("te")
                prev_otld_reader[half] = te_y

                def v_y(v, tt=tt, half=half, te_y=te_y):
                    S.wait(v, "te", te_y)
                    S.dwait(v, res_l[-1])
                    last = None
                    for dch in range(4):
                        src2 = resid_buf[:, half * 2048 + dch * 512:
                                         half * 2048 + (dch + 1) * 512]
                        last = v.tensor_tensor(
                            dst_bigA[:, tt * 2048 + dch * 512:
                                     tt * 2048 + (dch + 1) * 512],
                            PG[dch], src2, ALU.add)
                    S.inc(last, "ve", 1)
                emit("vector", v_y)
                prev_y_evac[0] = S.val("ve")
                ve_y = S.val("ve")

                if write_x2sp:
                    def s_x2(sy, tt=tt, ve_y=ve_y):
                        S.wait(sy, "ve", ve_y)
                        S.dma(sy, "st", out=scr["x2sp"][tt, :, :],
                              in_=dst_bigA[:, tt * 2048:(tt + 1) * 2048])
                    emit("sync", s_x2)
            return S.val("ve"), S.val("te")

        def load_brow(src_name, width=2048):
            mk = []

            def g_b(g):
                S.wait(g, "te")
                mk.append(S.dma(g, "gld", out=brow[0:1, 0:width],
                                in_=wts[src_name].rearrange("(o a) -> o a", o=1)))
            emit("gpsimd", g_b)
            return mk[0]

        brow_gld = load_brow("abv")
        tr_all_a = emit_ln_to_xnt("x0", bigA, otld[:, 2048:4096], (stage32, 2048))
        ve_z = emit_z_all(bigA, tr_all_a)

        wv_marks = []

        def s_wv(sy):
            S.dwait(sy, wprep_st["aWv"])
            wv_marks.append(S.dma(sy, "sld", out=bigB[:, :], in_=scr["WTva"]))
            S.dwait(sy, at_done[2])
            wv_marks.append(S.dma(sy, "sld", out=ATs[:, :],
                                  in_=scr["ATsc"][:, 2 * D:3 * D]))
        emit("sync", s_wv)
        vsp_done = emit_v_all(bigA, bigB, H, scr["Vspa"], tr_all_a,
                              wv_marks[0], ve_z, brow_gld, zv, wv_marks[1])

        state = {"sparity": 0, "oparity": 0,
                 "trip_last_exp": [0, 0], "ob_last_evac": [0, 0],
                 "et_last_read": [0, 0], "proj_evac_prev": 0}
        ot_st = []
        for h in range(H):
            hm = []

            def s_ws(sy, h=h, pe=state["proj_evac_prev"],
                     obm=max(state["ob_last_evac"])):
                if h == 0:
                    S.dwait(sy, wprep_st["aWk"])
                    for mk in vsp_done.values():
                        S.dwait(sy, mk)
                    S.dwait(sy, at_done[1])
                else:
                    S.wait(sy, "se", pe)
                    S.wait(sy, "ve", obm)
                hm.append(S.dma(
                    sy, "sld", out=b3(WS[0], 128),
                    in_=scr["WTqa"].rearrange("p (dc o) -> p dc o", o=D)[
                        :, :, h * 128:(h + 1) * 128]))
                hm.append(S.dma(
                    sy, "sld", out=b3(WS[1], 128),
                    in_=scr["WTka"].rearrange("p (dc o) -> p dc o", o=D)[
                        :, :, h * 128:(h + 1) * 128]))
                hm.append(S.dma(
                    sy, "sld", out=b3(Vh, 132),
                    in_=scr["Vspa"].rearrange("t p (hh w) -> p t hh w", w=132)[
                        :, :, h, :]))
                hm.append(S.dma(
                    sy, "sld", out=ATs[0:16, 0:384],
                    in_=scr["ATsc"].rearrange("r (pj o) -> r pj o", o=D)[
                        :, 0:3, h * 128:(h + 1) * 128]))
            emit("sync", s_ws)

            emit_projT(bigA, WS[0], QTh, bcolq[:, h:h + 1], 0, zq,
                       hm, PG, state, z_ve=ve_z)
            se_k = emit_projT(bigA, WS[1], KTh, bcolk[:, h:h + 1], 128, zk,
                              hm, KGB, state, z_ve=ve_z)

            ve_head = emit_attention_head(QTh, KTh, Vh, ET, False, OSg,
                                          se_k, 0, hm[2], state)

            otr = []

            def s_ot(sy, ve_head=ve_head):
                S.wait(sy, "ve", ve_head)
                if ot_st:
                    S.dwait(sy, ot_st[-1])
                otr.append(S.dma(sy, "tr", transpose_call=True,
                                 out=b3(OTt, 128), in_=OSg[:, :]))
            emit("sync", s_ot)

            def q_ot(sq, h=h):
                S.dwait(sq, otr[0])
                ot_st.append(S.dma(sq, "st", out=scr["OTsc"][h, :, :],
                                   in_=OTt[:, :]))
            emit("scalar", q_ot)
        attn_a_st = ot_st[-1]

        brow_gld = load_brow("abo")
        ve_x1, oproj_a_te = emit_oproj("WToa", wprep_st["aWo"], attn_a_st,
                                       brow_gld, True, "x0", stage32, bigA,
                                       False)



        def emit_ffn_wprep(first_ve):
            wtr = []
            gl_ch_acc = []

            for ot in range(NT):
                fm = []

                def s_ld(sy, ot=ot):
                    if ot < 2:
                        S.wait(sy, "ve", first_ve)
                    else:
                        S.wait(sy, "ve", k1_se[ot - 2])
                    fm.append(S.dma(
                        sy, "sld",
                        out=stage32[:, (ot % 2) * 2048:(ot % 2 + 1) * 2048],
                        in_=wts["fW"][ot * 128:(ot + 1) * 128, :]))
                emit("sync", s_ld)

                def v_q(v, ot=ot):
                    S.dwait(v, fm[-1])
                    r = S.inc(v.reduce_max(sl[:, 9:10],
                                           stage32[:, (ot % 2) * 2048:(ot % 2 + 1) * 2048],
                                           axis=AXX, apply_absolute_value=True),
                              "ve", 1)
                    S.wait(v, "ve", r)
                    r = S.inc(v.tensor_scalar(wscol[:, ot:ot + 1], sl[:, 9:10],
                                              1e-8, 1.0 / 127.0, ALU.max, ALU.mult),
                              "ve", 1)
                    S.wait(v, "ve", r)
                    S.inc(v.reciprocal(sl[:, 11:12], wscol[:, ot:ot + 1]), "ve", 1)
                emit("vector", v_q)
                ve_iw = S.val("ve")

                ch_acc = gl_ch_acc
                for ch in range(2):
                    co = (ot % 2) * 2048 + ch * 1024

                    def s_k1(se, co=co, ve_iw=ve_iw, ot=ot, ch=ch,
                             pva=(ch_acc[-1] if ch_acc else 0)):
                        S.wait(se, "ve", ve_iw)
                        if pva:
                            S.wait(se, "ve", pva)
                        if wtr and ch == 0:
                            S.dwait(se, wtr[-1])
                        S.inc(se.activation(k1t[:, :], stage32[:, co:co + 1024],
                                            AF.Identity, bias=cst[:, 1:2],
                                            scale=sl[:, 11:12]), "se", 1)
                    emit("scalar", s_k1)
                    se_k1 = S.val("se")

                    def v_r(v, co=co, se_k1=se_k1, ch=ch, ot=ot):
                        S.wait(v, "se", se_k1)
                        r = S.inc(v.scalar_tensor_tensor(
                            aux32[:, 0:1024], stage32[:, co:co + 1024],
                            sl[:, 11:12], k1t[:, :], ALU.mult, ALU.subtract),
                            "ve", 1)
                        S.wait(v, "ve", r)
                        r = S.inc(v.scalar_tensor_tensor(
                            aux32[:, 1024:2048], aux32[:, 0:1024], -127.5,
                            k1t[:, :], ALU.is_ge, ALU.add), "ve", 1)
                        S.wait(v, "ve", r)
                        S.inc(v.tensor_scalar(
                            wnat[:, ch * 1024:(ch + 1) * 1024],
                            aux32[:, 1024:2048], -128.0, 0.0, ALU.add, ALU.add,
                            accum_out=sl[:, 12 + ch:13 + ch]), "ve", 1)
                    emit("vector", v_r)
                    ch_acc.append(S.val("ve"))

                def v_s(v, ot=ot, m=ch_acc[-1]):
                    S.wait(v, "ve", m)
                    S.inc(v.tensor_tensor(wsumc[:, ot:ot + 1], sl[:, 12:13],
                                          sl[:, 13:14], ALU.add), "ve", 1)
                emit("vector", v_s)
                ve_wq = S.val("ve")

                def s_tr2(sy, ot=ot, ve_wq=ve_wq):
                    S.wait(sy, "ve", ve_wq)
                    if wtr:
                        S.dwait(sy, wst[-1])
                    wtr.append(S.dma(sy, "tr", transpose_call=True,
                                     out=b3(wtt, 128), in_=wnat[:, :]))
                emit("sync", s_tr2)

                def q_st(sq, ot=ot):
                    S.dwait(sq, wtr[-1])
                    wst.append(S.dma(
                        sq, "st",
                        out=scr["WTf"].rearrange("p (dc o) -> p dc o", o=D)[
                            :, :, ot * 128:(ot + 1) * 128],
                        in_=b3(wtt, 128)))
                emit("scalar", q_st)
                k1_se.append(ve_wq)
            def v_c2(v, pv=S.val("ve")):
                S.wait(v, "ve", pv)
                r = S.inc(v.tensor_scalar(betac[:, :], wsumc[:, :], -128.0, None,
                                          ALU.mult), "ve", 1)
                S.wait(v, "ve", r)
                r = S.inc(v.tensor_copy(c2hc[:, :], betac[:, :]), "ve", 1)
                S.wait(v, "ve", r)
                r = S.inc(v.tensor_tensor(wsumc[:, :], betac[:, :], c2hc[:, :],
                                          ALU.subtract), "ve", 1)
                S.wait(v, "ve", r)
                r = S.inc(v.tensor_copy(c2lc[:, :], wsumc[:, :]), "ve", 1)
                S.wait(v, "ve", r)
                r = S.inc(v.reciprocal(betac[:, :], wscol[:, :]), "ve", 1)
                S.wait(v, "ve", r)
                S.inc(v.tensor_tensor(betac[:, :], betac[:, :], fbcol[:, :],
                                      ALU.mult), "ve", 1)
            emit("vector", v_c2)
            ve_cols = S.val("ve")
            return ve_cols

        def emit_rows_and_wsb(ve_cols):
            for ot in range(NT):
                def v_cp(v, ot=ot, tp=S.val("te")):
                    S.wait(v, "ve", ve_cols)
                    S.wait(v, "te", tp)
                    r = S.inc(v.tensor_copy(alc[:, 1:2], betac[:, ot:ot + 1]),
                              "ve", 1)
                    S.wait(v, "ve", r)
                    r = S.inc(v.tensor_copy(alc[:, 2:3], c2hc[:, ot:ot + 1]),
                              "ve", 1)
                    S.wait(v, "ve", r)
                    S.inc(v.tensor_copy(alc[:, 3:4], c2lc[:, ot:ot + 1]), "ve", 1)
                emit("vector", v_cp)
                ve_cp = S.val("ve")

                def t_m(t, ve_cp=ve_cp):
                    S.wait(t, "ve", ve_cp)
                    S.inc(t.matmul(PZ[0:3, 0:128], alc[:, 1:4], identb[:, :],
                                   start=True, stop=True), "te", 1)
                emit("tensor", t_m)
                te_m = S.val("te")

                def v_row(v, ot=ot, te_m=te_m):
                    S.wait(v, "te", te_m)
                    S.inc(v.tensor_copy(rowsB[0:3, ot * 128:(ot + 1) * 128],
                                        PZ[0:3, 0:128]), "ve", 1)
                emit("vector", v_row)
            for ot in range(NT):
                def t_w(t, ot=ot, vp=S.val("ve")):
                    S.wait(t, "ve", vp)
                    S.inc(t.matmul(PZ[0:1, 128:256], wscol[:, ot:ot + 1],
                                   identf[:, :], start=True, stop=True), "te", 1)
                emit("tensor", t_w)
                te_w = S.val("te")

                def v_w2(v, ot=ot, te_w=te_w):
                    S.wait(v, "te", te_w)
                    S.inc(v.tensor_copy(stage32[0:1, 2048 + ot * 128:2048 + (ot + 1) * 128],
                                        PZ[0:1, 128:256]), "ve", 1)
                emit("vector", v_w2)
            ve_wsr = S.val("ve")
            for oc in range(4):
                def t_b(t, oc=oc, vp=ve_wsr):
                    S.wait(t, "ve", vp)
                    S.inc(t.matmul(PG[oc], onesf[0:1, :],
                                   stage32[0:1, 2048 + oc * 512:2048 + (oc + 1) * 512],
                                   start=True, stop=True), "te", 1)
                emit("tensor", t_b)
                te_b = S.val("te")

                def v_b(v, oc=oc, te_b=te_b):
                    S.wait(v, "te", te_b)
                    S.inc(v.tensor_copy(aux32[:, oc * 512:(oc + 1) * 512],
                                        PG[oc]), "ve", 1)
                emit("vector", v_b)
            return S.val("ve")

        def emit_ffn(first_ve, wqt_ready, final_out):
            xq_tr = []
            mm_te = []
            xq_ve = {}
            xq_chain = []
            ep_ve = [first_ve]
            out_st = []
            for tt in range(NT):
                half = tt % 2
                src = bigA[:, tt * 2048:(tt + 1) * 2048]
                pm0 = None
                if tt < 2:
                    pm0 = [("tr", p, S.dc["tr"][p]) for p in (0, 1)
                           if S.dc["tr"][p] > 0]
                se_ln = emit_ln_tile(src, xnt[:, half * 2048:(half + 1) * 2048],
                                     wnat[:, :], pre_mark=pm0,
                                     pre_ve=(ep_ve[0] if tt < 2 else None),
                                     dst_free=None,
                                     dst_free_ve=(xq_ve[tt - 2] if tt >= 2
                                                  else None))

                def v_qx(v, tt=tt, half=half, se_ln=se_ln):
                    S.wait(v, "se", se_ln)
                    r = S.inc(v.reduce_max(sl[:, 9:10],
                                           xnt[:, half * 2048:(half + 1) * 2048],
                                           axis=AXX, apply_absolute_value=True),
                              "ve", 1)
                    S.wait(v, "ve", r)
                    r = S.inc(v.tensor_scalar(sl[:, 10:11], sl[:, 9:10],
                                              1.0 / 127.0, 1e-12, ALU.mult, ALU.max),
                              "ve", 1)
                    S.wait(v, "ve", r)
                    r = S.inc(v.reciprocal(sl[:, 11:12], sl[:, 10:11]), "ve", 1)
                    S.wait(v, "ve", r)
                    if mm_te:
                        S.wait(v, "te", mm_te[-1])
                    S.inc(v.tensor_copy(alc[:, 0:1], sl[:, 11:12]), "ve", 1)
                emit("vector", v_qx)
                ve_iq = S.val("ve")

                def t_al(t, ve_iq=ve_iq):
                    S.wait(t, "ve", ve_iq)
                    S.inc(t.matmul(PZ[0:1, 0:128], alc[:, 0:1], identb[:, :],
                                   start=True, stop=True), "te", 1)
                emit("tensor", t_al)
                te_al = S.val("te")

                def v_al(v, te_al=te_al):
                    S.wait(v, "te", te_al)
                    S.inc(v.tensor_copy(rowsA[0:1, :], PZ[0:1, 0:128]), "ve", 1)
                emit("vector", v_al)
                ve_al = S.val("ve")

                for ch in range(2):
                    xo = half * 2048 + ch * 1024

                    def s_k1(se, xo=xo, ve_iq=ve_iq, tt=tt, ch=ch,
                             pva=(xq_chain[-1] if xq_chain else 0)):
                        S.wait(se, "ve", ve_iq)
                        if pva:
                            S.wait(se, "ve", pva)
                        if tt > 0 and ch == 0:
                            S.dwait(se, xq_tr[-1])
                        S.inc(se.activation(k1t[:, :], xnt[:, xo:xo + 1024],
                                            AF.Identity, bias=cst[:, 1:2],
                                            scale=sl[:, 11:12]), "se", 1)
                    emit("scalar", s_k1)
                    se_k1 = S.val("se")

                    def v_r(v, xo=xo, se_k1=se_k1, ch=ch):
                        S.wait(v, "se", se_k1)
                        r = S.inc(v.scalar_tensor_tensor(
                            stage32[:, ch * 1024:(ch + 1) * 1024],
                            xnt[:, xo:xo + 1024], sl[:, 11:12], k1t[:, :],
                            ALU.mult, ALU.subtract), "ve", 1)
                        S.wait(v, "ve", r)
                        S.inc(v.scalar_tensor_tensor(
                            otld[:, ch * 1024:(ch + 1) * 1024],
                            stage32[:, ch * 1024:(ch + 1) * 1024], -127.5,
                            k1t[:, :], ALU.is_ge, ALU.add), "ve", 1)
                    emit("vector", v_r)
                    xq_chain.append(S.val("ve"))
                ve_xq = S.val("ve")
                xq_ve[tt] = ve_xq

                def s_xtr(sy, tt=tt, ve_xq=ve_xq):
                    S.wait(sy, "ve", ve_xq)
                    if mm_te:
                        S.wait(sy, "te", mm_te[-1])
                    xq_tr.append(S.dma(sy, "tr", transpose_call=True,
                                       out=b3(otld[:, 2048:4096], 128),
                                       in_=otld[:, 0:2048]))
                emit("sync", s_xtr)

                def t_mm(t, tt=tt, ve_al=ve_al):
                    S.dwait(t, xq_tr[-1])
                    S.wait(t, "ve", ve_al)
                    if tt == 0:
                        for mk in wqt_ready:
                            S.dwait(t, mk)
                    S.wait(t, "ve", ep_ve[0])
                    last = None
                    for dc in range(ND):
                        for oc in range(4):
                            last = t.matmul(
                                PG[oc], b3(otld[:, 2048:4096], 128)[:, dc, :],
                                bigB[:, dc * 2048 + oc * 512:dc * 2048 + (oc + 1) * 512],
                                start=(dc == 0), stop=False)
                    for oc in range(4):
                        last = t.matmul(PG[oc], rowsA[0:3, :],
                                        rowsB[0:3, oc * 512:(oc + 1) * 512],
                                        start=False, stop=True)
                    S.inc(last, "te", 1)
                emit("tensor", t_mm)
                mm_te.append(S.val("te"))

                def v_ep(v, tt=tt, half=half, m=mm_te[-1]):
                    S.wait(v, "te", m)
                    if final_out and tt >= 1:
                        S.dwait(v, out_st[tt - 1])
                    last = None
                    for oc in range(4):
                        r = S.inc(v.scalar_tensor_tensor(
                            stage32[:, 2048 + oc * 512:2048 + (oc + 1) * 512],
                            PG[oc], sl[:, 10:11], aux32[:, oc * 512:(oc + 1) * 512],
                            ALU.mult, ALU.mult), "ve", 1)
                        S.wait(v, "ve", r)
                        last = v.tensor_tensor(
                            (stage32[:, 2048 + oc * 512:2048 + (oc + 1) * 512]
                             if final_out else
                             bigA[:, tt * 2048 + oc * 512:tt * 2048 + (oc + 1) * 512]),
                            stage32[:, 2048 + oc * 512:2048 + (oc + 1) * 512],
                            bigA[:, tt * 2048 + oc * 512:tt * 2048 + (oc + 1) * 512],
                            ALU.add)
                    S.inc(last, "ve", 1)
                emit("vector", v_ep)
                ep_ve[0] = S.val("ve")
                ve_y = S.val("ve")

                if final_out:
                    def s_o(sy, tt=tt, ve_y=ve_y):
                        S.wait(sy, "ve", ve_y)
                        out_st.append(S.dma(sy, "st",
                                            out=out_t[tt * 128:(tt + 1) * 128, :],
                                            in_=stage32[:, 2048:4096]))
                    emit("sync", s_o)
                else:
                    def s_o(sy, tt=tt, ve_y=ve_y):
                        S.wait(sy, "ve", ve_y)
                        out_st.append(S.dma(sy, "st",
                                            out=scr["x2sp"][tt, :, :],
                                            in_=bigA[:, tt * 2048:(tt + 1) * 2048]))
                    emit("sync", s_o)
            return S.val("ve"), out_st, mm_te[-1]

        if upto >= 2:
            k1_se, wst = [], []
            ve_cols = emit_ffn_wprep(ve_x1)
            ve_rows = emit_rows_and_wsb(ve_cols)

            wq_marks = []

            def s_wq(sy):
                S.dwait(sy, wst[-1])
                S.dwait(sy, wst[-2])
                S.wait(sy, "te", oproj_a_te)
                wq_marks.append(S.dma(sy, "sld", out=bigB[:, :], in_=scr["WTf"]))
            emit("sync", s_wq)

            ve_x2, x2_st, ffn1_te = emit_ffn(max(ve_x1, ve_rows), wq_marks, False)

        if upto >= 3:
            def g_tr(g):
                S.wait(g, "ve", ve_x2)
                S.inc(g.iota(stage32[0:1, 0:2048], pattern=[[1, 2048]], base=0,
                             channel_multiplier=0,
                             allow_small_or_imprecise_dtypes=True), "gp", 1)
            emit("gpsimd", g_tr)
            gp_tr = S.val("gp")

            def v_dp(v):
                S.wait(v, "ve", ve_x2)
                r = S.inc(v.tensor_scalar(k1t[:, 0:1], iotp[:, 0:1], 0.015625,
                                          None, ALU.mult), "ve", 1)
                S.wait(v, "ve", r)
                r = S.inc(v.scalar_tensor_tensor(sl[:, 9:10], k1t[:, 0:1], -64.0,
                                                 iotp[:, 0:1], ALU.mult, ALU.add),
                          "ve", 1)
                S.wait(v, "ve", r)
                r = S.inc(v.tensor_scalar(sl[:, 10:11], sl[:, 9:10], 0.0, None,
                                          ALU.is_lt), "ve", 1)
                S.wait(v, "ve", r)
                S.inc(v.scalar_tensor_tensor(sl[:, 11:12], sl[:, 10:11], 64.0,
                                             sl[:, 9:10], ALU.mult, ALU.add),
                      "ve", 1)
            emit("vector", v_dp)
            ve_dp = S.val("ve")

            def s_inv(se):
                S.wait(se, "ve", ve_dp)
                S.inc(se.activation(sl[:, 12:13], sl[:, 11:12], AF.Exp,
                                    bias=cst[:, 4:5], scale=EXPSC), "se", 1)
            emit("scalar", s_inv)
            se_inv = S.val("se")

            def v_invc(v, se_inv=se_inv, tp=S.val("te")):
                S.wait(v, "se", se_inv)
                S.wait(v, "te", tp)
                S.inc(v.tensor_copy(sl[:, 13:14], sl[:, 12:13]), "ve", 1)
            emit("vector", v_invc)
            ve_invc = S.val("ve")

            def t_inv(t, ve_invc=ve_invc):
                S.wait(t, "ve", ve_invc)
                S.inc(t.matmul(PZ[0:1, 128:256], sl[:, 13:14], identf[:, :],
                               start=True, stop=True), "te", 1)
            emit("tensor", t_inv)
            te_inv = S.val("te")

            def v_invr(v, te_inv=te_inv):
                S.wait(v, "te", te_inv)
                S.inc(v.tensor_copy(stage32[0:1, 2048:2176], PZ[0:1, 128:256]),
                      "ve", 1)
            emit("vector", v_invr)
            ve_invr = S.val("ve")

            th_ve = []
            for tc in range(4):
                def t_th(t, tc=tc, ve_invr=ve_invr, gp_tr=gp_tr):
                    S.wait(t, "ve", ve_invr)
                    S.wait(t, "gp", gp_tr)
                    if tc >= 1:
                        S.wait(t, "ve", th_ve[-1])
                    S.inc(t.matmul(PG[tc], stage32[0:1, 2048:2176],
                                   stage32[0:1, tc * 512:(tc + 1) * 512],
                                   start=True, stop=True), "te", 1)
                emit("tensor", t_th)
                te_th = S.val("te")

                def v_th(v, tc=tc, te_th=te_th):
                    S.wait(v, "te", te_th)
                    S.inc(v.tensor_copy(aux32[:, tc * 512:(tc + 1) * 512],
                                        PG[tc]), "ve", 1)
                emit("vector", v_th)
                th_ve.append(S.val("ve"))
            sin_done = []
            for part in range(2):
                for ch in range(2):
                    o = ch * 1024

                    def v_u(v, part=part, ch=ch, o=o, pv=th_ve[-1],
                            sins=list(sin_done)):
                        S.wait(v, "ve", pv)
                        for m in sins:
                            S.wait(v, "se", m)
                        if part == 0:
                            r = S.inc(v.tensor_scalar(
                                stage32[:, 2048 + o:2048 + o + 1024],
                                aux32[:, o:o + 1024], 0.0, None, ALU.add), "ve", 1)
                        else:
                            r = S.inc(v.tensor_scalar(
                                stage32[:, 2048 + o:2048 + o + 1024],
                                aux32[:, o:o + 1024], PI2_HI, PI2_LO,
                                ALU.add, ALU.add), "ve", 1)
                        S.wait(v, "ve", r)
                        r = S.inc(v.tensor_scalar(
                            k1t[:, :], stage32[:, 2048 + o:2048 + o + 1024],
                            INV_2PI, None, ALU.mult), "ve", 1)
                        S.wait(v, "ve", r)
                        r = S.inc(v.scalar_tensor_tensor(
                            stage32[:, 2048 + o:2048 + o + 1024], k1t[:, :],
                            -TWO_PI_HI, stage32[:, 2048 + o:2048 + o + 1024],
                            ALU.mult, ALU.add), "ve", 1)
                        S.wait(v, "ve", r)
                        r = S.inc(v.scalar_tensor_tensor(
                            stage32[:, 2048 + o:2048 + o + 1024], k1t[:, :],
                            -TWO_PI_LO, stage32[:, 2048 + o:2048 + o + 1024],
                            ALU.mult, ALU.add), "ve", 1)
                        S.wait(v, "ve", r)
                        r = S.inc(v.tensor_scalar(
                            k1t[:, :], stage32[:, 2048 + o:2048 + o + 1024],
                            PI_F32, None, ALU.is_ge), "ve", 1)
                        S.wait(v, "ve", r)
                        r = S.inc(v.scalar_tensor_tensor(
                            stage32[:, 2048 + o:2048 + o + 1024], k1t[:, :],
                            -TWO_PI_HI, stage32[:, 2048 + o:2048 + o + 1024],
                            ALU.mult, ALU.add), "ve", 1)
                        S.wait(v, "ve", r)
                        r = S.inc(v.scalar_tensor_tensor(
                            stage32[:, 2048 + o:2048 + o + 1024], k1t[:, :],
                            -TWO_PI_LO, stage32[:, 2048 + o:2048 + o + 1024],
                            ALU.mult, ALU.add), "ve", 1)
                        S.wait(v, "ve", r)
                        r = S.inc(v.tensor_scalar(
                            k1t[:, :], stage32[:, 2048 + o:2048 + o + 1024],
                            -PI_F32, None, ALU.is_lt), "ve", 1)
                        S.wait(v, "ve", r)
                        r = S.inc(v.scalar_tensor_tensor(
                            stage32[:, 2048 + o:2048 + o + 1024], k1t[:, :],
                            TWO_PI_HI, stage32[:, 2048 + o:2048 + o + 1024],
                            ALU.mult, ALU.add), "ve", 1)
                        S.wait(v, "ve", r)
                        r = S.inc(v.scalar_tensor_tensor(
                            stage32[:, 2048 + o:2048 + o + 1024], k1t[:, :],
                            TWO_PI_LO, stage32[:, 2048 + o:2048 + o + 1024],
                            ALU.mult, ALU.add), "ve", 1)
                        S.wait(v, "ve", r)
                        S.inc(v.tensor_scalar(
                            stage32[:, 2048 + o:2048 + o + 1024],
                            stage32[:, 2048 + o:2048 + o + 1024],
                            -PI_CLAMP, PI_CLAMP, ALU.max, ALU.min), "ve", 1)
                    emit("vector", v_u)
                    ve_u = S.val("ve")

                    def s_sin(se, part=part, ch=ch, o=o, ve_u=ve_u):
                        S.wait(se, "ve", ve_u)
                        S.inc(se.activation(
                            stage32[:, o:o + 1024] if part == 0 else
                            stage32[:, 2048 + o:2048 + o + 1024],
                            stage32[:, 2048 + o:2048 + o + 1024], AF.Sin,
                            bias=cst[:, 4:5], scale=1.0), "se", 1)
                    emit("scalar", s_sin)
                    sin_done.append(S.val("se"))
            def v_ct(v, sd=sin_done[-1]):
                for m in sin_done:
                    S.wait(v, "se", m)
                S.inc(v.tensor_tensor(aux32[:, :], stage32[:, 0:2048],
                                      stage32[:, 2048:4096], ALU.add), "ve", 1)
            emit("vector", v_ct)
            ve_ct = S.val("ve")

            tr_all_b = emit_ln_to_xnt("x2sp", bigA, wnat[:, :], (otld, 2048),
                                      src_ready=x2_st, dst_free_marks=x2_st[-2:])

            kv_marks = []

            def s_kv(sy):
                S.wait(sy, "te", ffn1_te)
                S.dwait(sy, wprep_st["bWk"])
                S.dwait(sy, wprep_st["bWv"])
                kv_marks.append(S.dma(sy, "sld", out=bigB[:, 0:8192],
                                      in_=scr["WTkb"]))
                kv_marks.append(S.dma(sy, "sld", out=bigB[:, 8192:16384],
                                      in_=scr["WTvb"]))
            emit("sync", s_kv)

            brow_gld_b = load_brow("bbv", 512)

            kt_st = []
            state_b = {"sparity": 0, "oparity": 0,
                       "trip_last_exp": [0, 0], "ob_last_evac": [0, 0],
                       "et_last_read": [0, 0], "proj_evac_prev": 0}
            for g in range(KVH):
                def t_kg(t, g=g, pe=S.val("se"), pv=S.val("ve")):
                    if g == 0:
                        for mk in tr_all_b:
                            S.dwait(t, mk)
                        S.dwait(t, kv_marks[0])
                        S.wait(t, "ve", ve_x2)
                    else:
                        S.wait(t, "ve", pv)
                    last = None
                    for dc in range(ND):
                        for tc2 in range(4):
                            last = t.matmul(
                                PG[tc2],
                                bigB[:, dc * 512 + g * 128:dc * 512 + (g + 1) * 128],
                                bigA[:, dc * 2048 + tc2 * 512:dc * 2048 + (tc2 + 1) * 512],
                                start=(dc == 0), stop=(dc == ND - 1))
                    S.inc(last, "te", 1)
                emit("tensor", t_kg)
                te_kg = S.val("te")

                def s_ke(se, g=g, te_kg=te_kg):
                    S.wait(se, "te", te_kg)
                    if kt_st:
                        S.dwait(se, kt_st[-1])
                    last = None
                    for tc2 in range(4):
                        last = se.activation(KTh[:, tc2 * 512:(tc2 + 1) * 512],
                                             PG[tc2], AF.Identity,
                                             bias=bkbc[:, g:g + 1], scale=1.0)
                    S.inc(last, "se", 1)
                emit("scalar", s_ke)
                se_ke = S.val("se")

                def v_kr(v, g=g, se_ke=se_ke):
                    S.wait(v, "se", se_ke)
                    S.wait(v, "ve", ve_ct)
                    S.inc(v.tensor_tensor(KTh[:, :], KTh[:, :], aux32[:, :],
                                          ALU.mult), "ve", 1)
                emit("vector", v_kr)
                ve_kr = S.val("ve")

                def s_ks(sy, g=g, ve_kr=ve_kr):
                    S.wait(sy, "ve", ve_kr)
                    kt_st.append(S.dma(sy, "st", out=scr["KTbs"][g, :, :],
                                       in_=KTh[:, :]))
                emit("sync", s_ks)
            kt_last = S.val("ve")

            vsp_b = emit_v_all(bigA, bigB[:, 8192:16384], KVH, scr["Vspb"],
                               tr_all_b, kv_marks[1], kt_last, brow_gld_b,
                               None, None)

            oht_st = []
            for h in range(H):
                g = h // 4
                hm2 = []

                def s_wsb(sy, h=h, g=g, pe=state_b["proj_evac_prev"],
                          obm=max(state_b["ob_last_evac"])):
                    if h == 0:
                        S.dwait(sy, wprep_st["bWq"])
                        for mk in vsp_b.values():
                            S.dwait(sy, mk)
                    else:
                        S.wait(sy, "se", pe)
                        S.wait(sy, "ve", obm)
                    hm2.append(S.dma(
                        sy, "sld", out=b3(WS[0], 128),
                        in_=scr["WTqb"].rearrange("p (dc o) -> p dc o", o=D)[
                            :, :, h * 128:(h + 1) * 128]))
                    if h % 4 == 0:
                        S.dwait(sy, kt_st[g])
                        hm2.append(S.dma(sy, "sld", out=KTh[:, :],
                                         in_=scr["KTbs"][g, :, :]))
                        hm2.append(S.dma(
                            sy, "sld", out=b3(Vh, 132),
                            in_=scr["Vspb"].rearrange(
                                "t p (hh w) -> p t hh w", w=132)[:, :, g, :]))
                emit("sync", s_wsb)
                vh_m = hm2[2] if h % 4 == 0 else last_vh[0]
                kt_m = hm2[1] if h % 4 == 0 else None
                if h % 4 == 0:
                    last_vh = [hm2[2]]

                emit_projT(bigA, WS[0], QTh, bqbc[:, h:h + 1], None, None,
                           [hm2[0]], PG, state_b, rope=True, z_ve=ve_ct)
                se_q = state_b["proj_evac_prev"]
                ve_rope = state_b["rope_ve"]

                ve_head = emit_attention_head(QTh, KTh, Vh, ET, True, OSg,
                                              se_q, ve_rope, vh_m, state_b,
                                              kt_mark=kt_m)

                otr2 = []

                def s_ot2(sy, ve_head=ve_head):
                    S.wait(sy, "ve", ve_head)
                    if oht_st:
                        S.dwait(sy, oht_st[-1])
                    otr2.append(S.dma(sy, "tr", transpose_call=True,
                                      out=b3(OTt, 128), in_=OSg[:, :]))
                emit("sync", s_ot2)

                def q_ot2(sq, h=h):
                    S.dwait(sq, otr2[0])
                    oht_st.append(S.dma(sq, "st", out=scr["OTsc"][h, :, :],
                                        in_=OTt[:, :]))
                emit("scalar", q_ot2)

            brow_gld2 = load_brow("bbo")
            ve_x3, oproj_b_te = emit_oproj("WTob", wprep_st["bWo"], oht_st[-1],
                                           brow_gld2, False, "x2sp", xnt, bigA,
                                           False)

        if upto >= 4:
            ve_rows2 = emit_rows_and_wsb(ve_x3)
            wq2_marks = []

            def s_wq2(sy):
                S.wait(sy, "te", oproj_b_te)
                wq2_marks.append(S.dma(sy, "sld", out=bigB[:, :], in_=scr["WTf"]))
            emit("sync", s_wq2)

            ve_x4, out_st4, _ = emit_ffn(max(ve_x3, ve_rows2), wq2_marks, True)

        def emit_debug_out(src_big):
            dbg_st = []
            for tt in range(NT):
                half = tt % 2

                def s_cp(se, tt=tt, half=half):
                    S.wait(se, "ve")
                    if tt >= 2:
                        S.dwait(se, dbg_st[tt - 2])
                    S.inc(se.activation(
                        stage32[:, half * 2048:(half + 1) * 2048],
                        src_big[:, tt * 2048:(tt + 1) * 2048],
                        AF.Identity, bias=cst[:, 4:5], scale=1.0), "se", 1)
                emit("scalar", s_cp)
                se_c = S.val("se")

                def s_st(sy, tt=tt, half=half, se_c=se_c):
                    S.wait(sy, "se", se_c)
                    dbg_st.append(S.dma(
                        sy, "st", out=out_t[tt * 128:(tt + 1) * 128, :],
                        in_=stage32[:, half * 2048:(half + 1) * 2048]))
                emit("sync", s_st)

        if upto in (1, 2, 3):
            emit_debug_out(bigA)

        def fin(sy):
            for nm in ["st", "gld", "sld", "tr", "glx", "grs"]:
                S.dwait_all(sy, nm)
            S.wait(sy, "ve")
            S.wait(sy, "se")
            S.wait(sy, "te")
        emit("sync", fin)

    return nc


def make_inputs_per_core(inputs, core):
    x = np.ascontiguousarray(inputs["L0_attn_a_x"][core])
    dm = np.tril(np.ones((128, 128), dtype=np.float32))
    m = {
        "x": x, "dmaski": dm,
        "aWq": inputs["L0_attn_a_Wq"], "aWk": inputs["L0_attn_a_Wk"],
        "aWv": inputs["L0_attn_a_Wv"], "aWo": inputs["L0_attn_a_Wo"],
        "abq": inputs["L0_attn_a_bq"], "abk": inputs["L0_attn_a_bk"],
        "abv": inputs["L0_attn_a_bv"], "abo": inputs["L0_attn_a_bo"],
        "aBq": inputs["L0_attn_a_Bq"], "aBk": inputs["L0_attn_a_Bk"],
        "aBv": inputs["L0_attn_a_Bv"], "aBo": inputs["L0_attn_a_Bo"],
        "aAq": inputs["L0_attn_a_Aq"], "aAk": inputs["L0_attn_a_Ak"],
        "aAv": inputs["L0_attn_a_Av"], "aAo": inputs["L0_attn_a_Ao"],
        "bWq": inputs["L0_attn_b_Wq"], "bWk": inputs["L0_attn_b_Wk"],
        "bWv": inputs["L0_attn_b_Wv"], "bWo": inputs["L0_attn_b_Wo"],
        "bbq": inputs["L0_attn_b_bq"], "bbk": inputs["L0_attn_b_bk"],
        "bbv": inputs["L0_attn_b_bv"], "bbo": inputs["L0_attn_b_bo"],
        "fW": inputs["L0_ffn_weight"], "fb": inputs["L0_ffn_bias"],
    }
    return {k: np.ascontiguousarray(v) for k, v in m.items()}


def kernel(**inputs):
    nc = build_nc(upto=4)
    in_maps = [make_inputs_per_core(inputs, c) for c in range(8)]
    res = run_bass_kernel_spmd(nc, in_maps, core_ids=list(range(8)))
    return np.stack([res.results[c]["out"] for c in range(8)], axis=0)


# version 02
# speedup vs baseline: 1.0414x; 1.0414x over previous
import numpy as np
import ml_dtypes
import concourse.bass as bass
import concourse.mybir as mybir
from concourse.bass_utils import run_bass_kernel_spmd
from contextlib import ExitStack

F32 = mybir.dt.float32
BF16 = mybir.dt.bfloat16
I32 = mybir.dt.int32
AF = mybir.ActivationFunctionType
ALU = mybir.AluOpType
AXX = mybir.AxisListType.X

L, D, NT, ND = 2048, 2048, 16, 16
H, HD, KVH = 16, 128, 4
ISQ = float(128.0 ** -0.5)
ESH = -12.0
EPS = 1e-5
INV_D = 1.0 / 2048.0
TWO_PI_HI = float(np.float32(2 * np.pi))
TWO_PI_LO = float(np.float32(2 * np.pi - np.float64(np.float32(2 * np.pi))))
PI2_HI = float(np.float32(np.pi / 2))
PI2_LO = float(np.float32(np.pi / 2 - np.float64(np.float32(np.pi / 2))))
PI_CLAMP = 3.1415925
PI_F32 = float(np.float32(np.pi))
INV_2PI = float(np.float32(1.0 / (2 * np.pi)))
EXPSC = float(np.float32(-2.0 * np.log(10000.0) / 128.0))


class Sems:

    def __init__(self):
        self.h, self.c = {}, {}
        self.dh, self.dc, self.di = {}, {}, {}

    def add(self, name, handle):
        self.h[name], self.c[name] = handle, 0

    def add_dma(self, name, h0, h1):
        self.dh[name] = [h0, h1]
        self.dc[name] = [0, 0]
        self.di[name] = 0

    def inc(self, inst, name, n=1):
        self.c[name] += n
        inst.then_inc(self.h[name], n)
        return self.c[name]

    def wait(self, eng, name, value=None):
        v = self.c[name] if value is None else value
        if v > 0:
            eng.wait_ge(self.h[name], v)

    def val(self, name):
        return self.c[name]

    def dma(self, eng, name, **kw):
        p = self.di[name] % 2
        self.di[name] += 1
        if self.dc[name][p] > 0:
            eng.wait_ge(self.dh[name][p], self.dc[name][p])
        tr = kw.pop("transpose_call", False)
        if tr:
            inst = eng.dma_start_transpose(kw["out"], kw["in_"])
        else:
            inst = eng.dma_start(**kw)
        self.dc[name][p] += 16
        inst.then_inc(self.dh[name][p], 16)
        return (name, p, self.dc[name][p])

    def dwait(self, eng, mark):
        name, p, v = mark
        eng.wait_ge(self.dh[name][p], v)

    def dwait_all(self, eng, name):
        for p in (0, 1):
            if self.dc[name][p] > 0:
                eng.wait_ge(self.dh[name][p], self.dc[name][p])


def build_nc(upto=4):
    nc = bass.Bass("TRN2", target_bir_lowering=False)

    x_in = nc.dram_tensor("x", [L, D], F32, kind="ExternalInput").ap()
    out_t = nc.dram_tensor("out", [L, D], F32, kind="ExternalOutput").ap()

    wts = {}
    for nm, shp in [
        ("aWq", [D, D]), ("aWk", [D, D]), ("aWv", [D, D]), ("aWo", [D, D]),
        ("abq", [D]), ("abk", [D]), ("abv", [D]), ("abo", [D]),
        ("aBq", [16, D]), ("aBk", [16, D]), ("aBv", [16, D]), ("aBo", [16, D]),
        ("aAq", [D, 16]), ("aAk", [D, 16]), ("aAv", [D, 16]), ("aAo", [D, 16]),
        ("bWq", [D, D]), ("bWk", [512, D]), ("bWv", [512, D]), ("bWo", [D, D]),
        ("bbq", [D]), ("bbk", [512]), ("bbv", [512]), ("bbo", [D]),
        ("fW", [D, D]), ("fb", [D]),
    ]:
        wts[nm] = nc.dram_tensor(nm, shp, F32, kind="ExternalInput").ap()
    dmask_in = nc.dram_tensor("dmaski", [128, 128], F32, kind="ExternalInput").ap()

    scr = {}
    for nm, shp, dt in [
        ("WTqa", [128, ND * D], BF16), ("WTka", [128, ND * D], BF16),
        ("WTva", [128, ND * D], BF16), ("WToa", [128, ND * D], BF16),
        ("WTqb", [128, ND * D], BF16), ("WTob", [128, ND * D], BF16),
        ("WTkb", [128, ND * 512], BF16), ("WTvb", [128, ND * 512], BF16),
        ("WTf", [128, ND * D], BF16),
        ("ATsc", [16, 4 * D], BF16),
        ("Vspa", [NT, 128, H * 132], BF16),
        ("Vspb", [NT, 128, KVH * 132], BF16),
        ("KTbs", [KVH, 128, L], BF16),
        ("OTsc", [H, 128, L], BF16),
        ("x2sp", [NT, 128, D], BF16),
    ]:
        scr[nm] = nc.dram_tensor(nm, shp, dt, kind="Internal").ap()

    ctx = ExitStack()
    with ctx:
        def sbt(name, shape, dtype):
            return ctx.enter_context(nc.sbuf_tensor(name, shape, dtype))

        bigA = sbt("bigA", [128, 32768], BF16)
        bigB = sbt("bigB", [128, 32768], BF16)
        stage32 = sbt("stage32", [128, 4096], F32)
        xnt = sbt("xnt", [128, 4096], BF16)
        vstg = sbt("vstg", [128, 1056], BF16)
        otld = sbt("otld", [128, 4096], BF16)
        k1t = sbt("k1t", [128, 1024], I32)
        aux32 = sbt("aux32", [128, 2048], F32)
        zq = sbt("zq", [16, 2048], BF16)
        zk = sbt("zk", [16, 2048], BF16)
        zv = sbt("zv", [16, 2048], BF16)
        zob = sbt("zob", [16, 256], BF16)
        BTq = sbt("BTq", [128, ND * 16], BF16)
        BTk = sbt("BTk", [128, ND * 16], BF16)
        BTv = sbt("BTv", [128, ND * 16], BF16)
        BTo = sbt("BTo", [128, ND * 16], BF16)
        ATs = sbt("ATs", [16, 2048], BF16)
        rowsB = sbt("rowsB", [3, 2048], BF16)
        rowsA = sbt("rowsA", [3, 128], BF16)
        bcolq = sbt("bcolq", [128, 16], F32)
        bcolk = sbt("bcolk", [128, 16], F32)
        bqbc = sbt("bqbc", [128, 16], F32)
        bkbc = sbt("bkbc", [128, 4], F32)
        fbcol = sbt("fbcol", [128, 16], F32)
        wscol = sbt("wscol", [128, 16], F32)
        wsumc = sbt("wsumc", [128, 16], F32)
        betac = sbt("betac", [128, 16], F32)
        c2hc = sbt("c2hc", [128, 16], BF16)
        c2lc = sbt("c2lc", [128, 16], BF16)
        alc = sbt("alc", [128, 4], BF16)
        sl = sbt("sl", [128, 16], F32)
        cst = sbt("cst", [128, 8], F32)
        dmask = sbt("dmask", [128, 128], BF16)
        identb = sbt("identb", [128, 128], BF16)
        identf = sbt("identf", [128, 128], F32)
        onesf = sbt("onesf", [1, 128], F32)
        onesr = sbt("onesr", [1, 128], BF16)
        iotp = sbt("iotp", [128, 1], F32)
        invd = sbt("invd", [128, 4], F32)
        wnat = sbt("wnat", [128, 2048], BF16)
        wtt = sbt("wtt", [128, 2048], BF16)
        Bnat = otld[0:16, 0:2048]
        brow = rowsB[0:1, :]

        ps_a = ctx.enter_context(nc.psum_tensor("ps_a", [128, 1536], F32))
        ps_b = ctx.enter_context(nc.psum_tensor("ps_b", [128, 1536], F32))
        ps_oa = ctx.enter_context(nc.psum_tensor("ps_oa", [128, 512], F32))
        ps_ob = ctx.enter_context(nc.psum_tensor("ps_ob", [128, 512], F32))
        PG = [ps_a[:, 0:512], ps_a[:, 512:1024], ps_a[:, 1024:1536], ps_b[:, 0:512]]
        KGB = [ps_b[:, 512:1024], ps_b[:, 1024:1536], ps_oa[:, 0:512], ps_ob[:, 0:512]]
        PZ = ps_b[:, 512:1024]
        STRIPLE = [ps_a, ps_b]

        S = Sems()
        for snm in ["te", "ve", "se", "gp"]:
            S.add(snm, ctx.enter_context(nc.semaphore(snm)))
        for snm in ["gld", "sld", "tr", "st", "glx", "grs"]:
            S.add_dma(snm, ctx.enter_context(nc.semaphore(snm + "0")),
                      ctx.enter_context(nc.semaphore(snm + "1")))
        block = ctx.enter_context(nc.Block())

        ET = [bigB[:, 0:8192], bigB[:, 8192:16384]]
        QTh = bigB[:, 16384:18432]
        KTh = bigB[:, 18432:20480]
        Vh = bigB[:, 20480:22592]
        OSg = bigB[:, 22592:24640]
        OTt = bigB[:, 24640:26688]
        WS = [bigB[:, 26688:28736], bigB[:, 28736:30784]]

        def b3(ap, w):
            return ap.rearrange("p (n w) -> p n w", w=w)

        def emit(engname, fn):
            getattr(block, engname)(fn)

        p0_marks = []

        def p0a_g(g):
            g.iota(iotp[:, :], pattern=[[0, 1]], base=0, channel_multiplier=1,
                   allow_small_or_imprecise_dtypes=True)
            S.inc(g.iota(stage32[:, 0:128], pattern=[[1, 128]], base=0,
                         channel_multiplier=0,
                         allow_small_or_imprecise_dtypes=True), "gp", 1)
            p0_marks.append(S.dma(g, "gld", out=dmask[:, :], in_=dmask_in))
            for dst, src, w in [(bcolq, wts["abq"], 16), (bcolk, wts["abk"], 16),
                                (bqbc, wts["bbq"], 16), (bkbc, wts["bbk"], 4),
                                (fbcol, wts["fb"], 16)]:
                p0_marks.append(S.dma(g, "gld", out=dst[:, 0:w],
                                      in_=src.rearrange("(a b) -> b a", b=128),
                                      allow_slow_non_contiguous=True))
        emit("gpsimd", p0a_g)

        def p0a_v(v):
            v.memset(cst[:, 0:1], ESH)
            v.memset(cst[:, 1:2], 128.0)
            v.memset(cst[:, 2:3], EPS)
            v.memset(cst[:, 3:4], -127.5)
            v.memset(cst[:, 4:5], 0.0)
            v.memset(onesr[0:1, :], 1.0)
            v.memset(rowsA[0:3, :], 1.0)
            v.memset(vstg[:, :], 1.0)
            for mk in p0_marks:
                S.dwait(v, mk)
            S.wait(v, "gp", 1)
            v.memset(onesf[0:1, :], 1.0)
            v.tensor_scalar(identf[:, :], stage32[:, 0:128], iotp[:, 0:1],
                            None, ALU.is_equal)
            S.inc(v.tensor_scalar(identb[:, :], stage32[:, 0:128], iotp[:, 0:1],
                                  None, ALU.is_equal), "ve", 1)
        emit("vector", p0a_v)
        ve_consts = S.val("ve")

        at_store_marks = []
        for i, nm in enumerate(["aAq", "aAk", "aAv", "aAo"]):
            ga_mark = []

            def g_a(g, i=i, nm=nm):
                if i > 0:
                    S.dwait(g, at_store_marks[-1])
                ga_mark.append(S.dma(
                    g, "gld",
                    out=b3(wnat[:, 0:256], 16),
                    in_=wts[nm].rearrange("(ot p) r -> p ot r", p=128)))
            emit("gpsimd", g_a)

            for ot in range(16):
                def t_a(t, ot=ot, vp=S.val("ve")):
                    if ot == 0:
                        S.dwait(t, ga_mark[0])
                        S.wait(t, "ve", ve_consts)
                    else:
                        S.wait(t, "ve", vp)
                    S.inc(t.matmul(PZ[0:16, 0:128],
                                   wnat[:, ot * 16:(ot + 1) * 16],
                                   identb[:, :], start=True, stop=True), "te", 1)
                emit("tensor", t_a)
                te_a = S.val("te")

                def v_a(v, ot=ot, te_a=te_a):
                    S.wait(v, "te", te_a)
                    S.inc(v.tensor_scalar(ATs[:, ot * 128:(ot + 1) * 128],
                                          PZ[0:16, 0:128], 0.5, None, ALU.mult),
                          "ve", 1)
                emit("vector", v_a)
            ve_a = S.val("ve")

            def s_at(sy, i=i, ve_a=ve_a):
                S.wait(sy, "ve", ve_a)
                at_store_marks.append(S.dma(
                    sy, "st", out=scr["ATsc"][:, i * D:(i + 1) * D],
                    in_=ATs[:, :]))
            emit("sync", s_at)
        at_done = list(at_store_marks)

        bld_marks, btr_marks = [], []
        for i, bt in enumerate([BTq, BTk, BTv, BTo]):
            def g_b(g, i=i):
                if i > 0:
                    S.dwait(g, btr_marks[-1])
                bld_marks.append(S.dma(
                    g, "gld", out=Bnat,
                    in_=wts[["aBq", "aBk", "aBv", "aBo"][i]]))
            emit("gpsimd", g_b)

            def s_b(sy, i=i, bt=bt):
                S.dwait(sy, bld_marks[-1])
                btr_marks.append(S.dma(sy, "tr", transpose_call=True,
                                       out=b3(bt, 16), in_=Bnat))
            emit("sync", s_b)
        bt_done = list(btr_marks)

        pw_ld, pw_tr, pw_st = [], [], []
        wprep_st = {}

        def prep_weight(wname, dramT, nrows):
            nt = nrows // 128
            for i in range(nt):
                def g_fn(g, i=i, wname=wname):
                    if pw_tr:
                        S.dwait(g, pw_tr[-1])
                    pw_ld.append(S.dma(g, "gld", out=wnat[:, :],
                                       in_=wts[wname][i * 128:(i + 1) * 128, :]))
                emit("gpsimd", g_fn)

                def s_fn(sq, i=i):
                    S.dwait(sq, pw_ld[-1])
                    if pw_st:
                        S.dwait(sq, pw_st[-1])
                    pw_tr.append(S.dma(sq, "tr", transpose_call=True,
                                       out=b3(wtt, 128), in_=wnat[:, :]))
                emit("scalar", s_fn)

                def q_fn(sq, i=i, dramT=dramT, nrows=nrows):
                    S.dwait(sq, pw_tr[-1])
                    pw_st.append(S.dma(
                        sq, "st",
                        out=dramT.rearrange("p (dc o) -> p dc o", o=nrows)[
                            :, :, i * 128:(i + 1) * 128],
                        in_=b3(wtt, 128)))
                emit("scalar", q_fn)
            wprep_st[wname] = pw_st[-1]

        prep_weight("aWv", scr["WTva"], D)
        prep_weight("aWq", scr["WTqa"], D)
        prep_weight("aWk", scr["WTka"], D)
        prep_weight("aWo", scr["WToa"], D)
        if upto >= 3:
            prep_weight("bWk", scr["WTkb"], 512)
            prep_weight("bWv", scr["WTvb"], 512)
            prep_weight("bWq", scr["WTqb"], D)
            prep_weight("bWo", scr["WTob"], D)

        def emit_ln_tile(src_ap, dst_ap, sq_ap, pre_mark=None, dst_free=None,
                         pre_ve=None, dst_free_ve=None):
            def v1(v):
                if pre_mark:
                    for mk in pre_mark:
                        S.dwait(v, mk)
                if pre_ve:
                    S.wait(v, "ve", pre_ve)
                rs = S.inc(v.reduce_sum(sl[:, 0:1], src_ap, axis=AXX), "ve", 1)
                S.wait(v, "ve", rs)
                rs = S.inc(v.scalar_tensor_tensor(sq_ap, src_ap, 1.0, src_ap,
                                                  ALU.mult, ALU.mult,
                                                  accum_out=sl[:, 1:2]), "ve", 1)
                S.wait(v, "ve", rs)
                rs = S.inc(v.tensor_scalar(sl[:, 2:3], sl[:, 0:1], INV_D, None,
                                           ALU.mult), "ve", 1)
                S.wait(v, "ve", rs)
                rs = S.inc(v.tensor_tensor(sl[:, 4:5], sl[:, 2:3], sl[:, 2:3],
                                           ALU.mult), "ve", 1)
                S.wait(v, "ve", rs)
                S.inc(v.scalar_tensor_tensor(sl[:, 5:6], sl[:, 1:2], INV_D,
                                             sl[:, 4:5], ALU.mult, ALU.subtract),
                      "ve", 1)
            emit("vector", v1)
            ve_var = S.val("ve")

            def s2(se):
                S.wait(se, "ve", ve_var)
                r = S.inc(se.activation(sl[:, 6:7], sl[:, 5:6], AF.Ln,
                                        bias=cst[:, 2:3], scale=1.0), "se", 1)
                S.wait(se, "se", r)
                S.inc(se.activation(sl[:, 7:8], sl[:, 6:7], AF.Exp,
                                    bias=cst[:, 4:5], scale=-0.5), "se", 1)
            emit("scalar", s2)
            se_sd = S.val("se")

            def v3(v):
                S.wait(v, "se", se_sd)
                S.inc(v.tensor_scalar(sl[:, 8:9], sl[:, 2:3], sl[:, 7:8], -1.0,
                                      ALU.mult, ALU.mult), "ve", 1)
            emit("vector", v3)
            ve_nm = S.val("ve")

            def s3(se):
                S.wait(se, "ve", ve_nm)
                if dst_free is not None:
                    S.dwait(se, dst_free)
                if dst_free_ve:
                    S.wait(se, "ve", dst_free_ve)
                S.inc(se.activation(dst_ap, src_ap, AF.Identity,
                                    bias=sl[:, 8:9], scale=sl[:, 7:8]), "se", 1)
            emit("scalar", s3)
            return S.val("se")

        def emit_ln_to_xnt(src_kind, xnt_dst, sq_ap, ld_halves,
                           src_ready=None, sq_free=None, dst_free_marks=None):
            prev_se = [0, 0]
            tr_marks = []
            for tt in range(NT):
                half = tt % 2
                ldt, ldw = ld_halves
                ld_buf = ldt[:, half * ldw:(half + 1) * ldw]
                src = (x_in[tt * 128:(tt + 1) * 128, :] if src_kind == "x0"
                       else scr["x2sp"][tt, :, :])

                lm = []

                def g_ld(g, ld_buf=ld_buf, src=src, tt=tt, pse=prev_se[half]):
                    if tt < 2:
                        S.wait(g, "ve", ve_consts)
                    else:
                        S.wait(g, "se", pse)
                    if src_ready is not None:
                        S.dwait(g, src_ready[tt])
                    lm.append(S.dma(g, "glx", out=ld_buf, in_=src))
                emit("gpsimd", g_ld)

                se_done = emit_ln_tile(
                    ld_buf, xnt[:, half * 2048:(half + 1) * 2048], sq_ap,
                    pre_mark=(lm + (sq_free or []) if tt < 2 else lm),
                    dst_free=(tr_marks[tt - 2] if tt >= 2 else None))
                prev_se[half] = se_done

                def s_tr(sy, tt=tt, half=half, se_done=se_done):
                    if tt == 0 and dst_free_marks:
                        for mk in dst_free_marks:
                            S.dwait(sy, mk)
                    S.wait(sy, "se", se_done)
                    tr_marks.append(S.dma(
                        sy, "tr", transpose_call=True,
                        out=b3(xnt_dst, 2048)[:, :, tt * 128:(tt + 1) * 128],
                        in_=xnt[:, half * 2048:(half + 1) * 2048]))
                emit("sync", s_tr)
            return tr_marks[-2:]

        def emit_z_all(xnT, tr_xnt_all):
            for i, (bt, zdst) in enumerate(zip([BTq, BTk, BTv], [zq, zk, zv])):
                for tc in range(4):
                    ve_prev = S.val("ve")

                    def t_fn(t, i=i, tc=tc, bt=bt, ve_prev=ve_prev):
                        if i == 0 and tc == 0:
                            for mk in tr_xnt_all:
                                S.dwait(t, mk)
                            for mk in bt_done:
                                S.dwait(t, mk)
                            S.wait(t, "ve", ve_consts)
                        else:
                            S.wait(t, "ve", ve_prev)
                        inst = None
                        for dc in range(ND):
                            inst = t.matmul(
                                PZ[0:16, :], b3(bt, 16)[:, dc, :],
                                xnT[:, dc * 2048 + tc * 512:dc * 2048 + (tc + 1) * 512],
                                start=(dc == 0), stop=(dc == ND - 1))
                        S.inc(inst, "te", 1)
                    emit("tensor", t_fn)
                    te_z = S.val("te")

                    def v_fn(v, zdst=zdst, tc=tc, te_z=te_z):
                        S.wait(v, "te", te_z)
                        S.inc(v.tensor_copy(zdst[:, tc * 512:(tc + 1) * 512],
                                            PZ[0:16, :]), "ve", 1)
                    emit("vector", v_fn)
            return S.val("ve")

        def emit_v_all(xnT, wvT_sb, nheads, vsp, tr_xnt_all, wv_ld_mark,
                       first_ve_wait, brow_gld, zvbuf, av_ready_sld):
            noc = (nheads * 128) // 512
            vst_marks = {}
            vhalf_free = {}
            for tt in range(NT):
                for oc in range(noc):
                    gidx = tt * noc + oc
                    half = gidx % 2

                    def t_fn(t, tt=tt, oc=oc, gidx=gidx):
                        if gidx == 0:
                            for mk in tr_xnt_all:
                                S.dwait(t, mk)
                            S.dwait(t, wv_ld_mark)
                            S.wait(t, "ve", first_ve_wait)
                            S.dwait(t, brow_gld)
                            if av_ready_sld:
                                S.dwait(t, av_ready_sld)
                        if gidx >= 4:
                            S.wait(t, "ve", vhalf_free[gidx - 4])
                        last = None
                        for dc in range(ND):
                            last = t.matmul(
                                PG[gidx % 4],
                                b3(xnT, 2048)[:, dc, tt * 128:(tt + 1) * 128],
                                wvT_sb[:, dc * (nheads * 128) + oc * 512:
                                       dc * (nheads * 128) + (oc + 1) * 512],
                                start=(dc == 0), stop=False)
                        if zvbuf is not None:
                            last = t.matmul(
                                PG[gidx % 4],
                                zvbuf[0:16, tt * 128:(tt + 1) * 128],
                                ATs[0:16, oc * 512:(oc + 1) * 512],
                                start=False, stop=False)
                        last = t.matmul(PG[gidx % 4], onesr[0:1, :],
                                        brow[0:1, oc * 512:(oc + 1) * 512],
                                        start=False, stop=True)
                        S.inc(last, "te", 1)
                    emit("tensor", t_fn)
                    te_v = S.val("te")

                    def v_fn(v, tt=tt, oc=oc, gidx=gidx, half=half, te_v=te_v):
                        S.wait(v, "te", te_v)
                        if gidx >= 2:
                            S.dwait(v, vst_marks[gidx - 2])
                        S.inc(v.tensor_copy(
                            b3(vstg[:, half * 528:(half + 1) * 528], 132)[:, :, 0:128],
                            b3(PG[gidx % 4], 128)), "ve", 1)
                    emit("vector", v_fn)
                    vhalf_free[gidx] = S.val("ve")

                    def s_sp(sy, tt=tt, oc=oc, half=half, vm=vhalf_free[gidx],
                             gidx2=gidx):
                        S.wait(sy, "ve", vm)
                        vst_marks[gidx2] = S.dma(
                            sy, "st",
                            out=vsp[tt, :, oc * 528:(oc + 1) * 528],
                            in_=vstg[:, half * 528:(half + 1) * 528])
                    emit("sync", s_sp)
            return vst_marks

        def emit_projT(xnT, ws_buf, dst, bias_col_ap, lora_at, zbuf,
                       sld_mark, groups, state, rope=False, z_ve=0):
            def t_fn(t):
                for mk in sld_mark:
                    S.dwait(t, mk)
                if z_ve:
                    S.wait(t, "ve", z_ve)
                S.wait(t, "ve", state["proj_evac_prev"])
                S.wait(t, "se", max(state["trip_last_exp"]))
                S.wait(t, "ve", max(state["ob_last_evac"]))
                last = None
                for dc in range(ND):
                    for tc in range(4):
                        last = t.matmul(
                            groups[tc], b3(ws_buf, 128)[:, dc, :],
                            xnT[:, dc * 2048 + tc * 512:dc * 2048 + (tc + 1) * 512],
                            start=(dc == 0),
                            stop=(dc == ND - 1 and lora_at is None))
                if lora_at is not None:
                    for tc in range(4):
                        last = t.matmul(
                            groups[tc], ATs[0:16, lora_at:lora_at + 128],
                            zbuf[0:16, tc * 512:(tc + 1) * 512],
                            start=False, stop=True)
                S.inc(last, "te", 1)
            emit("tensor", t_fn)
            te_p = S.val("te")

            def v_fn(v):
                S.wait(v, "te", te_p)
                last = None
                for tc in range(4):
                    last = v.tensor_scalar(dst[:, tc * 512:(tc + 1) * 512],
                                           groups[tc], bias_col_ap, None,
                                           ALU.add)
                S.inc(last, "ve", 1)
                if rope:
                    S.wait(v, "ve", S.val("ve"))
                    S.inc(v.tensor_tensor(dst[:, :], dst[:, :], aux32[:, :],
                                          ALU.mult), "ve", 1)
            emit("vector", v_fn)
            state["proj_evac_prev"] = S.val("ve")
            if rope:
                state["rope_ve"] = S.val("ve")
            return S.val("ve")

        def emit_attention_head(qT, kT, vhbuf, etbufs, causal, osg,
                                se_ready, ve_ready, vh_mark, state,
                                kt_mark=None):
            exp_qc, mask_qc = {}, {}

            def emit_s(qc):
                kt_hi = (qc * 4 + 4) if causal else 16
                ngrp = (kt_hi + 2) // 3
                exp_marks = []
                for ktg in range(ngrp):
                    kts = list(range(3 * ktg, min(3 * ktg + 3, kt_hi)))
                    gsz = len(kts)
                    par = state["sparity"] % 2
                    state["sparity"] += 1
                    trip = STRIPLE[par]
                    need_exp = state["trip_last_exp"][par]

                    def t_s(t, kts=kts, trip=trip, need_exp=need_exp, qc=qc,
                            first=(ktg == 0 and qc == 0)):
                        if first:
                            S.wait(t, "ve", se_ready)
                            if ve_ready:
                                S.wait(t, "ve", ve_ready)
                            if kt_mark is not None:
                                S.dwait(t, kt_mark)
                        S.wait(t, "se", need_exp)
                        inst = None
                        for j, kt in enumerate(kts):
                            inst = t.matmul(
                                trip[:, j * 512:(j + 1) * 512],
                                kT[:, kt * 128:(kt + 1) * 128],
                                qT[:, qc * 512:(qc + 1) * 512],
                                start=True, stop=True)
                        S.inc(inst, "te", 1)
                    emit("tensor", t_s)
                    te_s = S.val("te")

                    etdst = etbufs[qc % 2]
                    oread = state["et_last_read"][qc % 2]

                    def s_e(se, te_s=te_s, trip=trip, gsz=gsz, etdst=etdst,
                            ktg=ktg, oread=oread):
                        S.wait(se, "te", te_s)
                        S.wait(se, "te", oread)
                        S.inc(se.activation(
                            etdst[:, (3 * ktg) * 512:(3 * ktg + gsz) * 512],
                            trip[:, 0:gsz * 512], AF.Exp,
                            bias=cst[:, 0:1], scale=ISQ), "se", 1)
                    emit("scalar", s_e)
                    exp_marks.append(S.val("se"))
                    state["trip_last_exp"][par] = S.val("se")
                exp_qc[qc] = exp_marks

                if causal:
                    mm = {}
                    for qt_l in range(4):
                        qt = qc * 4 + qt_l
                        kt = qt
                        gi = kt // 3
                        off = kt * 512 + qt_l * 128

                        def v_m(v, off=off, qc=qc, em=exp_marks[gi]):
                            S.wait(v, "se", em)
                            S.inc(v.tensor_tensor(
                                etbufs[qc % 2][:, off:off + 128],
                                etbufs[qc % 2][:, off:off + 128],
                                dmask[:, :], ALU.mult), "ve", 1)
                        emit("vector", v_m)
                        mm[qt_l] = S.val("ve")
                    mask_qc[qc] = mm

            def emit_o(qc):
                for qt_l in range(4):
                    qt = qc * 4 + qt_l
                    kts = list(range(qt + 1)) if causal else list(range(16))
                    opar = state["oparity"] % 2
                    state["oparity"] += 1
                    ob = [ps_oa, ps_ob][opar]
                    ev_prev = state["ob_last_evac"][opar]
                    mm = mask_qc.get(qc, {}).get(qt_l, 0)

                    def t_o(t, qt_l=qt_l, kts=kts, ob=ob, qc=qc, ev_prev=ev_prev,
                            em=exp_qc[qc][-1], mm=mm,
                            first=(qc == 0 and qt_l == 0)):
                        if first and vh_mark is not None:
                            S.dwait(t, vh_mark)
                        S.wait(t, "se", em)
                        if mm:
                            S.wait(t, "ve", mm)
                        S.wait(t, "ve", ev_prev)
                        inst = None
                        for idx, kt in enumerate(kts):
                            inst = t.matmul(
                                ob[:, 0:129],
                                etbufs[qc % 2][:, kt * 512 + qt_l * 128:
                                               kt * 512 + qt_l * 128 + 128],
                                b3(vhbuf, 132)[:, kt, 0:129],
                                start=(idx == 0), stop=(idx == len(kts) - 1))
                        S.inc(inst, "te", 1)
                    emit("tensor", t_o)
                    te_o = S.val("te")
                    state["et_last_read"][qc % 2] = te_o

                    def v_o(v, qt=qt, ob=ob, te_o=te_o):
                        S.wait(v, "te", te_o)
                        rc = S.inc(v.reciprocal(invd[:, qt % 4:qt % 4 + 1],
                                                ob[:, 128:129]), "ve", 1)
                        S.wait(v, "ve", rc)
                        S.inc(v.tensor_scalar(
                            osg[:, qt * 128:(qt + 1) * 128], ob[:, 0:128],
                            invd[:, qt % 4:qt % 4 + 1], None, ALU.mult), "ve", 1)
                    emit("vector", v_o)
                    state["ob_last_evac"][opar] = S.val("ve")

            for qc in range(4):
                emit_s(qc)
                if qc >= 1:
                    emit_o(qc - 1)
            emit_o(3)
            return S.val("ve")

        def emit_oproj(wtname, wt_st_mark, attn_done_st, bias_gld, lora,
                       resid_kind, resid_buf, dst_bigA, write_x2sp):
            pre_se = S.val("se")
            pre_trm = [("tr", p, S.dc["tr"][p]) for p in (0, 1)
                       if S.dc["tr"][p] > 0]
            wo_marks = []

            def s_wo(sy):
                S.dwait(sy, wt_st_mark)
                S.dwait(sy, attn_done_st)
                wo_marks.append(S.dma(sy, "sld", out=bigB[:, :], in_=scr[wtname]))
                if lora:
                    S.dwait(sy, at_done[3])
                    wo_marks.append(S.dma(sy, "sld", out=ATs[:, :],
                                          in_=scr["ATsc"][:, 3 * D:4 * D]))
            emit("sync", s_wo)

            zo_ve = {}
            prev_otld_reader = [0, 0]
            prev_y_evac = [0]
            for tt in range(NT):
                half = tt % 2

                otm_l = []

                def s_otl(sy, tt=tt, half=half, pe=prev_otld_reader[half]):
                    if tt == 0:
                        for mk in wo_marks:
                            S.dwait(sy, mk)
                    if tt >= 2:
                        S.wait(sy, "te", pe)
                    otm_l.append(S.dma(
                        sy, "sld",
                        out=otld[:, half * 2048:(half + 1) * 2048],
                        in_=scr["OTsc"].rearrange("h p t -> p h t")[
                            :, :, tt * 128:(tt + 1) * 128]))
                emit("sync", s_otl)

                res_l = []

                def g_res(g, tt=tt, half=half, py=prev_y_evac[0]):
                    if tt < 2:
                        S.wait(g, "se", pre_se)
                        for mk in pre_trm:
                            S.dwait(g, mk)
                    else:
                        S.wait(g, "ve", py)
                    src = (x_in[tt * 128:(tt + 1) * 128, :] if resid_kind == "x0"
                           else scr["x2sp"][tt, :, :])
                    res_l.append(S.dma(
                        g, "grs",
                        out=resid_buf[:, half * 2048:(half + 1) * 2048],
                        in_=src))
                emit("gpsimd", g_res)

                if lora:
                    def t_zo(t, tt=tt, half=half,
                             pz=zo_ve.get(tt - 2, 0)):
                        S.dwait(t, otm_l[-1])
                        S.wait(t, "ve", pz)
                        inst = None
                        for oc in range(16):
                            inst = t.matmul(
                                PZ[0:16, 0:128], b3(BTo, 16)[:, oc, :],
                                otld[:, half * 2048 + oc * 128:
                                     half * 2048 + (oc + 1) * 128],
                                start=(oc == 0), stop=(oc == 15))
                        S.inc(inst, "te", 1)
                    emit("tensor", t_zo)
                    te_zo = S.val("te")

                    def v_zo(v, half=half, te_zo=te_zo):
                        S.wait(v, "te", te_zo)
                        S.inc(v.tensor_copy(zob[:, half * 128:(half + 1) * 128],
                                            PZ[0:16, 0:128]), "ve", 1)
                    emit("vector", v_zo)
                    zo_ve[tt] = S.val("ve")

                def t_y(t, tt=tt, half=half,
                        zv_=zo_ve.get(tt, 0), py=prev_y_evac[0]):
                    S.dwait(t, otm_l[-1])
                    if lora:
                        S.wait(t, "ve", zv_)
                    S.wait(t, "ve", py)
                    S.dwait(t, bias_gld)
                    last = None
                    for dch in range(4):
                        for oc in range(16):
                            last = t.matmul(
                                PG[dch],
                                otld[:, half * 2048 + oc * 128:
                                     half * 2048 + (oc + 1) * 128],
                                bigB[:, oc * 2048 + dch * 512:
                                     oc * 2048 + (dch + 1) * 512],
                                start=(oc == 0), stop=False)
                        if lora:
                            last = t.matmul(
                                PG[dch], zob[0:16, half * 128:(half + 1) * 128],
                                ATs[0:16, dch * 512:(dch + 1) * 512],
                                start=False, stop=False)
                        last = t.matmul(
                            PG[dch], onesr[0:1, :],
                            brow[0:1, dch * 512:(dch + 1) * 512],
                            start=False, stop=True)
                    S.inc(last, "te", 1)
                emit("tensor", t_y)
                te_y = S.val("te")
                prev_otld_reader[half] = te_y

                def v_y(v, tt=tt, half=half, te_y=te_y):
                    S.wait(v, "te", te_y)
                    S.dwait(v, res_l[-1])
                    last = None
                    for dch in range(4):
                        src2 = resid_buf[:, half * 2048 + dch * 512:
                                         half * 2048 + (dch + 1) * 512]
                        last = v.tensor_tensor(
                            dst_bigA[:, tt * 2048 + dch * 512:
                                     tt * 2048 + (dch + 1) * 512],
                            PG[dch], src2, ALU.add)
                    S.inc(last, "ve", 1)
                emit("vector", v_y)
                prev_y_evac[0] = S.val("ve")
                ve_y = S.val("ve")

                if write_x2sp:
                    def s_x2(sy, tt=tt, ve_y=ve_y):
                        S.wait(sy, "ve", ve_y)
                        S.dma(sy, "st", out=scr["x2sp"][tt, :, :],
                              in_=dst_bigA[:, tt * 2048:(tt + 1) * 2048])
                    emit("sync", s_x2)
            return S.val("ve"), S.val("te")

        def load_brow(src_name, width=2048):
            mk = []

            def g_b(g):
                S.wait(g, "te")
                mk.append(S.dma(g, "gld", out=brow[0:1, 0:width],
                                in_=wts[src_name].rearrange("(o a) -> o a", o=1)))
            emit("gpsimd", g_b)
            return mk[0]

        wprep_state = {"wtr": [], "gl_ch_acc": [], "k1_se": [], "wst": [],
                       "fm": []}

        def emit_ffn_wprep_tile(ot, first_se, wnat_free, wtt_free):
            wtr = wprep_state["wtr"]
            ch_acc = wprep_state["gl_ch_acc"]
            k1_se = wprep_state["k1_se"]
            wst = wprep_state["wst"]
            fm = []

            def s_ld(sy, ot=ot):
                if ot < 2:
                    S.wait(sy, "se", first_se)
                else:
                    S.wait(sy, "ve", k1_se[ot - 2])
                fm.append(S.dma(
                    sy, "sld",
                    out=stage32[:, (ot % 2) * 2048:(ot % 2 + 1) * 2048],
                    in_=wts["fW"][ot * 128:(ot + 1) * 128, :]))
            emit("sync", s_ld)

            def v_q(v, ot=ot):
                S.dwait(v, fm[-1])
                r = S.inc(v.reduce_max(sl[:, 9:10],
                                       stage32[:, (ot % 2) * 2048:(ot % 2 + 1) * 2048],
                                       axis=AXX, apply_absolute_value=True),
                          "ve", 1)
                S.wait(v, "ve", r)
                r = S.inc(v.tensor_scalar(wscol[:, ot:ot + 1], sl[:, 9:10],
                                          1e-8, 1.0 / 127.0, ALU.max, ALU.mult),
                          "ve", 1)
                S.wait(v, "ve", r)
                S.inc(v.reciprocal(sl[:, 11:12], wscol[:, ot:ot + 1]), "ve", 1)
            emit("vector", v_q)
            ve_iw = S.val("ve")

            for ch in range(2):
                co = (ot % 2) * 2048 + ch * 1024

                def s_k1(se, co=co, ve_iw=ve_iw, ot=ot, ch=ch,
                         pva=(ch_acc[-1] if ch_acc else 0)):
                    S.wait(se, "ve", ve_iw)
                    if pva:
                        S.wait(se, "ve", pva)
                    if ot == 0 and ch == 0 and wnat_free is not None:
                        S.dwait(se, wnat_free)
                    if wtr and ch == 0:
                        S.dwait(se, wtr[-1])
                    S.inc(se.activation(k1t[:, :], stage32[:, co:co + 1024],
                                        AF.Identity, bias=cst[:, 1:2],
                                        scale=sl[:, 11:12]), "se", 1)
                emit("scalar", s_k1)
                se_k1 = S.val("se")

                def v_r(v, co=co, se_k1=se_k1, ch=ch, ot=ot):
                    S.wait(v, "se", se_k1)
                    r = S.inc(v.scalar_tensor_tensor(
                        aux32[:, 0:1024], stage32[:, co:co + 1024],
                        sl[:, 11:12], k1t[:, :], ALU.mult, ALU.subtract),
                        "ve", 1)
                    S.wait(v, "ve", r)
                    r = S.inc(v.scalar_tensor_tensor(
                        aux32[:, 1024:2048], aux32[:, 0:1024], -127.5,
                        k1t[:, :], ALU.is_ge, ALU.add), "ve", 1)
                    S.wait(v, "ve", r)
                    S.inc(v.tensor_scalar(
                        wnat[:, ch * 1024:(ch + 1) * 1024],
                        aux32[:, 1024:2048], -128.0, 0.0, ALU.add, ALU.add,
                        accum_out=sl[:, 12 + ch:13 + ch]), "ve", 1)
                emit("vector", v_r)
                ch_acc.append(S.val("ve"))

            def v_s(v, ot=ot, m=ch_acc[-1]):
                S.wait(v, "ve", m)
                S.inc(v.tensor_tensor(wsumc[:, ot:ot + 1], sl[:, 12:13],
                                      sl[:, 13:14], ALU.add), "ve", 1)
            emit("vector", v_s)
            ve_wq = S.val("ve")

            def s_tr2(sy, ot=ot, ve_wq=ve_wq):
                S.wait(sy, "ve", ve_wq)
                if ot == 0 and wtt_free is not None:
                    S.dwait(sy, wtt_free)
                if wst:
                    S.dwait(sy, wst[-1])
                wtr.append(S.dma(sy, "tr", transpose_call=True,
                                 out=b3(wtt, 128), in_=wnat[:, :]))
            emit("sync", s_tr2)

            def q_st(sq, ot=ot):
                S.dwait(sq, wtr[-1])
                wst.append(S.dma(
                    sq, "st",
                    out=scr["WTf"].rearrange("p (dc o) -> p dc o", o=D)[
                        :, :, ot * 128:(ot + 1) * 128],
                    in_=b3(wtt, 128)))
            emit("scalar", q_st)
            k1_se.append(ve_wq)

        def emit_wprep_cols():
            def v_c2(v, pv=S.val("ve")):
                S.wait(v, "ve", pv)
                r = S.inc(v.tensor_scalar(betac[:, :], wsumc[:, :], -128.0, None,
                                          ALU.mult), "ve", 1)
                S.wait(v, "ve", r)
                r = S.inc(v.tensor_copy(c2hc[:, :], betac[:, :]), "ve", 1)
                S.wait(v, "ve", r)
                r = S.inc(v.tensor_tensor(wsumc[:, :], betac[:, :], c2hc[:, :],
                                          ALU.subtract), "ve", 1)
                S.wait(v, "ve", r)
                r = S.inc(v.tensor_copy(c2lc[:, :], wsumc[:, :]), "ve", 1)
                S.wait(v, "ve", r)
                r = S.inc(v.reciprocal(betac[:, :], wscol[:, :]), "ve", 1)
                S.wait(v, "ve", r)
                S.inc(v.tensor_tensor(betac[:, :], betac[:, :], fbcol[:, :],
                                      ALU.mult), "ve", 1)
            emit("vector", v_c2)
            return S.val("ve")

        brow_gld = load_brow("abv")
        tr_all_a = emit_ln_to_xnt("x0", bigA, otld[:, 2048:4096], (stage32, 2048))
        ve_z = emit_z_all(bigA, tr_all_a)

        wv_marks = []

        def s_wv(sy):
            S.dwait(sy, wprep_st["aWv"])
            wv_marks.append(S.dma(sy, "sld", out=bigB[:, :], in_=scr["WTva"]))
            S.dwait(sy, at_done[2])
            wv_marks.append(S.dma(sy, "sld", out=ATs[:, :],
                                  in_=scr["ATsc"][:, 2 * D:3 * D]))
        emit("sync", s_wv)
        vsp_done = emit_v_all(bigA, bigB, H, scr["Vspa"], tr_all_a,
                              wv_marks[0], ve_z, brow_gld, zv, wv_marks[1])

        state = {"sparity": 0, "oparity": 0,
                 "trip_last_exp": [0, 0], "ob_last_evac": [0, 0],
                 "et_last_read": [0, 0], "proj_evac_prev": 0}
        ln_a_se = S.val("se")
        pw_tr_last = pw_tr[-1] if pw_tr else None
        pw_st_last = pw_st[-1] if pw_st else None
        ot_st = []
        for h in range(H):
            if upto >= 2:
                emit_ffn_wprep_tile(h, ln_a_se, pw_tr_last, pw_st_last)
            hm = []

            def s_ws(sy, h=h, pe=state["proj_evac_prev"],
                     obm=max(state["ob_last_evac"])):
                if h == 0:
                    S.dwait(sy, wprep_st["aWk"])
                    for mk in vsp_done.values():
                        S.dwait(sy, mk)
                    S.dwait(sy, at_done[1])
                else:
                    S.wait(sy, "ve", max(pe, obm))
                hm.append(S.dma(
                    sy, "sld", out=b3(WS[0], 128),
                    in_=scr["WTqa"].rearrange("p (dc o) -> p dc o", o=D)[
                        :, :, h * 128:(h + 1) * 128]))
                hm.append(S.dma(
                    sy, "sld", out=b3(WS[1], 128),
                    in_=scr["WTka"].rearrange("p (dc o) -> p dc o", o=D)[
                        :, :, h * 128:(h + 1) * 128]))
                hm.append(S.dma(
                    sy, "sld", out=b3(Vh, 132),
                    in_=scr["Vspa"].rearrange("t p (hh w) -> p t hh w", w=132)[
                        :, :, h, :]))
                hm.append(S.dma(
                    sy, "sld", out=ATs[0:16, 0:384],
                    in_=scr["ATsc"].rearrange("r (pj o) -> r pj o", o=D)[
                        :, 0:3, h * 128:(h + 1) * 128]))
            emit("sync", s_ws)

            emit_projT(bigA, WS[0], QTh, bcolq[:, h:h + 1], 0, zq,
                       hm, PG, state, z_ve=ve_z)
            se_k = emit_projT(bigA, WS[1], KTh, bcolk[:, h:h + 1], 128, zk,
                              hm, KGB, state, z_ve=ve_z)

            ve_head = emit_attention_head(QTh, KTh, Vh, ET, False, OSg,
                                          se_k, 0, hm[2], state)

            otr = []

            def s_ot(sy, ve_head=ve_head):
                S.wait(sy, "ve", ve_head)
                if ot_st:
                    S.dwait(sy, ot_st[-1])
                otr.append(S.dma(sy, "tr", transpose_call=True,
                                 out=b3(OTt, 128), in_=OSg[:, :]))
            emit("sync", s_ot)

            def q_ot(sq, h=h):
                S.dwait(sq, otr[0])
                ot_st.append(S.dma(sq, "st", out=scr["OTsc"][h, :, :],
                                   in_=OTt[:, :]))
            emit("scalar", q_ot)
        attn_a_st = ot_st[-1]

        brow_gld = load_brow("abo")
        ve_x1, oproj_a_te = emit_oproj("WToa", wprep_st["aWo"], attn_a_st,
                                       brow_gld, True, "x0", stage32, bigA,
                                       False)



        def emit_rows_and_wsb(ve_cols):
            for ot in range(NT):
                def v_cp(v, ot=ot, tp=S.val("te")):
                    S.wait(v, "ve", ve_cols)
                    S.wait(v, "te", tp)
                    r = S.inc(v.tensor_copy(alc[:, 1:2], betac[:, ot:ot + 1]),
                              "ve", 1)
                    S.wait(v, "ve", r)
                    r = S.inc(v.tensor_copy(alc[:, 2:3], c2hc[:, ot:ot + 1]),
                              "ve", 1)
                    S.wait(v, "ve", r)
                    S.inc(v.tensor_copy(alc[:, 3:4], c2lc[:, ot:ot + 1]), "ve", 1)
                emit("vector", v_cp)
                ve_cp = S.val("ve")

                def t_m(t, ve_cp=ve_cp):
                    S.wait(t, "ve", ve_cp)
                    S.inc(t.matmul(PZ[0:3, 0:128], alc[:, 1:4], identb[:, :],
                                   start=True, stop=True), "te", 1)
                emit("tensor", t_m)
                te_m = S.val("te")

                def v_row(v, ot=ot, te_m=te_m):
                    S.wait(v, "te", te_m)
                    S.inc(v.tensor_copy(rowsB[0:3, ot * 128:(ot + 1) * 128],
                                        PZ[0:3, 0:128]), "ve", 1)
                emit("vector", v_row)
            for ot in range(NT):
                def t_w(t, ot=ot, vp=S.val("ve")):
                    S.wait(t, "ve", vp)
                    S.inc(t.matmul(PZ[0:1, 128:256], wscol[:, ot:ot + 1],
                                   identf[:, :], start=True, stop=True), "te", 1)
                emit("tensor", t_w)
                te_w = S.val("te")

                def v_w2(v, ot=ot, te_w=te_w):
                    S.wait(v, "te", te_w)
                    S.inc(v.tensor_copy(stage32[0:1, 2048 + ot * 128:2048 + (ot + 1) * 128],
                                        PZ[0:1, 128:256]), "ve", 1)
                emit("vector", v_w2)
            ve_wsr = S.val("ve")
            for oc in range(4):
                def t_b(t, oc=oc, vp=ve_wsr):
                    S.wait(t, "ve", vp)
                    S.inc(t.matmul(PG[oc], onesf[0:1, :],
                                   stage32[0:1, 2048 + oc * 512:2048 + (oc + 1) * 512],
                                   start=True, stop=True), "te", 1)
                emit("tensor", t_b)
                te_b = S.val("te")

                def v_b(v, oc=oc, te_b=te_b):
                    S.wait(v, "te", te_b)
                    S.inc(v.tensor_copy(aux32[:, oc * 512:(oc + 1) * 512],
                                        PG[oc]), "ve", 1)
                emit("vector", v_b)
            return S.val("ve")

        def emit_ffn(first_ve, wqt_ready, final_out):
            xq_tr = []
            mm_te = []
            xq_ve = {}
            xq_chain = []
            ep_ve = [first_ve]
            out_st = []
            for tt in range(NT):
                half = tt % 2
                src = bigA[:, tt * 2048:(tt + 1) * 2048]
                pm0 = None
                if tt < 2:
                    pm0 = [("tr", p, S.dc["tr"][p]) for p in (0, 1)
                           if S.dc["tr"][p] > 0]
                se_ln = emit_ln_tile(src, xnt[:, half * 2048:(half + 1) * 2048],
                                     wnat[:, :], pre_mark=pm0,
                                     pre_ve=(ep_ve[0] if tt < 2 else None),
                                     dst_free=None,
                                     dst_free_ve=(xq_ve[tt - 2] if tt >= 2
                                                  else None))

                def v_qx(v, tt=tt, half=half, se_ln=se_ln):
                    S.wait(v, "se", se_ln)
                    r = S.inc(v.reduce_max(sl[:, 9:10],
                                           xnt[:, half * 2048:(half + 1) * 2048],
                                           axis=AXX, apply_absolute_value=True),
                              "ve", 1)
                    S.wait(v, "ve", r)
                    r = S.inc(v.tensor_scalar(sl[:, 10:11], sl[:, 9:10],
                                              1.0 / 127.0, 1e-12, ALU.mult, ALU.max),
                              "ve", 1)
                    S.wait(v, "ve", r)
                    r = S.inc(v.reciprocal(sl[:, 11:12], sl[:, 10:11]), "ve", 1)
                    S.wait(v, "ve", r)
                    if mm_te:
                        S.wait(v, "te", mm_te[-1])
                    S.inc(v.tensor_copy(alc[:, 0:1], sl[:, 11:12]), "ve", 1)
                emit("vector", v_qx)
                ve_iq = S.val("ve")

                def t_al(t, ve_iq=ve_iq):
                    S.wait(t, "ve", ve_iq)
                    S.inc(t.matmul(PZ[0:1, 0:128], alc[:, 0:1], identb[:, :],
                                   start=True, stop=True), "te", 1)
                emit("tensor", t_al)
                te_al = S.val("te")

                def v_al(v, te_al=te_al):
                    S.wait(v, "te", te_al)
                    S.inc(v.tensor_copy(rowsA[0:1, :], PZ[0:1, 0:128]), "ve", 1)
                emit("vector", v_al)
                ve_al = S.val("ve")

                for ch in range(2):
                    xo = half * 2048 + ch * 1024

                    def s_k1(se, xo=xo, ve_iq=ve_iq, tt=tt, ch=ch,
                             pva=(xq_chain[-1] if xq_chain else 0)):
                        S.wait(se, "ve", ve_iq)
                        if pva:
                            S.wait(se, "ve", pva)
                        if tt > 0 and ch == 0:
                            S.dwait(se, xq_tr[-1])
                        S.inc(se.activation(k1t[:, :], xnt[:, xo:xo + 1024],
                                            AF.Identity, bias=cst[:, 1:2],
                                            scale=sl[:, 11:12]), "se", 1)
                    emit("scalar", s_k1)
                    se_k1 = S.val("se")

                    def v_r(v, xo=xo, se_k1=se_k1, ch=ch):
                        S.wait(v, "se", se_k1)
                        r = S.inc(v.scalar_tensor_tensor(
                            stage32[:, ch * 1024:(ch + 1) * 1024],
                            xnt[:, xo:xo + 1024], sl[:, 11:12], k1t[:, :],
                            ALU.mult, ALU.subtract), "ve", 1)
                        S.wait(v, "ve", r)
                        S.inc(v.scalar_tensor_tensor(
                            otld[:, ch * 1024:(ch + 1) * 1024],
                            stage32[:, ch * 1024:(ch + 1) * 1024], -127.5,
                            k1t[:, :], ALU.is_ge, ALU.add), "ve", 1)
                    emit("vector", v_r)
                    xq_chain.append(S.val("ve"))
                ve_xq = S.val("ve")
                xq_ve[tt] = ve_xq

                def s_xtr(sy, tt=tt, ve_xq=ve_xq):
                    S.wait(sy, "ve", ve_xq)
                    if mm_te:
                        S.wait(sy, "te", mm_te[-1])
                    xq_tr.append(S.dma(sy, "tr", transpose_call=True,
                                       out=b3(otld[:, 2048:4096], 128),
                                       in_=otld[:, 0:2048]))
                emit("sync", s_xtr)

                def t_mm(t, tt=tt, ve_al=ve_al):
                    S.dwait(t, xq_tr[-1])
                    S.wait(t, "ve", ve_al)
                    if tt == 0:
                        for mk in wqt_ready:
                            S.dwait(t, mk)
                    S.wait(t, "ve", ep_ve[0])
                    last = None
                    for dc in range(ND):
                        for oc in range(4):
                            last = t.matmul(
                                PG[oc], b3(otld[:, 2048:4096], 128)[:, dc, :],
                                bigB[:, dc * 2048 + oc * 512:dc * 2048 + (oc + 1) * 512],
                                start=(dc == 0), stop=False)
                    for oc in range(4):
                        last = t.matmul(PG[oc], rowsA[0:3, :],
                                        rowsB[0:3, oc * 512:(oc + 1) * 512],
                                        start=False, stop=True)
                    S.inc(last, "te", 1)
                emit("tensor", t_mm)
                mm_te.append(S.val("te"))

                def v_ep(v, tt=tt, half=half, m=mm_te[-1]):
                    S.wait(v, "te", m)
                    if final_out and tt >= 1:
                        S.dwait(v, out_st[tt - 1])
                    last = None
                    for oc in range(4):
                        r = S.inc(v.scalar_tensor_tensor(
                            stage32[:, 2048 + oc * 512:2048 + (oc + 1) * 512],
                            PG[oc], sl[:, 10:11], aux32[:, oc * 512:(oc + 1) * 512],
                            ALU.mult, ALU.mult), "ve", 1)
                        S.wait(v, "ve", r)
                        last = v.tensor_tensor(
                            (stage32[:, 2048 + oc * 512:2048 + (oc + 1) * 512]
                             if final_out else
                             bigA[:, tt * 2048 + oc * 512:tt * 2048 + (oc + 1) * 512]),
                            stage32[:, 2048 + oc * 512:2048 + (oc + 1) * 512],
                            bigA[:, tt * 2048 + oc * 512:tt * 2048 + (oc + 1) * 512],
                            ALU.add)
                    S.inc(last, "ve", 1)
                emit("vector", v_ep)
                ep_ve[0] = S.val("ve")
                ve_y = S.val("ve")

                if final_out:
                    def s_o(sy, tt=tt, ve_y=ve_y):
                        S.wait(sy, "ve", ve_y)
                        out_st.append(S.dma(sy, "st",
                                            out=out_t[tt * 128:(tt + 1) * 128, :],
                                            in_=stage32[:, 2048:4096]))
                    emit("sync", s_o)
                else:
                    def s_o(sy, tt=tt, ve_y=ve_y):
                        S.wait(sy, "ve", ve_y)
                        out_st.append(S.dma(sy, "st",
                                            out=scr["x2sp"][tt, :, :],
                                            in_=bigA[:, tt * 2048:(tt + 1) * 2048]))
                    emit("sync", s_o)
            return S.val("ve"), out_st, mm_te[-1]

        if upto >= 2:
            ve_cols = emit_wprep_cols()
            ve_rows = emit_rows_and_wsb(ve_cols)
            wst = wprep_state["wst"]

            wq_marks = []

            def s_wq(sy):
                S.dwait(sy, wst[-1])
                S.dwait(sy, wst[-2])
                S.wait(sy, "te", oproj_a_te)
                wq_marks.append(S.dma(sy, "sld", out=bigB[:, :], in_=scr["WTf"]))
            emit("sync", s_wq)

            ve_x2, x2_st, ffn1_te = emit_ffn(max(ve_x1, ve_rows), wq_marks, False)

        if upto >= 3:
            def g_tr(g):
                S.wait(g, "ve", ve_x2)
                S.inc(g.iota(stage32[0:1, 0:2048], pattern=[[1, 2048]], base=0,
                             channel_multiplier=0,
                             allow_small_or_imprecise_dtypes=True), "gp", 1)
            emit("gpsimd", g_tr)
            gp_tr = S.val("gp")

            def v_dp(v):
                S.wait(v, "ve", ve_x2)
                r = S.inc(v.tensor_scalar(k1t[:, 0:1], iotp[:, 0:1], 0.015625,
                                          None, ALU.mult), "ve", 1)
                S.wait(v, "ve", r)
                r = S.inc(v.scalar_tensor_tensor(sl[:, 9:10], k1t[:, 0:1], -64.0,
                                                 iotp[:, 0:1], ALU.mult, ALU.add),
                          "ve", 1)
                S.wait(v, "ve", r)
                r = S.inc(v.tensor_scalar(sl[:, 10:11], sl[:, 9:10], 0.0, None,
                                          ALU.is_lt), "ve", 1)
                S.wait(v, "ve", r)
                S.inc(v.scalar_tensor_tensor(sl[:, 11:12], sl[:, 10:11], 64.0,
                                             sl[:, 9:10], ALU.mult, ALU.add),
                      "ve", 1)
            emit("vector", v_dp)
            ve_dp = S.val("ve")

            def s_inv(se):
                S.wait(se, "ve", ve_dp)
                S.inc(se.activation(sl[:, 12:13], sl[:, 11:12], AF.Exp,
                                    bias=cst[:, 4:5], scale=EXPSC), "se", 1)
            emit("scalar", s_inv)
            se_inv = S.val("se")

            def v_invc(v, se_inv=se_inv, tp=S.val("te")):
                S.wait(v, "se", se_inv)
                S.wait(v, "te", tp)
                S.inc(v.tensor_copy(sl[:, 13:14], sl[:, 12:13]), "ve", 1)
            emit("vector", v_invc)
            ve_invc = S.val("ve")

            def t_inv(t, ve_invc=ve_invc):
                S.wait(t, "ve", ve_invc)
                S.inc(t.matmul(PZ[0:1, 128:256], sl[:, 13:14], identf[:, :],
                               start=True, stop=True), "te", 1)
            emit("tensor", t_inv)
            te_inv = S.val("te")

            def v_invr(v, te_inv=te_inv):
                S.wait(v, "te", te_inv)
                S.inc(v.tensor_copy(stage32[0:1, 2048:2176], PZ[0:1, 128:256]),
                      "ve", 1)
            emit("vector", v_invr)
            ve_invr = S.val("ve")

            th_ve = []
            for tc in range(4):
                def t_th(t, tc=tc, ve_invr=ve_invr, gp_tr=gp_tr):
                    S.wait(t, "ve", ve_invr)
                    S.wait(t, "gp", gp_tr)
                    if tc >= 1:
                        S.wait(t, "ve", th_ve[-1])
                    S.inc(t.matmul(PG[tc], stage32[0:1, 2048:2176],
                                   stage32[0:1, tc * 512:(tc + 1) * 512],
                                   start=True, stop=True), "te", 1)
                emit("tensor", t_th)
                te_th = S.val("te")

                def v_th(v, tc=tc, te_th=te_th):
                    S.wait(v, "te", te_th)
                    S.inc(v.tensor_copy(aux32[:, tc * 512:(tc + 1) * 512],
                                        PG[tc]), "ve", 1)
                emit("vector", v_th)
                th_ve.append(S.val("ve"))
            sin_done = []
            for part in range(2):
                for ch in range(2):
                    o = ch * 1024

                    def v_u(v, part=part, ch=ch, o=o, pv=th_ve[-1],
                            sins=list(sin_done)):
                        S.wait(v, "ve", pv)
                        for m in sins:
                            S.wait(v, "se", m)
                        if part == 0:
                            r = S.inc(v.tensor_scalar(
                                stage32[:, 2048 + o:2048 + o + 1024],
                                aux32[:, o:o + 1024], 0.0, None, ALU.add), "ve", 1)
                        else:
                            r = S.inc(v.tensor_scalar(
                                stage32[:, 2048 + o:2048 + o + 1024],
                                aux32[:, o:o + 1024], PI2_HI, PI2_LO,
                                ALU.add, ALU.add), "ve", 1)
                        S.wait(v, "ve", r)
                        r = S.inc(v.tensor_scalar(
                            k1t[:, :], stage32[:, 2048 + o:2048 + o + 1024],
                            INV_2PI, None, ALU.mult), "ve", 1)
                        S.wait(v, "ve", r)
                        r = S.inc(v.scalar_tensor_tensor(
                            stage32[:, 2048 + o:2048 + o + 1024], k1t[:, :],
                            -TWO_PI_HI, stage32[:, 2048 + o:2048 + o + 1024],
                            ALU.mult, ALU.add), "ve", 1)
                        S.wait(v, "ve", r)
                        r = S.inc(v.scalar_tensor_tensor(
                            stage32[:, 2048 + o:2048 + o + 1024], k1t[:, :],
                            -TWO_PI_LO, stage32[:, 2048 + o:2048 + o + 1024],
                            ALU.mult, ALU.add), "ve", 1)
                        S.wait(v, "ve", r)
                        r = S.inc(v.tensor_scalar(
                            k1t[:, :], stage32[:, 2048 + o:2048 + o + 1024],
                            PI_F32, None, ALU.is_ge), "ve", 1)
                        S.wait(v, "ve", r)
                        r = S.inc(v.scalar_tensor_tensor(
                            stage32[:, 2048 + o:2048 + o + 1024], k1t[:, :],
                            -TWO_PI_HI, stage32[:, 2048 + o:2048 + o + 1024],
                            ALU.mult, ALU.add), "ve", 1)
                        S.wait(v, "ve", r)
                        r = S.inc(v.scalar_tensor_tensor(
                            stage32[:, 2048 + o:2048 + o + 1024], k1t[:, :],
                            -TWO_PI_LO, stage32[:, 2048 + o:2048 + o + 1024],
                            ALU.mult, ALU.add), "ve", 1)
                        S.wait(v, "ve", r)
                        r = S.inc(v.tensor_scalar(
                            k1t[:, :], stage32[:, 2048 + o:2048 + o + 1024],
                            -PI_F32, None, ALU.is_lt), "ve", 1)
                        S.wait(v, "ve", r)
                        r = S.inc(v.scalar_tensor_tensor(
                            stage32[:, 2048 + o:2048 + o + 1024], k1t[:, :],
                            TWO_PI_HI, stage32[:, 2048 + o:2048 + o + 1024],
                            ALU.mult, ALU.add), "ve", 1)
                        S.wait(v, "ve", r)
                        r = S.inc(v.scalar_tensor_tensor(
                            stage32[:, 2048 + o:2048 + o + 1024], k1t[:, :],
                            TWO_PI_LO, stage32[:, 2048 + o:2048 + o + 1024],
                            ALU.mult, ALU.add), "ve", 1)
                        S.wait(v, "ve", r)
                        S.inc(v.tensor_scalar(
                            stage32[:, 2048 + o:2048 + o + 1024],
                            stage32[:, 2048 + o:2048 + o + 1024],
                            -PI_CLAMP, PI_CLAMP, ALU.max, ALU.min), "ve", 1)
                    emit("vector", v_u)
                    ve_u = S.val("ve")

                    def s_sin(se, part=part, ch=ch, o=o, ve_u=ve_u):
                        S.wait(se, "ve", ve_u)
                        S.inc(se.activation(
                            stage32[:, o:o + 1024] if part == 0 else
                            stage32[:, 2048 + o:2048 + o + 1024],
                            stage32[:, 2048 + o:2048 + o + 1024], AF.Sin,
                            bias=cst[:, 4:5], scale=1.0), "se", 1)
                    emit("scalar", s_sin)
                    sin_done.append(S.val("se"))
            def v_ct(v, sd=sin_done[-1]):
                for m in sin_done:
                    S.wait(v, "se", m)
                S.inc(v.tensor_tensor(aux32[:, :], stage32[:, 0:2048],
                                      stage32[:, 2048:4096], ALU.add), "ve", 1)
            emit("vector", v_ct)
            ve_ct = S.val("ve")

            tr_all_b = emit_ln_to_xnt("x2sp", bigA, wnat[:, :], (otld, 2048),
                                      src_ready=x2_st, dst_free_marks=x2_st[-2:])

            kv_marks = []

            def s_kv(sy):
                S.wait(sy, "te", ffn1_te)
                S.dwait(sy, wprep_st["bWk"])
                S.dwait(sy, wprep_st["bWv"])
                kv_marks.append(S.dma(sy, "sld", out=bigB[:, 0:8192],
                                      in_=scr["WTkb"]))
                kv_marks.append(S.dma(sy, "sld", out=bigB[:, 8192:16384],
                                      in_=scr["WTvb"]))
            emit("sync", s_kv)

            brow_gld_b = load_brow("bbv", 512)

            kt_st = []
            state_b = {"sparity": 0, "oparity": 0,
                       "trip_last_exp": [0, 0], "ob_last_evac": [0, 0],
                       "et_last_read": [0, 0], "proj_evac_prev": 0}
            for g in range(KVH):
                def t_kg(t, g=g, pe=S.val("se"), pv=S.val("ve")):
                    if g == 0:
                        for mk in tr_all_b:
                            S.dwait(t, mk)
                        S.dwait(t, kv_marks[0])
                        S.wait(t, "ve", ve_x2)
                    else:
                        S.wait(t, "ve", pv)
                    last = None
                    for dc in range(ND):
                        for tc2 in range(4):
                            last = t.matmul(
                                PG[tc2],
                                bigB[:, dc * 512 + g * 128:dc * 512 + (g + 1) * 128],
                                bigA[:, dc * 2048 + tc2 * 512:dc * 2048 + (tc2 + 1) * 512],
                                start=(dc == 0), stop=(dc == ND - 1))
                    S.inc(last, "te", 1)
                emit("tensor", t_kg)
                te_kg = S.val("te")

                def s_ke(se, g=g, te_kg=te_kg):
                    S.wait(se, "te", te_kg)
                    if kt_st:
                        S.dwait(se, kt_st[-1])
                    last = None
                    for tc2 in range(4):
                        last = se.activation(KTh[:, tc2 * 512:(tc2 + 1) * 512],
                                             PG[tc2], AF.Identity,
                                             bias=bkbc[:, g:g + 1], scale=1.0)
                    S.inc(last, "se", 1)
                emit("scalar", s_ke)
                se_ke = S.val("se")

                def v_kr(v, g=g, se_ke=se_ke):
                    S.wait(v, "se", se_ke)
                    S.wait(v, "ve", ve_ct)
                    S.inc(v.tensor_tensor(KTh[:, :], KTh[:, :], aux32[:, :],
                                          ALU.mult), "ve", 1)
                emit("vector", v_kr)
                ve_kr = S.val("ve")

                def s_ks(sy, g=g, ve_kr=ve_kr):
                    S.wait(sy, "ve", ve_kr)
                    kt_st.append(S.dma(sy, "st", out=scr["KTbs"][g, :, :],
                                       in_=KTh[:, :]))
                emit("sync", s_ks)
            kt_last = S.val("ve")

            vsp_b = emit_v_all(bigA, bigB[:, 8192:16384], KVH, scr["Vspb"],
                               tr_all_b, kv_marks[1], kt_last, brow_gld_b,
                               None, None)

            oht_st = []
            for h in range(H):
                g = h // 4
                hm2 = []

                def s_wsb(sy, h=h, g=g, pe=state_b["proj_evac_prev"],
                          obm=max(state_b["ob_last_evac"])):
                    if h == 0:
                        S.dwait(sy, wprep_st["bWq"])
                        for mk in vsp_b.values():
                            S.dwait(sy, mk)
                    else:
                        S.wait(sy, "ve", max(pe, obm))
                    hm2.append(S.dma(
                        sy, "sld", out=b3(WS[0], 128),
                        in_=scr["WTqb"].rearrange("p (dc o) -> p dc o", o=D)[
                            :, :, h * 128:(h + 1) * 128]))
                    if h % 4 == 0:
                        S.dwait(sy, kt_st[g])
                        hm2.append(S.dma(sy, "sld", out=KTh[:, :],
                                         in_=scr["KTbs"][g, :, :]))
                        hm2.append(S.dma(
                            sy, "sld", out=b3(Vh, 132),
                            in_=scr["Vspb"].rearrange(
                                "t p (hh w) -> p t hh w", w=132)[:, :, g, :]))
                emit("sync", s_wsb)
                vh_m = hm2[2] if h % 4 == 0 else last_vh[0]
                kt_m = hm2[1] if h % 4 == 0 else None
                if h % 4 == 0:
                    last_vh = [hm2[2]]

                emit_projT(bigA, WS[0], QTh, bqbc[:, h:h + 1], None, None,
                           [hm2[0]], PG, state_b, rope=True, z_ve=ve_ct)
                se_q = state_b["rope_ve"]
                ve_rope = 0

                ve_head = emit_attention_head(QTh, KTh, Vh, ET, True, OSg,
                                              se_q, ve_rope, vh_m, state_b,
                                              kt_mark=kt_m)

                otr2 = []

                def s_ot2(sy, ve_head=ve_head):
                    S.wait(sy, "ve", ve_head)
                    if oht_st:
                        S.dwait(sy, oht_st[-1])
                    otr2.append(S.dma(sy, "tr", transpose_call=True,
                                      out=b3(OTt, 128), in_=OSg[:, :]))
                emit("sync", s_ot2)

                def q_ot2(sq, h=h):
                    S.dwait(sq, otr2[0])
                    oht_st.append(S.dma(sq, "st", out=scr["OTsc"][h, :, :],
                                        in_=OTt[:, :]))
                emit("scalar", q_ot2)

            brow_gld2 = load_brow("bbo")
            ve_x3, oproj_b_te = emit_oproj("WTob", wprep_st["bWo"], oht_st[-1],
                                           brow_gld2, False, "x2sp", xnt, bigA,
                                           False)

        if upto >= 4:
            ve_rows2 = emit_rows_and_wsb(ve_x3)
            wq2_marks = []

            def s_wq2(sy):
                S.wait(sy, "te", oproj_b_te)
                wq2_marks.append(S.dma(sy, "sld", out=bigB[:, :], in_=scr["WTf"]))
            emit("sync", s_wq2)

            ve_x4, out_st4, _ = emit_ffn(max(ve_x3, ve_rows2), wq2_marks, True)

        def emit_debug_out(src_big):
            dbg_st = []
            for tt in range(NT):
                half = tt % 2

                def s_cp(se, tt=tt, half=half):
                    S.wait(se, "ve")
                    if tt >= 2:
                        S.dwait(se, dbg_st[tt - 2])
                    S.inc(se.activation(
                        stage32[:, half * 2048:(half + 1) * 2048],
                        src_big[:, tt * 2048:(tt + 1) * 2048],
                        AF.Identity, bias=cst[:, 4:5], scale=1.0), "se", 1)
                emit("scalar", s_cp)
                se_c = S.val("se")

                def s_st(sy, tt=tt, half=half, se_c=se_c):
                    S.wait(sy, "se", se_c)
                    dbg_st.append(S.dma(
                        sy, "st", out=out_t[tt * 128:(tt + 1) * 128, :],
                        in_=stage32[:, half * 2048:(half + 1) * 2048]))
                emit("sync", s_st)

        if upto in (1, 2, 3):
            emit_debug_out(bigA)

        def fin(sy):
            for nm in ["st", "gld", "sld", "tr", "glx", "grs"]:
                S.dwait_all(sy, nm)
            S.wait(sy, "ve")
            S.wait(sy, "se")
            S.wait(sy, "te")
        emit("sync", fin)

    return nc


def make_inputs_per_core(inputs, core):
    x = np.ascontiguousarray(inputs["L0_attn_a_x"][core])
    dm = np.tril(np.ones((128, 128), dtype=np.float32))
    m = {
        "x": x, "dmaski": dm,
        "aWq": inputs["L0_attn_a_Wq"], "aWk": inputs["L0_attn_a_Wk"],
        "aWv": inputs["L0_attn_a_Wv"], "aWo": inputs["L0_attn_a_Wo"],
        "abq": inputs["L0_attn_a_bq"], "abk": inputs["L0_attn_a_bk"],
        "abv": inputs["L0_attn_a_bv"], "abo": inputs["L0_attn_a_bo"],
        "aBq": inputs["L0_attn_a_Bq"], "aBk": inputs["L0_attn_a_Bk"],
        "aBv": inputs["L0_attn_a_Bv"], "aBo": inputs["L0_attn_a_Bo"],
        "aAq": inputs["L0_attn_a_Aq"], "aAk": inputs["L0_attn_a_Ak"],
        "aAv": inputs["L0_attn_a_Av"], "aAo": inputs["L0_attn_a_Ao"],
        "bWq": inputs["L0_attn_b_Wq"], "bWk": inputs["L0_attn_b_Wk"],
        "bWv": inputs["L0_attn_b_Wv"], "bWo": inputs["L0_attn_b_Wo"],
        "bbq": inputs["L0_attn_b_bq"], "bbk": inputs["L0_attn_b_bk"],
        "bbv": inputs["L0_attn_b_bv"], "bbo": inputs["L0_attn_b_bo"],
        "fW": inputs["L0_ffn_weight"], "fb": inputs["L0_ffn_bias"],
    }
    return {k: np.ascontiguousarray(v) for k, v in m.items()}


def kernel(**inputs):
    nc = build_nc(upto=4)
    in_maps = [make_inputs_per_core(inputs, c) for c in range(8)]
    res = run_bass_kernel_spmd(nc, in_maps, core_ids=list(range(8)))
    return np.stack([res.results[c]["out"] for c in range(8)], axis=0)


# version 03
# speedup vs baseline: 1.1254x; 1.0807x over previous
import numpy as np
import ml_dtypes
import concourse.bass as bass
import concourse.mybir as mybir
from concourse.bass_utils import run_bass_kernel_spmd
from contextlib import ExitStack

F32 = mybir.dt.float32
BF16 = mybir.dt.bfloat16
I32 = mybir.dt.int32
AF = mybir.ActivationFunctionType
ALU = mybir.AluOpType
AXX = mybir.AxisListType.X

L, D, NT, ND = 2048, 2048, 16, 16
H, HD, KVH = 16, 128, 4
ISQ = float(128.0 ** -0.5)
ESH = -12.0
EPS = 1e-5
INV_D = 1.0 / 2048.0
TWO_PI_HI = float(np.float32(2 * np.pi))
TWO_PI_LO = float(np.float32(2 * np.pi - np.float64(np.float32(2 * np.pi))))
PI2_HI = float(np.float32(np.pi / 2))
PI2_LO = float(np.float32(np.pi / 2 - np.float64(np.float32(np.pi / 2))))
PI_CLAMP = 3.1415925
PI_F32 = float(np.float32(np.pi))
INV_2PI = float(np.float32(1.0 / (2 * np.pi)))
EXPSC = float(np.float32(-2.0 * np.log(10000.0) / 128.0))


class Sems:

    def __init__(self):
        self.h, self.c = {}, {}
        self.dh, self.dc, self.di = {}, {}, {}

    def add(self, name, handle):
        self.h[name], self.c[name] = handle, 0

    def add_dma(self, name, h0, h1):
        self.dh[name] = [h0, h1]
        self.dc[name] = [0, 0]
        self.di[name] = 0

    def inc(self, inst, name, n=1):
        self.c[name] += n
        inst.then_inc(self.h[name], n)
        return self.c[name]

    def wait(self, eng, name, value=None):
        v = self.c[name] if value is None else value
        if v > 0:
            eng.wait_ge(self.h[name], v)

    def val(self, name):
        return self.c[name]

    def dma(self, eng, name, **kw):
        p = self.di[name] % 2
        self.di[name] += 1
        if self.dc[name][p] > 0:
            eng.wait_ge(self.dh[name][p], self.dc[name][p])
        tr = kw.pop("transpose_call", False)
        if tr:
            inst = eng.dma_start_transpose(kw["out"], kw["in_"])
        else:
            inst = eng.dma_start(**kw)
        self.dc[name][p] += 16
        inst.then_inc(self.dh[name][p], 16)
        return (name, p, self.dc[name][p])

    def dwait(self, eng, mark):
        name, p, v = mark
        eng.wait_ge(self.dh[name][p], v)

    def dwait_all(self, eng, name):
        for p in (0, 1):
            if self.dc[name][p] > 0:
                eng.wait_ge(self.dh[name][p], self.dc[name][p])


def build_nc(upto=4):
    nc = bass.Bass("TRN2", target_bir_lowering=False)

    x_in = nc.dram_tensor("x", [L, D], F32, kind="ExternalInput").ap()
    out_t = nc.dram_tensor("out", [L, D], F32, kind="ExternalOutput").ap()

    wts = {}
    for nm, shp in [
        ("aWq", [D, D]), ("aWk", [D, D]), ("aWv", [D, D]), ("aWo", [D, D]),
        ("abq", [D]), ("abk", [D]), ("abv", [D]), ("abo", [D]),
        ("aBq", [16, D]), ("aBk", [16, D]), ("aBv", [16, D]), ("aBo", [16, D]),
        ("aAq", [D, 16]), ("aAk", [D, 16]), ("aAv", [D, 16]), ("aAo", [D, 16]),
        ("bWq", [D, D]), ("bWk", [512, D]), ("bWv", [512, D]), ("bWo", [D, D]),
        ("bbq", [D]), ("bbk", [512]), ("bbv", [512]), ("bbo", [D]),
        ("fW", [D, D]), ("fb", [D]),
    ]:
        wts[nm] = nc.dram_tensor(nm, shp, F32, kind="ExternalInput").ap()
    dmask_in = nc.dram_tensor("dmaski", [128, 128], F32, kind="ExternalInput").ap()

    scr = {}
    for nm, shp, dt in [
        ("WTqa", [128, ND * D], BF16), ("WTka", [128, ND * D], BF16),
        ("WTva", [128, ND * D], BF16), ("WToa", [128, ND * D], BF16),
        ("WTqb", [128, ND * D], BF16), ("WTob", [128, ND * D], BF16),
        ("WTkb", [128, ND * 512], BF16), ("WTvb", [128, ND * 512], BF16),
        ("WTf", [128, ND * D], BF16),
        ("ATsc", [16, 4 * D], BF16),
        ("Vspa", [NT, 128, H * 132], BF16),
        ("Vspb", [NT, 128, KVH * 132], BF16),
        ("KTbs", [KVH, 128, L], BF16),
        ("OTsc", [H, 128, L], BF16),
        ("x2sp", [NT, 128, D], BF16),
    ]:
        scr[nm] = nc.dram_tensor(nm, shp, dt, kind="Internal").ap()

    ctx = ExitStack()
    with ctx:
        def sbt(name, shape, dtype):
            return ctx.enter_context(nc.sbuf_tensor(name, shape, dtype))

        bigA = sbt("bigA", [128, 32768], BF16)
        bigB = sbt("bigB", [128, 32768], BF16)
        stage32 = sbt("stage32", [128, 4096], F32)
        xnt = sbt("xnt", [128, 4096], BF16)
        vstg = sbt("vstg", [128, 1056], BF16)
        otld = sbt("otld", [128, 4096], BF16)
        k1t = sbt("k1t", [128, 1024], I32)
        aux32 = sbt("aux32", [128, 2048], F32)
        zq = sbt("zq", [16, 2048], BF16)
        zk = sbt("zk", [16, 2048], BF16)
        zv = sbt("zv", [16, 2048], BF16)
        zob = sbt("zob", [16, 256], BF16)
        BTq = sbt("BTq", [128, ND * 16], BF16)
        BTk = sbt("BTk", [128, ND * 16], BF16)
        BTv = sbt("BTv", [128, ND * 16], BF16)
        BTo = sbt("BTo", [128, ND * 16], BF16)
        ATs = sbt("ATs", [16, 2048], BF16)
        rowsB = sbt("rowsB", [3, 2048], BF16)
        rowsA = sbt("rowsA", [3, 128], BF16)
        bcolq = sbt("bcolq", [128, 16], F32)
        bcolk = sbt("bcolk", [128, 16], F32)
        bqbc = sbt("bqbc", [128, 16], F32)
        bkbc = sbt("bkbc", [128, 4], F32)
        fbcol = sbt("fbcol", [128, 16], F32)
        wscol = sbt("wscol", [128, 16], F32)
        wsumc = sbt("wsumc", [128, 16], F32)
        betac = sbt("betac", [128, 16], F32)
        c2hc = sbt("c2hc", [128, 16], BF16)
        c2lc = sbt("c2lc", [128, 16], BF16)
        alc = sbt("alc", [128, 4], BF16)
        sl = sbt("sl", [128, 16], F32)
        cst = sbt("cst", [128, 8], F32)
        dmask = sbt("dmask", [128, 128], BF16)
        identb = sbt("identb", [128, 128], BF16)
        identf = sbt("identf", [128, 128], F32)
        onesf = sbt("onesf", [1, 128], F32)
        onesr = sbt("onesr", [1, 128], BF16)
        iotp = sbt("iotp", [128, 1], F32)
        invd = sbt("invd", [128, 4], F32)
        wnat = sbt("wnat", [128, 2048], BF16)
        wtt = sbt("wtt", [128, 2048], BF16)
        Bnat = otld[0:16, 0:2048]
        brow = rowsB[0:1, :]

        ps_a = ctx.enter_context(nc.psum_tensor("ps_a", [128, 1536], F32))
        ps_b = ctx.enter_context(nc.psum_tensor("ps_b", [128, 1536], F32))
        ps_oa = ctx.enter_context(nc.psum_tensor("ps_oa", [128, 512], F32))
        ps_ob = ctx.enter_context(nc.psum_tensor("ps_ob", [128, 512], F32))
        PG = [ps_a[:, 0:512], ps_a[:, 512:1024], ps_a[:, 1024:1536], ps_b[:, 0:512]]
        KGB = [ps_b[:, 512:1024], ps_b[:, 1024:1536], ps_oa[:, 0:512], ps_ob[:, 0:512]]
        PZ = ps_b[:, 512:1024]
        STRIPLE = [ps_a, ps_b]

        S = Sems()
        for snm in ["te", "ve", "se", "gp"]:
            S.add(snm, ctx.enter_context(nc.semaphore(snm)))
        for snm in ["gld", "sld", "tr", "st", "glx", "grs"]:
            S.add_dma(snm, ctx.enter_context(nc.semaphore(snm + "0")),
                      ctx.enter_context(nc.semaphore(snm + "1")))
        block = ctx.enter_context(nc.Block())

        ET = [bigB[:, 0:8192], bigB[:, 8192:16384]]
        QTh = bigB[:, 16384:18432]
        KTh = bigB[:, 18432:20480]
        Vh = bigB[:, 20480:22592]
        OSg = bigB[:, 22592:24640]
        OTt = bigB[:, 24640:26688]
        WS = [bigB[:, 26688:28736], bigB[:, 28736:30784]]

        def b3(ap, w):
            return ap.rearrange("p (n w) -> p n w", w=w)

        def emit(engname, fn):
            getattr(block, engname)(fn)

        from concourse.hw_specs import get_activation_tables
        _tabs = list(get_activation_tables(nc.m.arch).keys())
        _NLE = _tabs.index("natural_log_exp_and_others")
        _TRIG = _tabs.index("trig_and_small")

        def preload_tbl(se, idx):
            se.add_instruction(mybir.InstLoadActFuncSet(
                name=nc.get_next_instruction_name(),
                act_func_set_id=idx, ins=[], outs=[]))

        def p0_se(se):
            preload_tbl(se, _NLE)
        emit("scalar", p0_se)

        p0_marks = []

        def p0a_g(g):
            g.iota(iotp[:, :], pattern=[[0, 1]], base=0, channel_multiplier=1,
                   allow_small_or_imprecise_dtypes=True)
            S.inc(g.iota(stage32[:, 0:128], pattern=[[1, 128]], base=0,
                         channel_multiplier=0,
                         allow_small_or_imprecise_dtypes=True), "gp", 1)
            p0_marks.append(S.dma(g, "gld", out=dmask[:, :], in_=dmask_in))
            for dst, src, w in [(bcolq, wts["abq"], 16), (bcolk, wts["abk"], 16),
                                (bqbc, wts["bbq"], 16), (bkbc, wts["bbk"], 4),
                                (fbcol, wts["fb"], 16)]:
                p0_marks.append(S.dma(g, "gld", out=dst[:, 0:w],
                                      in_=src.rearrange("(a b) -> b a", b=128),
                                      allow_slow_non_contiguous=True))
        emit("gpsimd", p0a_g)

        def p0a_v(v):
            v.memset(cst[:, 0:1], ESH)
            v.memset(cst[:, 1:2], 128.0)
            v.memset(cst[:, 2:3], EPS)
            v.memset(cst[:, 3:4], -127.5)
            v.memset(cst[:, 4:5], 0.0)
            v.memset(onesr[0:1, :], 1.0)
            v.memset(rowsA[0:3, :], 1.0)
            v.memset(vstg[:, :], 1.0)
            for mk in p0_marks:
                S.dwait(v, mk)
            S.wait(v, "gp", 1)
            v.memset(onesf[0:1, :], 1.0)
            v.tensor_scalar(identf[:, :], stage32[:, 0:128], iotp[:, 0:1],
                            None, ALU.is_equal)
            S.inc(v.tensor_scalar(identb[:, :], stage32[:, 0:128], iotp[:, 0:1],
                                  None, ALU.is_equal), "ve", 1)
        emit("vector", p0a_v)
        ve_consts = S.val("ve")

        at_store_marks = []
        for i, nm in enumerate(["aAq", "aAk", "aAv", "aAo"]):
            ga_mark = []

            def g_a(g, i=i, nm=nm):
                if i > 0:
                    S.dwait(g, at_store_marks[-1])
                ga_mark.append(S.dma(
                    g, "gld",
                    out=b3(wnat[:, 0:256], 16),
                    in_=wts[nm].rearrange("(ot p) r -> p ot r", p=128)))
            emit("gpsimd", g_a)

            for ot in range(16):
                def t_a(t, ot=ot, vp=S.val("ve")):
                    if ot == 0:
                        S.dwait(t, ga_mark[0])
                        S.wait(t, "ve", ve_consts)
                    else:
                        S.wait(t, "ve", vp)
                    S.inc(t.matmul(PZ[0:16, 0:128],
                                   wnat[:, ot * 16:(ot + 1) * 16],
                                   identb[:, :], start=True, stop=True), "te", 1)
                emit("tensor", t_a)
                te_a = S.val("te")

                def v_a(v, ot=ot, te_a=te_a):
                    S.wait(v, "te", te_a)
                    S.inc(v.tensor_scalar(ATs[:, ot * 128:(ot + 1) * 128],
                                          PZ[0:16, 0:128], 0.5, None, ALU.mult),
                          "ve", 1)
                emit("vector", v_a)
            ve_a = S.val("ve")

            def s_at(sy, i=i, ve_a=ve_a):
                S.wait(sy, "ve", ve_a)
                at_store_marks.append(S.dma(
                    sy, "st", out=scr["ATsc"][:, i * D:(i + 1) * D],
                    in_=ATs[:, :]))
            emit("sync", s_at)
        at_done = list(at_store_marks)

        bld_marks, btr_marks = [], []
        for i, bt in enumerate([BTq, BTk, BTv, BTo]):
            def g_b(g, i=i):
                if i > 0:
                    S.dwait(g, btr_marks[-1])
                bld_marks.append(S.dma(
                    g, "gld", out=Bnat,
                    in_=wts[["aBq", "aBk", "aBv", "aBo"][i]]))
            emit("gpsimd", g_b)

            def s_b(sy, i=i, bt=bt):
                S.dwait(sy, bld_marks[-1])
                btr_marks.append(S.dma(sy, "tr", transpose_call=True,
                                       out=b3(bt, 16), in_=Bnat))
            emit("sync", s_b)
        bt_done = list(btr_marks)

        pw_ld, pw_tr, pw_st = [], [], []
        wprep_st = {}

        def prep_weight(wname, dramT, nrows):
            nt = nrows // 128
            for i in range(nt):
                def g_fn(g, i=i, wname=wname):
                    if pw_tr:
                        S.dwait(g, pw_tr[-1])
                    pw_ld.append(S.dma(g, "gld", out=wnat[:, :],
                                       in_=wts[wname][i * 128:(i + 1) * 128, :]))
                emit("gpsimd", g_fn)

                def s_fn(sq, i=i):
                    S.dwait(sq, pw_ld[-1])
                    if pw_st:
                        S.dwait(sq, pw_st[-1])
                    pw_tr.append(S.dma(sq, "tr", transpose_call=True,
                                       out=b3(wtt, 128), in_=wnat[:, :]))
                emit("scalar", s_fn)

                def q_fn(sq, i=i, dramT=dramT, nrows=nrows):
                    S.dwait(sq, pw_tr[-1])
                    pw_st.append(S.dma(
                        sq, "st",
                        out=dramT.rearrange("p (dc o) -> p dc o", o=nrows)[
                            :, :, i * 128:(i + 1) * 128],
                        in_=b3(wtt, 128)))
                emit("scalar", q_fn)
            wprep_st[wname] = pw_st[-1]

        prep_weight("aWv", scr["WTva"], D)
        prep_weight("aWq", scr["WTqa"], D)
        prep_weight("aWk", scr["WTka"], D)
        prep_weight("aWo", scr["WToa"], D)
        if upto >= 3:
            prep_weight("bWk", scr["WTkb"], 512)
            prep_weight("bWv", scr["WTvb"], 512)
            prep_weight("bWq", scr["WTqb"], D)
            prep_weight("bWo", scr["WTob"], D)

        def emit_ln_tile(src_ap, dst_ap, sq_ap, pre_mark=None, dst_free=None,
                         pre_ve=None, dst_free_ve=None):
            def v1(v):
                if pre_mark:
                    for mk in pre_mark:
                        S.dwait(v, mk)
                if pre_ve:
                    S.wait(v, "ve", pre_ve)
                rs = S.inc(v.reduce_sum(sl[:, 0:1], src_ap, axis=AXX), "ve", 1)
                S.wait(v, "ve", rs)
                rs = S.inc(v.scalar_tensor_tensor(sq_ap, src_ap, 1.0, src_ap,
                                                  ALU.mult, ALU.mult,
                                                  accum_out=sl[:, 1:2]), "ve", 1)
                S.wait(v, "ve", rs)
                rs = S.inc(v.tensor_scalar(sl[:, 2:3], sl[:, 0:1], INV_D, None,
                                           ALU.mult), "ve", 1)
                S.wait(v, "ve", rs)
                rs = S.inc(v.tensor_tensor(sl[:, 4:5], sl[:, 2:3], sl[:, 2:3],
                                           ALU.mult), "ve", 1)
                S.wait(v, "ve", rs)
                S.inc(v.scalar_tensor_tensor(sl[:, 5:6], sl[:, 1:2], INV_D,
                                             sl[:, 4:5], ALU.mult, ALU.subtract),
                      "ve", 1)
            emit("vector", v1)
            ve_var = S.val("ve")

            def s2(se):
                S.wait(se, "ve", ve_var)
                r = S.inc(se.activation(sl[:, 6:7], sl[:, 5:6], AF.Ln,
                                        bias=cst[:, 2:3], scale=1.0), "se", 1)
                S.wait(se, "se", r)
                S.inc(se.activation(sl[:, 7:8], sl[:, 6:7], AF.Exp,
                                    bias=cst[:, 4:5], scale=-0.5), "se", 1)
            emit("scalar", s2)
            se_sd = S.val("se")

            def v3(v):
                S.wait(v, "se", se_sd)
                S.inc(v.tensor_scalar(sl[:, 8:9], sl[:, 2:3], sl[:, 7:8], -1.0,
                                      ALU.mult, ALU.mult), "ve", 1)
            emit("vector", v3)
            ve_nm = S.val("ve")

            def s3(se):
                S.wait(se, "ve", ve_nm)
                if dst_free is not None:
                    S.dwait(se, dst_free)
                if dst_free_ve:
                    S.wait(se, "ve", dst_free_ve)
                S.inc(se.activation(dst_ap, src_ap, AF.Identity,
                                    bias=sl[:, 8:9], scale=sl[:, 7:8]), "se", 1)
            emit("scalar", s3)
            return S.val("se")

        def emit_ln_to_xnt(src_kind, xnt_dst, sq_ap, ld_halves,
                           src_ready=None, sq_free=None, dst_free_marks=None):
            prev_se = [0, 0]
            tr_marks = []
            for tt in range(NT):
                half = tt % 2
                ldt, ldw = ld_halves
                ld_buf = ldt[:, half * ldw:(half + 1) * ldw]
                src = (x_in[tt * 128:(tt + 1) * 128, :] if src_kind == "x0"
                       else scr["x2sp"][tt, :, :])

                lm = []

                def g_ld(g, ld_buf=ld_buf, src=src, tt=tt, pse=prev_se[half]):
                    if tt < 2:
                        S.wait(g, "ve", ve_consts)
                    else:
                        S.wait(g, "se", pse)
                    if src_ready is not None:
                        S.dwait(g, src_ready[tt])
                    lm.append(S.dma(g, "glx", out=ld_buf, in_=src))
                emit("gpsimd", g_ld)

                se_done = emit_ln_tile(
                    ld_buf, xnt[:, half * 2048:(half + 1) * 2048], sq_ap,
                    pre_mark=(lm + (sq_free or []) if tt < 2 else lm),
                    dst_free=(tr_marks[tt - 2] if tt >= 2 else None))
                prev_se[half] = se_done

                def s_tr(sy, tt=tt, half=half, se_done=se_done):
                    if tt == 0 and dst_free_marks:
                        for mk in dst_free_marks:
                            S.dwait(sy, mk)
                    S.wait(sy, "se", se_done)
                    tr_marks.append(S.dma(
                        sy, "tr", transpose_call=True,
                        out=b3(xnt_dst, 2048)[:, :, tt * 128:(tt + 1) * 128],
                        in_=xnt[:, half * 2048:(half + 1) * 2048]))
                emit("sync", s_tr)
            return tr_marks[-2:]

        def emit_z_all(xnT, tr_xnt_all):
            for i, (bt, zdst) in enumerate(zip([BTq, BTk, BTv], [zq, zk, zv])):
                ve_prev = S.val("ve")

                def t_fn(t, i=i, bt=bt, ve_prev=ve_prev):
                    if i == 0:
                        for mk in tr_xnt_all:
                            S.dwait(t, mk)
                        for mk in bt_done:
                            S.dwait(t, mk)
                        S.wait(t, "ve", ve_consts)
                    else:
                        S.wait(t, "ve", ve_prev)
                    last = None
                    for dc in range(ND):
                        for tc in range(4):
                            last = t.matmul(
                                PG[tc][0:16, :], b3(bt, 16)[:, dc, :],
                                xnT[:, dc * 2048 + tc * 512:dc * 2048 + (tc + 1) * 512],
                                start=(dc == 0), stop=(dc == ND - 1))
                    S.inc(last, "te", 1)
                emit("tensor", t_fn)
                te_z = S.val("te")

                def v_fn(v, zdst=zdst, te_z=te_z):
                    S.wait(v, "te", te_z)
                    last = None
                    for tc in range(4):
                        last = v.tensor_copy(zdst[:, tc * 512:(tc + 1) * 512],
                                             PG[tc][0:16, :])
                    S.inc(last, "ve", 1)
                emit("vector", v_fn)
            return S.val("ve")

        def emit_v_all(xnT, wvT_sb, nheads, vsp, tr_xnt_all, wv_ld_mark,
                       first_ve_wait, brow_gld, zvbuf, av_ready_sld):
            noc = (nheads * 128) // 512
            vst_marks = {}
            vhalf_free = {}
            for tt in range(NT):
                def t_fn(t, tt=tt):
                    if tt == 0:
                        for mk in tr_xnt_all:
                            S.dwait(t, mk)
                        S.dwait(t, wv_ld_mark)
                        S.wait(t, "ve", first_ve_wait)
                        S.dwait(t, brow_gld)
                        if av_ready_sld:
                            S.dwait(t, av_ready_sld)
                    for oc in range(noc):
                        gidx = tt * noc + oc
                        if gidx >= 4:
                            S.wait(t, "ve", vhalf_free[gidx - 4])
                    last = None
                    for dc in range(ND):
                        for oc in range(noc):
                            last = t.matmul(
                                PG[(tt * noc + oc) % 4],
                                b3(xnT, 2048)[:, dc, tt * 128:(tt + 1) * 128],
                                wvT_sb[:, dc * (nheads * 128) + oc * 512:
                                       dc * (nheads * 128) + (oc + 1) * 512],
                                start=(dc == 0), stop=False)
                    for oc in range(noc):
                        gidx = tt * noc + oc
                        if zvbuf is not None:
                            t.matmul(PG[gidx % 4],
                                     zvbuf[0:16, tt * 128:(tt + 1) * 128],
                                     ATs[0:16, oc * 512:(oc + 1) * 512],
                                     start=False, stop=False)
                        last = t.matmul(PG[gidx % 4], onesr[0:1, :],
                                        brow[0:1, oc * 512:(oc + 1) * 512],
                                        start=False, stop=True)
                        S.inc(last, "te", 1)
                emit("tensor", t_fn)
                te_last = S.val("te")

                for oc in range(noc):
                    gidx = tt * noc + oc
                    half = gidx % 2
                    te_oc = te_last - (noc - 1 - oc)

                    def v_fn(v, tt=tt, oc=oc, gidx=gidx, half=half, te_oc=te_oc):
                        S.wait(v, "te", te_oc)
                        if gidx >= 2:
                            S.dwait(v, vst_marks[gidx - 2])
                        S.inc(v.tensor_copy(
                            b3(vstg[:, half * 528:(half + 1) * 528], 132)[:, :, 0:128],
                            b3(PG[gidx % 4], 128)), "ve", 1)
                    emit("vector", v_fn)
                    vhalf_free[gidx] = S.val("ve")

                    def s_sp(sy, tt=tt, oc=oc, half=half, vm=vhalf_free[gidx],
                             gidx2=gidx):
                        S.wait(sy, "ve", vm)
                        vst_marks[gidx2] = S.dma(
                            sy, "st",
                            out=vsp[tt, :, oc * 528:(oc + 1) * 528],
                            in_=vstg[:, half * 528:(half + 1) * 528])
                    emit("sync", s_sp)
            return vst_marks

        def emit_projT(xnT, ws_buf, dst, bias_col_ap, lora_at, zbuf,
                       sld_mark, groups, state, rope=False, z_ve=0):
            def t_fn(t):
                for mk in sld_mark:
                    S.dwait(t, mk)
                if z_ve:
                    S.wait(t, "ve", z_ve)
                S.wait(t, "ve", state["proj_evac_prev"])
                S.wait(t, "se", max(state["trip_last_exp"]))
                S.wait(t, "ve", max(state["ob_last_evac"]))
                last = None
                for dc in range(ND):
                    for tc in range(4):
                        last = t.matmul(
                            groups[tc], b3(ws_buf, 128)[:, dc, :],
                            xnT[:, dc * 2048 + tc * 512:dc * 2048 + (tc + 1) * 512],
                            start=(dc == 0),
                            stop=(dc == ND - 1 and lora_at is None))
                if lora_at is not None:
                    for tc in range(4):
                        last = t.matmul(
                            groups[tc], ATs[0:16, lora_at:lora_at + 128],
                            zbuf[0:16, tc * 512:(tc + 1) * 512],
                            start=False, stop=True)
                S.inc(last, "te", 1)
            emit("tensor", t_fn)
            te_p = S.val("te")

            def v_fn(v):
                S.wait(v, "te", te_p)
                last = None
                for tc in range(4):
                    last = v.tensor_scalar(dst[:, tc * 512:(tc + 1) * 512],
                                           groups[tc], bias_col_ap, None,
                                           ALU.add)
                S.inc(last, "ve", 1)
                if rope:
                    S.wait(v, "ve", S.val("ve"))
                    S.inc(v.tensor_tensor(dst[:, :], dst[:, :], aux32[:, :],
                                          ALU.mult), "ve", 1)
            emit("vector", v_fn)
            state["proj_evac_prev"] = S.val("ve")
            if rope:
                state["rope_ve"] = S.val("ve")
            return S.val("ve")

        def emit_attention_head(qT, kT, vhbuf, etbufs, causal, osg,
                                se_ready, ve_ready, vh_mark, state,
                                kt_mark=None):
            exp_qc, mask_qc = {}, {}

            def emit_s(qc):
                kt_hi = (qc * 4 + 4) if causal else 16
                ngrp = (kt_hi + 2) // 3
                exp_marks = []
                for ktg in range(ngrp):
                    kts = list(range(3 * ktg, min(3 * ktg + 3, kt_hi)))
                    gsz = len(kts)
                    par = state["sparity"] % 2
                    state["sparity"] += 1
                    trip = STRIPLE[par]
                    need_exp = state["trip_last_exp"][par]

                    def t_s(t, kts=kts, trip=trip, need_exp=need_exp, qc=qc,
                            first=(ktg == 0 and qc == 0)):
                        if first:
                            S.wait(t, "ve", se_ready)
                            if ve_ready:
                                S.wait(t, "ve", ve_ready)
                            if kt_mark is not None:
                                S.dwait(t, kt_mark)
                        S.wait(t, "se", need_exp)
                        inst = None
                        for j, kt in enumerate(kts):
                            inst = t.matmul(
                                trip[:, j * 512:(j + 1) * 512],
                                kT[:, kt * 128:(kt + 1) * 128],
                                qT[:, qc * 512:(qc + 1) * 512],
                                start=True, stop=True)
                        S.inc(inst, "te", 1)
                    emit("tensor", t_s)
                    te_s = S.val("te")

                    etdst = etbufs[qc % 2]
                    oread = state["et_last_read"][qc % 2]

                    def s_e(se, te_s=te_s, trip=trip, gsz=gsz, etdst=etdst,
                            ktg=ktg, oread=oread):
                        S.wait(se, "te", te_s)
                        S.wait(se, "te", oread)
                        S.inc(se.activation(
                            etdst[:, (3 * ktg) * 512:(3 * ktg + gsz) * 512],
                            trip[:, 0:gsz * 512], AF.Exp,
                            bias=cst[:, 0:1], scale=ISQ), "se", 1)
                    emit("scalar", s_e)
                    exp_marks.append(S.val("se"))
                    state["trip_last_exp"][par] = S.val("se")
                exp_qc[qc] = exp_marks

                if causal:
                    mm = {}
                    for qt_l in range(4):
                        qt = qc * 4 + qt_l
                        kt = qt
                        gi = kt // 3
                        off = kt * 512 + qt_l * 128

                        def v_m(v, off=off, qc=qc, em=exp_marks[gi]):
                            S.wait(v, "se", em)
                            S.inc(v.tensor_tensor(
                                etbufs[qc % 2][:, off:off + 128],
                                etbufs[qc % 2][:, off:off + 128],
                                dmask[:, :], ALU.mult), "ve", 1)
                        emit("vector", v_m)
                        mm[qt_l] = S.val("ve")
                    mask_qc[qc] = mm

            def emit_o(qc):
                for qt_l in range(4):
                    qt = qc * 4 + qt_l
                    kts = list(range(qt + 1)) if causal else list(range(16))
                    opar = state["oparity"] % 2
                    state["oparity"] += 1
                    ob = [ps_oa, ps_ob][opar]
                    ev_prev = state["ob_last_evac"][opar]
                    mm = mask_qc.get(qc, {}).get(qt_l, 0)

                    def t_o(t, qt_l=qt_l, kts=kts, ob=ob, qc=qc, ev_prev=ev_prev,
                            em=exp_qc[qc][-1], mm=mm,
                            first=(qc == 0 and qt_l == 0)):
                        if first and vh_mark is not None:
                            S.dwait(t, vh_mark)
                        S.wait(t, "se", em)
                        if mm:
                            S.wait(t, "ve", mm)
                        S.wait(t, "ve", ev_prev)
                        inst = None
                        for idx, kt in enumerate(kts):
                            inst = t.matmul(
                                ob[:, 0:129],
                                etbufs[qc % 2][:, kt * 512 + qt_l * 128:
                                               kt * 512 + qt_l * 128 + 128],
                                b3(vhbuf, 132)[:, kt, 0:129],
                                start=(idx == 0), stop=(idx == len(kts) - 1))
                        S.inc(inst, "te", 1)
                    emit("tensor", t_o)
                    te_o = S.val("te")
                    state["et_last_read"][qc % 2] = te_o

                    def v_o(v, qt=qt, ob=ob, te_o=te_o):
                        S.wait(v, "te", te_o)
                        rc = S.inc(v.reciprocal(invd[:, qt % 4:qt % 4 + 1],
                                                ob[:, 128:129]), "ve", 1)
                        S.wait(v, "ve", rc)
                        S.inc(v.tensor_scalar(
                            osg[:, qt * 128:(qt + 1) * 128], ob[:, 0:128],
                            invd[:, qt % 4:qt % 4 + 1], None, ALU.mult), "ve", 1)
                    emit("vector", v_o)
                    state["ob_last_evac"][opar] = S.val("ve")

            for qc in range(4):
                emit_s(qc)
                if qc >= 1:
                    emit_o(qc - 1)
            emit_o(3)
            return S.val("ve")

        def emit_oproj(wtname, wt_st_mark, attn_done_st, bias_gld, lora,
                       resid_kind, resid_buf, dst_bigA, write_x2sp):
            pre_se = S.val("se")
            pre_trm = [("tr", p, S.dc["tr"][p]) for p in (0, 1)
                       if S.dc["tr"][p] > 0]
            wo_marks = []

            def s_wo(sy):
                S.dwait(sy, wt_st_mark)
                S.dwait(sy, attn_done_st)
                wo_marks.append(S.dma(sy, "sld", out=bigB[:, :], in_=scr[wtname]))
                if lora:
                    S.dwait(sy, at_done[3])
                    wo_marks.append(S.dma(sy, "sld", out=ATs[:, :],
                                          in_=scr["ATsc"][:, 3 * D:4 * D]))
            emit("sync", s_wo)

            zo_ve = {}
            prev_otld_reader = [0, 0]
            prev_y_evac = [0]
            for tt in range(NT):
                half = tt % 2

                otm_l = []

                def s_otl(sy, tt=tt, half=half, pe=prev_otld_reader[half]):
                    if tt == 0:
                        for mk in wo_marks:
                            S.dwait(sy, mk)
                    if tt >= 2:
                        S.wait(sy, "te", pe)
                    otm_l.append(S.dma(
                        sy, "sld",
                        out=otld[:, half * 2048:(half + 1) * 2048],
                        in_=scr["OTsc"].rearrange("h p t -> p h t")[
                            :, :, tt * 128:(tt + 1) * 128]))
                emit("sync", s_otl)

                res_l = []

                def g_res(g, tt=tt, half=half, py=prev_y_evac[0]):
                    if tt < 2:
                        S.wait(g, "se", pre_se)
                        for mk in pre_trm:
                            S.dwait(g, mk)
                    else:
                        S.wait(g, "ve", py)
                    src = (x_in[tt * 128:(tt + 1) * 128, :] if resid_kind == "x0"
                           else scr["x2sp"][tt, :, :])
                    res_l.append(S.dma(
                        g, "grs",
                        out=resid_buf[:, half * 2048:(half + 1) * 2048],
                        in_=src))
                emit("gpsimd", g_res)

                if lora:
                    def t_zo(t, tt=tt, half=half,
                             pz=zo_ve.get(tt - 2, 0)):
                        S.dwait(t, otm_l[-1])
                        S.wait(t, "ve", pz)
                        inst = None
                        for oc in range(16):
                            inst = t.matmul(
                                PZ[0:16, 0:128], b3(BTo, 16)[:, oc, :],
                                otld[:, half * 2048 + oc * 128:
                                     half * 2048 + (oc + 1) * 128],
                                start=(oc == 0), stop=(oc == 15))
                        S.inc(inst, "te", 1)
                    emit("tensor", t_zo)
                    te_zo = S.val("te")

                    def v_zo(v, half=half, te_zo=te_zo):
                        S.wait(v, "te", te_zo)
                        S.inc(v.tensor_copy(zob[:, half * 128:(half + 1) * 128],
                                            PZ[0:16, 0:128]), "ve", 1)
                    emit("vector", v_zo)
                    zo_ve[tt] = S.val("ve")

                def t_y(t, tt=tt, half=half,
                        zv_=zo_ve.get(tt, 0), py=prev_y_evac[0]):
                    S.dwait(t, otm_l[-1])
                    if lora:
                        S.wait(t, "ve", zv_)
                    S.wait(t, "ve", py)
                    S.dwait(t, bias_gld)
                    last = None
                    for oc in range(16):
                        for dch in range(4):
                            last = t.matmul(
                                PG[dch],
                                otld[:, half * 2048 + oc * 128:
                                     half * 2048 + (oc + 1) * 128],
                                bigB[:, oc * 2048 + dch * 512:
                                     oc * 2048 + (dch + 1) * 512],
                                start=(oc == 0), stop=False)
                    for dch in range(4):
                        if lora:
                            last = t.matmul(
                                PG[dch], zob[0:16, half * 128:(half + 1) * 128],
                                ATs[0:16, dch * 512:(dch + 1) * 512],
                                start=False, stop=False)
                        last = t.matmul(
                            PG[dch], onesr[0:1, :],
                            brow[0:1, dch * 512:(dch + 1) * 512],
                            start=False, stop=True)
                    S.inc(last, "te", 1)
                emit("tensor", t_y)
                te_y = S.val("te")
                prev_otld_reader[half] = te_y

                def v_y(v, tt=tt, half=half, te_y=te_y):
                    S.wait(v, "te", te_y)
                    S.dwait(v, res_l[-1])
                    last = None
                    for dch in range(4):
                        src2 = resid_buf[:, half * 2048 + dch * 512:
                                         half * 2048 + (dch + 1) * 512]
                        last = v.tensor_tensor(
                            dst_bigA[:, tt * 2048 + dch * 512:
                                     tt * 2048 + (dch + 1) * 512],
                            PG[dch], src2, ALU.add)
                    S.inc(last, "ve", 1)
                emit("vector", v_y)
                prev_y_evac[0] = S.val("ve")
                ve_y = S.val("ve")

                if write_x2sp:
                    def s_x2(sy, tt=tt, ve_y=ve_y):
                        S.wait(sy, "ve", ve_y)
                        S.dma(sy, "st", out=scr["x2sp"][tt, :, :],
                              in_=dst_bigA[:, tt * 2048:(tt + 1) * 2048])
                    emit("sync", s_x2)
            return S.val("ve"), S.val("te")

        def load_brow(src_name, width=2048):
            mk = []

            def g_b(g):
                S.wait(g, "te")
                mk.append(S.dma(g, "gld", out=brow[0:1, 0:width],
                                in_=wts[src_name].rearrange("(o a) -> o a", o=1)))
            emit("gpsimd", g_b)
            return mk[0]

        wprep_state = {"wtr": [], "gl_ch_acc": [], "k1_se": [], "wst": [],
                       "fm": []}

        def emit_ffn_wprep_tile(ot, first_se, wnat_free, wtt_free):
            wtr = wprep_state["wtr"]
            ch_acc = wprep_state["gl_ch_acc"]
            k1_se = wprep_state["k1_se"]
            wst = wprep_state["wst"]
            fm = []

            def s_ld(sy, ot=ot):
                if ot < 2:
                    S.wait(sy, "se", first_se)
                else:
                    S.wait(sy, "ve", k1_se[ot - 2])
                fm.append(S.dma(
                    sy, "sld",
                    out=stage32[:, (ot % 2) * 2048:(ot % 2 + 1) * 2048],
                    in_=wts["fW"][ot * 128:(ot + 1) * 128, :]))
            emit("sync", s_ld)

            def v_q(v, ot=ot):
                S.dwait(v, fm[-1])
                r = S.inc(v.reduce_max(sl[:, 9:10],
                                       stage32[:, (ot % 2) * 2048:(ot % 2 + 1) * 2048],
                                       axis=AXX, apply_absolute_value=True),
                          "ve", 1)
                S.wait(v, "ve", r)
                r = S.inc(v.tensor_scalar(wscol[:, ot:ot + 1], sl[:, 9:10],
                                          1e-8, 1.0 / 127.0, ALU.max, ALU.mult),
                          "ve", 1)
                S.wait(v, "ve", r)
                S.inc(v.reciprocal(sl[:, 11:12], wscol[:, ot:ot + 1]), "ve", 1)
            emit("vector", v_q)
            ve_iw = S.val("ve")

            for ch in range(2):
                co = (ot % 2) * 2048 + ch * 1024

                def s_k1(se, co=co, ve_iw=ve_iw, ot=ot, ch=ch,
                         pva=(ch_acc[-1] if ch_acc else 0)):
                    S.wait(se, "ve", ve_iw)
                    if pva:
                        S.wait(se, "ve", pva)
                    if ot == 0 and ch == 0 and wnat_free is not None:
                        S.dwait(se, wnat_free)
                    if wtr and ch == 0:
                        S.dwait(se, wtr[-1])
                    S.inc(se.activation(k1t[:, :], stage32[:, co:co + 1024],
                                        AF.Identity, bias=cst[:, 1:2],
                                        scale=sl[:, 11:12]), "se", 1)
                emit("scalar", s_k1)
                se_k1 = S.val("se")

                def v_r(v, co=co, se_k1=se_k1, ch=ch, ot=ot):
                    S.wait(v, "se", se_k1)
                    r = S.inc(v.scalar_tensor_tensor(
                        aux32[:, 0:1024], stage32[:, co:co + 1024],
                        sl[:, 11:12], k1t[:, :], ALU.mult, ALU.subtract),
                        "ve", 1)
                    S.wait(v, "ve", r)
                    r = S.inc(v.scalar_tensor_tensor(
                        aux32[:, 1024:2048], aux32[:, 0:1024], -127.5,
                        k1t[:, :], ALU.is_ge, ALU.add), "ve", 1)
                    S.wait(v, "ve", r)
                    S.inc(v.tensor_scalar(
                        wnat[:, ch * 1024:(ch + 1) * 1024],
                        aux32[:, 1024:2048], -128.0, 0.0, ALU.add, ALU.add,
                        accum_out=sl[:, 12 + ch:13 + ch]), "ve", 1)
                emit("vector", v_r)
                ch_acc.append(S.val("ve"))

            def v_s(v, ot=ot, m=ch_acc[-1]):
                S.wait(v, "ve", m)
                S.inc(v.tensor_tensor(wsumc[:, ot:ot + 1], sl[:, 12:13],
                                      sl[:, 13:14], ALU.add), "ve", 1)
            emit("vector", v_s)
            ve_wq = S.val("ve")

            def s_tr2(sy, ot=ot, ve_wq=ve_wq):
                S.wait(sy, "ve", ve_wq)
                if ot == 0 and wtt_free is not None:
                    S.dwait(sy, wtt_free)
                if wst:
                    S.dwait(sy, wst[-1])
                wtr.append(S.dma(sy, "tr", transpose_call=True,
                                 out=b3(wtt, 128), in_=wnat[:, :]))
            emit("sync", s_tr2)

            def q_st(sq, ot=ot):
                S.dwait(sq, wtr[-1])
                wst.append(S.dma(
                    sq, "st",
                    out=scr["WTf"].rearrange("p (dc o) -> p dc o", o=D)[
                        :, :, ot * 128:(ot + 1) * 128],
                    in_=b3(wtt, 128)))
            emit("scalar", q_st)
            k1_se.append(ve_wq)

        def emit_wprep_cols():
            def v_c2(v, pv=S.val("ve")):
                S.wait(v, "ve", pv)
                r = S.inc(v.tensor_scalar(betac[:, :], wsumc[:, :], -128.0, None,
                                          ALU.mult), "ve", 1)
                S.wait(v, "ve", r)
                r = S.inc(v.tensor_copy(c2hc[:, :], betac[:, :]), "ve", 1)
                S.wait(v, "ve", r)
                r = S.inc(v.tensor_tensor(wsumc[:, :], betac[:, :], c2hc[:, :],
                                          ALU.subtract), "ve", 1)
                S.wait(v, "ve", r)
                r = S.inc(v.tensor_copy(c2lc[:, :], wsumc[:, :]), "ve", 1)
                S.wait(v, "ve", r)
                r = S.inc(v.reciprocal(betac[:, :], wscol[:, :]), "ve", 1)
                S.wait(v, "ve", r)
                S.inc(v.tensor_tensor(betac[:, :], betac[:, :], fbcol[:, :],
                                      ALU.mult), "ve", 1)
            emit("vector", v_c2)
            return S.val("ve")

        brow_gld = load_brow("abv")
        tr_all_a = emit_ln_to_xnt("x0", bigA, otld[:, 2048:4096], (stage32, 2048))
        ve_z = emit_z_all(bigA, tr_all_a)

        wv_marks = []

        def s_wv(sy):
            S.dwait(sy, wprep_st["aWv"])
            wv_marks.append(S.dma(sy, "sld", out=bigB[:, :], in_=scr["WTva"]))
            S.dwait(sy, at_done[2])
            wv_marks.append(S.dma(sy, "sld", out=ATs[:, :],
                                  in_=scr["ATsc"][:, 2 * D:3 * D]))
        emit("sync", s_wv)
        vsp_done = emit_v_all(bigA, bigB, H, scr["Vspa"], tr_all_a,
                              wv_marks[0], ve_z, brow_gld, zv, wv_marks[1])

        state = {"sparity": 0, "oparity": 0,
                 "trip_last_exp": [0, 0], "ob_last_evac": [0, 0],
                 "et_last_read": [0, 0], "proj_evac_prev": 0}
        ln_a_se = S.val("se")
        pw_tr_last = pw_tr[-1] if pw_tr else None
        pw_st_last = pw_st[-1] if pw_st else None
        ot_st = []
        for h in range(H):
            if upto >= 2:
                emit_ffn_wprep_tile(h, ln_a_se, pw_tr_last, pw_st_last)
            hm = []

            def s_ws(sy, h=h, pe=state["proj_evac_prev"],
                     obm=max(state["ob_last_evac"])):
                if h == 0:
                    S.dwait(sy, wprep_st["aWk"])
                    for mk in vsp_done.values():
                        S.dwait(sy, mk)
                    S.dwait(sy, at_done[1])
                else:
                    S.wait(sy, "ve", max(pe, obm))
                hm.append(S.dma(
                    sy, "sld", out=b3(WS[0], 128),
                    in_=scr["WTqa"].rearrange("p (dc o) -> p dc o", o=D)[
                        :, :, h * 128:(h + 1) * 128]))
                hm.append(S.dma(
                    sy, "sld", out=b3(WS[1], 128),
                    in_=scr["WTka"].rearrange("p (dc o) -> p dc o", o=D)[
                        :, :, h * 128:(h + 1) * 128]))
                hm.append(S.dma(
                    sy, "sld", out=b3(Vh, 132),
                    in_=scr["Vspa"].rearrange("t p (hh w) -> p t hh w", w=132)[
                        :, :, h, :]))
                hm.append(S.dma(
                    sy, "sld", out=ATs[0:16, 0:384],
                    in_=scr["ATsc"].rearrange("r (pj o) -> r pj o", o=D)[
                        :, 0:3, h * 128:(h + 1) * 128]))
            emit("sync", s_ws)

            emit_projT(bigA, WS[0], QTh, bcolq[:, h:h + 1], 0, zq,
                       hm, PG, state, z_ve=ve_z)
            se_k = emit_projT(bigA, WS[1], KTh, bcolk[:, h:h + 1], 128, zk,
                              hm, KGB, state, z_ve=ve_z)

            ve_head = emit_attention_head(QTh, KTh, Vh, ET, False, OSg,
                                          se_k, 0, hm[2], state)

            otr = []

            def s_ot(sy, ve_head=ve_head):
                S.wait(sy, "ve", ve_head)
                if ot_st:
                    S.dwait(sy, ot_st[-1])
                otr.append(S.dma(sy, "tr", transpose_call=True,
                                 out=b3(OTt, 128), in_=OSg[:, :]))
            emit("sync", s_ot)

            def q_ot(sq, h=h):
                S.dwait(sq, otr[0])
                ot_st.append(S.dma(sq, "st", out=scr["OTsc"][h, :, :],
                                   in_=OTt[:, :]))
            emit("scalar", q_ot)
        attn_a_st = ot_st[-1]

        brow_gld = load_brow("abo")
        ve_x1, oproj_a_te = emit_oproj("WToa", wprep_st["aWo"], attn_a_st,
                                       brow_gld, True, "x0", stage32, bigA,
                                       False)



        def emit_rows_and_wsb(ve_cols):
            for ot in range(NT):
                def v_cp(v, ot=ot, tp=S.val("te")):
                    S.wait(v, "ve", ve_cols)
                    S.wait(v, "te", tp)
                    r = S.inc(v.tensor_copy(alc[:, 1:2], betac[:, ot:ot + 1]),
                              "ve", 1)
                    S.wait(v, "ve", r)
                    r = S.inc(v.tensor_copy(alc[:, 2:3], c2hc[:, ot:ot + 1]),
                              "ve", 1)
                    S.wait(v, "ve", r)
                    S.inc(v.tensor_copy(alc[:, 3:4], c2lc[:, ot:ot + 1]), "ve", 1)
                emit("vector", v_cp)
                ve_cp = S.val("ve")

                def t_m(t, ve_cp=ve_cp):
                    S.wait(t, "ve", ve_cp)
                    S.inc(t.matmul(PZ[0:3, 0:128], alc[:, 1:4], identb[:, :],
                                   start=True, stop=True), "te", 1)
                emit("tensor", t_m)
                te_m = S.val("te")

                def v_row(v, ot=ot, te_m=te_m):
                    S.wait(v, "te", te_m)
                    S.inc(v.tensor_copy(rowsB[0:3, ot * 128:(ot + 1) * 128],
                                        PZ[0:3, 0:128]), "ve", 1)
                emit("vector", v_row)
            for ot in range(NT):
                def t_w(t, ot=ot, vp=S.val("ve")):
                    S.wait(t, "ve", vp)
                    S.inc(t.matmul(PZ[0:1, 128:256], wscol[:, ot:ot + 1],
                                   identf[:, :], start=True, stop=True), "te", 1)
                emit("tensor", t_w)
                te_w = S.val("te")

                def v_w2(v, ot=ot, te_w=te_w):
                    S.wait(v, "te", te_w)
                    S.inc(v.tensor_copy(stage32[0:1, 2048 + ot * 128:2048 + (ot + 1) * 128],
                                        PZ[0:1, 128:256]), "ve", 1)
                emit("vector", v_w2)
            ve_wsr = S.val("ve")
            for oc in range(4):
                def t_b(t, oc=oc, vp=ve_wsr):
                    S.wait(t, "ve", vp)
                    S.inc(t.matmul(PG[oc], onesf[0:1, :],
                                   stage32[0:1, 2048 + oc * 512:2048 + (oc + 1) * 512],
                                   start=True, stop=True), "te", 1)
                emit("tensor", t_b)
                te_b = S.val("te")

                def v_b(v, oc=oc, te_b=te_b):
                    S.wait(v, "te", te_b)
                    S.inc(v.tensor_copy(aux32[:, oc * 512:(oc + 1) * 512],
                                        PG[oc]), "ve", 1)
                emit("vector", v_b)
            return S.val("ve")

        def emit_ffn(first_ve, wqt_ready, final_out):
            xq_tr = []
            mm_te = []
            xq_ve = {}
            xq_chain = []
            ep_ve = [first_ve]
            out_st = []
            for tt in range(NT):
                half = tt % 2
                src = bigA[:, tt * 2048:(tt + 1) * 2048]
                pm0 = None
                if tt < 2:
                    pm0 = [("tr", p, S.dc["tr"][p]) for p in (0, 1)
                           if S.dc["tr"][p] > 0]
                se_ln = emit_ln_tile(src, xnt[:, half * 2048:(half + 1) * 2048],
                                     wnat[:, :], pre_mark=pm0,
                                     pre_ve=(ep_ve[0] if tt < 2 else None),
                                     dst_free=(xq_tr[tt - 2] if tt >= 2 else None),
                                     dst_free_ve=(xq_ve[tt - 2] if tt >= 2
                                                  else None))

                def v_qx(v, tt=tt, half=half, se_ln=se_ln):
                    S.wait(v, "se", se_ln)
                    r = S.inc(v.reduce_max(sl[:, 9:10],
                                           xnt[:, half * 2048:(half + 1) * 2048],
                                           axis=AXX, apply_absolute_value=True),
                              "ve", 1)
                    S.wait(v, "ve", r)
                    r = S.inc(v.tensor_scalar(sl[:, 10:11], sl[:, 9:10],
                                              1.0 / 127.0, 1e-12, ALU.mult, ALU.max),
                              "ve", 1)
                    S.wait(v, "ve", r)
                    r = S.inc(v.reciprocal(sl[:, 11:12], sl[:, 10:11]), "ve", 1)
                    S.wait(v, "ve", r)
                    if mm_te:
                        S.wait(v, "te", mm_te[-1])
                    S.inc(v.tensor_copy(alc[:, 0:1], sl[:, 11:12]), "ve", 1)
                emit("vector", v_qx)
                ve_iq = S.val("ve")

                def t_al(t, ve_iq=ve_iq):
                    S.wait(t, "ve", ve_iq)
                    S.inc(t.matmul(PZ[0:1, 0:128], alc[:, 0:1], identb[:, :],
                                   start=True, stop=True), "te", 1)
                emit("tensor", t_al)
                te_al = S.val("te")

                def v_al(v, te_al=te_al):
                    S.wait(v, "te", te_al)
                    S.inc(v.tensor_copy(rowsA[0:1, :], PZ[0:1, 0:128]), "ve", 1)
                emit("vector", v_al)
                ve_al = S.val("ve")

                for ch in range(2):
                    xo = half * 2048 + ch * 1024

                    def s_k1(se, xo=xo, ve_iq=ve_iq, tt=tt, ch=ch,
                             pva=(xq_chain[-1] if xq_chain else 0)):
                        S.wait(se, "ve", ve_iq)
                        if pva:
                            S.wait(se, "ve", pva)
                        if tt > 0 and ch == 0:
                            S.dwait(se, xq_tr[-1])
                        S.inc(se.activation(k1t[:, :], xnt[:, xo:xo + 1024],
                                            AF.Identity, bias=cst[:, 1:2],
                                            scale=sl[:, 11:12]), "se", 1)
                    emit("scalar", s_k1)
                    se_k1 = S.val("se")

                    def v_r(v, xo=xo, se_k1=se_k1, ch=ch):
                        S.wait(v, "se", se_k1)
                        r = S.inc(v.scalar_tensor_tensor(
                            stage32[:, ch * 1024:(ch + 1) * 1024],
                            xnt[:, xo:xo + 1024], sl[:, 11:12], k1t[:, :],
                            ALU.mult, ALU.subtract), "ve", 1)
                        S.wait(v, "ve", r)
                        S.inc(v.scalar_tensor_tensor(
                            xnt[:, xo:xo + 1024],
                            stage32[:, ch * 1024:(ch + 1) * 1024], -127.5,
                            k1t[:, :], ALU.is_ge, ALU.add), "ve", 1)
                    emit("vector", v_r)
                    xq_chain.append(S.val("ve"))
                ve_xq = S.val("ve")
                xq_ve[tt] = ve_xq

                def s_xtr(sy, tt=tt, half=half, ve_xq=ve_xq):
                    S.wait(sy, "ve", ve_xq)
                    if len(mm_te) >= 2:
                        S.wait(sy, "te", mm_te[-2])
                    xq_tr.append(S.dma(
                        sy, "tr", transpose_call=True,
                        out=b3(otld[:, half * 2048:(half + 1) * 2048], 128),
                        in_=xnt[:, half * 2048:(half + 1) * 2048]))
                emit("sync", s_xtr)

                def t_mm(t, tt=tt, half=half, ve_al=ve_al):
                    S.dwait(t, xq_tr[-1])
                    S.wait(t, "ve", ve_al)
                    if tt == 0:
                        for mk in wqt_ready:
                            S.dwait(t, mk)
                    S.wait(t, "ve", ep_ve[0])
                    last = None
                    for dc in range(ND):
                        for oc in range(4):
                            last = t.matmul(
                                PG[oc],
                                b3(otld[:, half * 2048:(half + 1) * 2048], 128)[:, dc, :],
                                bigB[:, dc * 2048 + oc * 512:dc * 2048 + (oc + 1) * 512],
                                start=(dc == 0), stop=False)
                    for oc in range(4):
                        last = t.matmul(PG[oc], rowsA[0:3, :],
                                        rowsB[0:3, oc * 512:(oc + 1) * 512],
                                        start=False, stop=True)
                    S.inc(last, "te", 1)
                emit("tensor", t_mm)
                mm_te.append(S.val("te"))

                def v_ep(v, tt=tt, half=half, m=mm_te[-1]):
                    S.wait(v, "te", m)
                    if final_out and tt >= 1:
                        S.dwait(v, out_st[tt - 1])
                    last = None
                    for oc in range(4):
                        r = S.inc(v.scalar_tensor_tensor(
                            stage32[:, 2048 + oc * 512:2048 + (oc + 1) * 512],
                            PG[oc], sl[:, 10:11], aux32[:, oc * 512:(oc + 1) * 512],
                            ALU.mult, ALU.mult), "ve", 1)
                        S.wait(v, "ve", r)
                        last = v.tensor_tensor(
                            (stage32[:, 2048 + oc * 512:2048 + (oc + 1) * 512]
                             if final_out else
                             bigA[:, tt * 2048 + oc * 512:tt * 2048 + (oc + 1) * 512]),
                            stage32[:, 2048 + oc * 512:2048 + (oc + 1) * 512],
                            bigA[:, tt * 2048 + oc * 512:tt * 2048 + (oc + 1) * 512],
                            ALU.add)
                    S.inc(last, "ve", 1)
                emit("vector", v_ep)
                ep_ve[0] = S.val("ve")
                ve_y = S.val("ve")

                if final_out:
                    def s_o(sy, tt=tt, ve_y=ve_y):
                        S.wait(sy, "ve", ve_y)
                        out_st.append(S.dma(sy, "st",
                                            out=out_t[tt * 128:(tt + 1) * 128, :],
                                            in_=stage32[:, 2048:4096]))
                    emit("sync", s_o)
                else:
                    def s_o(sy, tt=tt, ve_y=ve_y):
                        S.wait(sy, "ve", ve_y)
                        out_st.append(S.dma(sy, "st",
                                            out=scr["x2sp"][tt, :, :],
                                            in_=bigA[:, tt * 2048:(tt + 1) * 2048]))
                    emit("sync", s_o)
            return S.val("ve"), out_st, mm_te[-1]

        if upto >= 2:
            ve_cols = emit_wprep_cols()
            ve_rows = emit_rows_and_wsb(ve_cols)
            wst = wprep_state["wst"]

            wq_marks = []

            def s_wq(sy):
                S.dwait(sy, wst[-1])
                S.dwait(sy, wst[-2])
                S.wait(sy, "te", oproj_a_te)
                wq_marks.append(S.dma(sy, "sld", out=bigB[:, :], in_=scr["WTf"]))
            emit("sync", s_wq)

            ve_x2, x2_st, ffn1_te = emit_ffn(max(ve_x1, ve_rows), wq_marks, False)

        if upto >= 3:
            def g_tr(g):
                S.wait(g, "ve", ve_x2)
                S.inc(g.iota(stage32[0:1, 0:2048], pattern=[[1, 2048]], base=0,
                             channel_multiplier=0,
                             allow_small_or_imprecise_dtypes=True), "gp", 1)
            emit("gpsimd", g_tr)
            gp_tr = S.val("gp")

            def v_dp(v):
                S.wait(v, "ve", ve_x2)
                r = S.inc(v.tensor_scalar(k1t[:, 0:1], iotp[:, 0:1], 0.015625,
                                          None, ALU.mult), "ve", 1)
                S.wait(v, "ve", r)
                r = S.inc(v.scalar_tensor_tensor(sl[:, 9:10], k1t[:, 0:1], -64.0,
                                                 iotp[:, 0:1], ALU.mult, ALU.add),
                          "ve", 1)
                S.wait(v, "ve", r)
                r = S.inc(v.tensor_scalar(sl[:, 10:11], sl[:, 9:10], 0.0, None,
                                          ALU.is_lt), "ve", 1)
                S.wait(v, "ve", r)
                S.inc(v.scalar_tensor_tensor(sl[:, 11:12], sl[:, 10:11], 64.0,
                                             sl[:, 9:10], ALU.mult, ALU.add),
                      "ve", 1)
            emit("vector", v_dp)
            ve_dp = S.val("ve")

            def s_inv(se):
                S.wait(se, "ve", ve_dp)
                S.inc(se.activation(sl[:, 12:13], sl[:, 11:12], AF.Exp,
                                    bias=cst[:, 4:5], scale=EXPSC), "se", 1)
            emit("scalar", s_inv)
            se_inv = S.val("se")

            def v_invc(v, se_inv=se_inv, tp=S.val("te")):
                S.wait(v, "se", se_inv)
                S.wait(v, "te", tp)
                S.inc(v.tensor_copy(sl[:, 13:14], sl[:, 12:13]), "ve", 1)
            emit("vector", v_invc)
            ve_invc = S.val("ve")

            def t_inv(t, ve_invc=ve_invc):
                S.wait(t, "ve", ve_invc)
                S.inc(t.matmul(PZ[0:1, 128:256], sl[:, 13:14], identf[:, :],
                               start=True, stop=True), "te", 1)
            emit("tensor", t_inv)
            te_inv = S.val("te")

            def v_invr(v, te_inv=te_inv):
                S.wait(v, "te", te_inv)
                S.inc(v.tensor_copy(stage32[0:1, 2048:2176], PZ[0:1, 128:256]),
                      "ve", 1)
            emit("vector", v_invr)
            ve_invr = S.val("ve")

            th_ve = []
            for tc in range(4):
                def t_th(t, tc=tc, ve_invr=ve_invr, gp_tr=gp_tr):
                    S.wait(t, "ve", ve_invr)
                    S.wait(t, "gp", gp_tr)
                    if tc >= 1:
                        S.wait(t, "ve", th_ve[-1])
                    S.inc(t.matmul(PG[tc], stage32[0:1, 2048:2176],
                                   stage32[0:1, tc * 512:(tc + 1) * 512],
                                   start=True, stop=True), "te", 1)
                emit("tensor", t_th)
                te_th = S.val("te")

                def v_th(v, tc=tc, te_th=te_th):
                    S.wait(v, "te", te_th)
                    S.inc(v.tensor_copy(aux32[:, tc * 512:(tc + 1) * 512],
                                        PG[tc]), "ve", 1)
                emit("vector", v_th)
                th_ve.append(S.val("ve"))
            sin_done = []
            for part in range(2):
                for ch in range(2):
                    o = ch * 1024

                    def v_u(v, part=part, ch=ch, o=o, pv=th_ve[-1],
                            sins=list(sin_done)):
                        S.wait(v, "ve", pv)
                        for m in sins:
                            S.wait(v, "se", m)
                        if part == 0:
                            r = S.inc(v.tensor_scalar(
                                stage32[:, 2048 + o:2048 + o + 1024],
                                aux32[:, o:o + 1024], 0.0, None, ALU.add), "ve", 1)
                        else:
                            r = S.inc(v.tensor_scalar(
                                stage32[:, 2048 + o:2048 + o + 1024],
                                aux32[:, o:o + 1024], PI2_HI, PI2_LO,
                                ALU.add, ALU.add), "ve", 1)
                        S.wait(v, "ve", r)
                        r = S.inc(v.tensor_scalar(
                            k1t[:, :], stage32[:, 2048 + o:2048 + o + 1024],
                            INV_2PI, None, ALU.mult), "ve", 1)
                        S.wait(v, "ve", r)
                        r = S.inc(v.scalar_tensor_tensor(
                            stage32[:, 2048 + o:2048 + o + 1024], k1t[:, :],
                            -TWO_PI_HI, stage32[:, 2048 + o:2048 + o + 1024],
                            ALU.mult, ALU.add), "ve", 1)
                        S.wait(v, "ve", r)
                        r = S.inc(v.scalar_tensor_tensor(
                            stage32[:, 2048 + o:2048 + o + 1024], k1t[:, :],
                            -TWO_PI_LO, stage32[:, 2048 + o:2048 + o + 1024],
                            ALU.mult, ALU.add), "ve", 1)
                        S.wait(v, "ve", r)
                        r = S.inc(v.tensor_scalar(
                            k1t[:, :], stage32[:, 2048 + o:2048 + o + 1024],
                            PI_F32, None, ALU.is_ge), "ve", 1)
                        S.wait(v, "ve", r)
                        r = S.inc(v.scalar_tensor_tensor(
                            stage32[:, 2048 + o:2048 + o + 1024], k1t[:, :],
                            -TWO_PI_HI, stage32[:, 2048 + o:2048 + o + 1024],
                            ALU.mult, ALU.add), "ve", 1)
                        S.wait(v, "ve", r)
                        r = S.inc(v.scalar_tensor_tensor(
                            stage32[:, 2048 + o:2048 + o + 1024], k1t[:, :],
                            -TWO_PI_LO, stage32[:, 2048 + o:2048 + o + 1024],
                            ALU.mult, ALU.add), "ve", 1)
                        S.wait(v, "ve", r)
                        r = S.inc(v.tensor_scalar(
                            k1t[:, :], stage32[:, 2048 + o:2048 + o + 1024],
                            -PI_F32, None, ALU.is_lt), "ve", 1)
                        S.wait(v, "ve", r)
                        r = S.inc(v.scalar_tensor_tensor(
                            stage32[:, 2048 + o:2048 + o + 1024], k1t[:, :],
                            TWO_PI_HI, stage32[:, 2048 + o:2048 + o + 1024],
                            ALU.mult, ALU.add), "ve", 1)
                        S.wait(v, "ve", r)
                        r = S.inc(v.scalar_tensor_tensor(
                            stage32[:, 2048 + o:2048 + o + 1024], k1t[:, :],
                            TWO_PI_LO, stage32[:, 2048 + o:2048 + o + 1024],
                            ALU.mult, ALU.add), "ve", 1)
                        S.wait(v, "ve", r)
                        S.inc(v.tensor_scalar(
                            stage32[:, 2048 + o:2048 + o + 1024],
                            stage32[:, 2048 + o:2048 + o + 1024],
                            -PI_CLAMP, PI_CLAMP, ALU.max, ALU.min), "ve", 1)
                    emit("vector", v_u)
                    ve_u = S.val("ve")

                    def s_sin(se, part=part, ch=ch, o=o, ve_u=ve_u,
                              first=(part == 0 and ch == 0)):
                        S.wait(se, "ve", ve_u)
                        if first:
                            preload_tbl(se, _TRIG)
                        S.inc(se.activation(
                            stage32[:, o:o + 1024] if part == 0 else
                            stage32[:, 2048 + o:2048 + o + 1024],
                            stage32[:, 2048 + o:2048 + o + 1024], AF.Sin,
                            bias=cst[:, 4:5], scale=1.0), "se", 1)
                    emit("scalar", s_sin)
                    sin_done.append(S.val("se"))
            def s_rest(se):
                for m in sin_done:
                    S.wait(se, "se", m)
                preload_tbl(se, _NLE)
            emit("scalar", s_rest)

            def v_ct(v, sd=sin_done[-1]):
                for m in sin_done:
                    S.wait(v, "se", m)
                S.inc(v.tensor_tensor(aux32[:, :], stage32[:, 0:2048],
                                      stage32[:, 2048:4096], ALU.add), "ve", 1)
            emit("vector", v_ct)
            ve_ct = S.val("ve")

            tr_all_b = emit_ln_to_xnt("x2sp", bigA, wnat[:, :], (otld, 2048),
                                      src_ready=x2_st, dst_free_marks=x2_st[-2:])

            kv_marks = []

            def s_kv(sy):
                S.wait(sy, "te", ffn1_te)
                S.dwait(sy, wprep_st["bWk"])
                S.dwait(sy, wprep_st["bWv"])
                kv_marks.append(S.dma(sy, "sld", out=bigB[:, 0:8192],
                                      in_=scr["WTkb"]))
                kv_marks.append(S.dma(sy, "sld", out=bigB[:, 8192:16384],
                                      in_=scr["WTvb"]))
            emit("sync", s_kv)

            brow_gld_b = load_brow("bbv", 512)

            kt_st = []
            state_b = {"sparity": 0, "oparity": 0,
                       "trip_last_exp": [0, 0], "ob_last_evac": [0, 0],
                       "et_last_read": [0, 0], "proj_evac_prev": 0}
            for g in range(KVH):
                def t_kg(t, g=g, pe=S.val("se"), pv=S.val("ve")):
                    if g == 0:
                        for mk in tr_all_b:
                            S.dwait(t, mk)
                        S.dwait(t, kv_marks[0])
                        S.wait(t, "ve", ve_x2)
                    else:
                        S.wait(t, "ve", pv)
                    last = None
                    for dc in range(ND):
                        for tc2 in range(4):
                            last = t.matmul(
                                PG[tc2],
                                bigB[:, dc * 512 + g * 128:dc * 512 + (g + 1) * 128],
                                bigA[:, dc * 2048 + tc2 * 512:dc * 2048 + (tc2 + 1) * 512],
                                start=(dc == 0), stop=(dc == ND - 1))
                    S.inc(last, "te", 1)
                emit("tensor", t_kg)
                te_kg = S.val("te")

                def s_ke(se, g=g, te_kg=te_kg):
                    S.wait(se, "te", te_kg)
                    if kt_st:
                        S.dwait(se, kt_st[-1])
                    last = None
                    for tc2 in range(4):
                        last = se.activation(KTh[:, tc2 * 512:(tc2 + 1) * 512],
                                             PG[tc2], AF.Identity,
                                             bias=bkbc[:, g:g + 1], scale=1.0)
                    S.inc(last, "se", 1)
                emit("scalar", s_ke)
                se_ke = S.val("se")

                def v_kr(v, g=g, se_ke=se_ke):
                    S.wait(v, "se", se_ke)
                    S.wait(v, "ve", ve_ct)
                    S.inc(v.tensor_tensor(KTh[:, :], KTh[:, :], aux32[:, :],
                                          ALU.mult), "ve", 1)
                emit("vector", v_kr)
                ve_kr = S.val("ve")

                def s_ks(sy, g=g, ve_kr=ve_kr):
                    S.wait(sy, "ve", ve_kr)
                    kt_st.append(S.dma(sy, "st", out=scr["KTbs"][g, :, :],
                                       in_=KTh[:, :]))
                emit("sync", s_ks)
            kt_last = S.val("ve")

            vsp_b = emit_v_all(bigA, bigB[:, 8192:16384], KVH, scr["Vspb"],
                               tr_all_b, kv_marks[1], kt_last, brow_gld_b,
                               None, None)

            oht_st = []
            for h in range(H):
                g = h // 4
                hm2 = []

                def s_wsb(sy, h=h, g=g, pe=state_b["proj_evac_prev"],
                          obm=max(state_b["ob_last_evac"])):
                    if h == 0:
                        S.dwait(sy, wprep_st["bWq"])
                        for mk in vsp_b.values():
                            S.dwait(sy, mk)
                    else:
                        S.wait(sy, "ve", max(pe, obm))
                    hm2.append(S.dma(
                        sy, "sld", out=b3(WS[0], 128),
                        in_=scr["WTqb"].rearrange("p (dc o) -> p dc o", o=D)[
                            :, :, h * 128:(h + 1) * 128]))
                    if h % 4 == 0:
                        S.dwait(sy, kt_st[g])
                        hm2.append(S.dma(sy, "sld", out=KTh[:, :],
                                         in_=scr["KTbs"][g, :, :]))
                        hm2.append(S.dma(
                            sy, "sld", out=b3(Vh, 132),
                            in_=scr["Vspb"].rearrange(
                                "t p (hh w) -> p t hh w", w=132)[:, :, g, :]))
                emit("sync", s_wsb)
                vh_m = hm2[2] if h % 4 == 0 else last_vh[0]
                kt_m = hm2[1] if h % 4 == 0 else None
                if h % 4 == 0:
                    last_vh = [hm2[2]]

                emit_projT(bigA, WS[0], QTh, bqbc[:, h:h + 1], None, None,
                           [hm2[0]], PG, state_b, rope=True, z_ve=ve_ct)
                se_q = state_b["rope_ve"]
                ve_rope = 0

                ve_head = emit_attention_head(QTh, KTh, Vh, ET, True, OSg,
                                              se_q, ve_rope, vh_m, state_b,
                                              kt_mark=kt_m)

                otr2 = []

                def s_ot2(sy, ve_head=ve_head):
                    S.wait(sy, "ve", ve_head)
                    if oht_st:
                        S.dwait(sy, oht_st[-1])
                    otr2.append(S.dma(sy, "tr", transpose_call=True,
                                      out=b3(OTt, 128), in_=OSg[:, :]))
                emit("sync", s_ot2)

                def q_ot2(sq, h=h):
                    S.dwait(sq, otr2[0])
                    oht_st.append(S.dma(sq, "st", out=scr["OTsc"][h, :, :],
                                        in_=OTt[:, :]))
                emit("scalar", q_ot2)

            brow_gld2 = load_brow("bbo")
            ve_x3, oproj_b_te = emit_oproj("WTob", wprep_st["bWo"], oht_st[-1],
                                           brow_gld2, False, "x2sp", xnt, bigA,
                                           False)

        if upto >= 4:
            ve_rows2 = emit_rows_and_wsb(ve_x3)
            wq2_marks = []

            def s_wq2(sy):
                S.wait(sy, "te", oproj_b_te)
                wq2_marks.append(S.dma(sy, "sld", out=bigB[:, :], in_=scr["WTf"]))
            emit("sync", s_wq2)

            ve_x4, out_st4, _ = emit_ffn(max(ve_x3, ve_rows2), wq2_marks, True)

        def emit_debug_out(src_big):
            dbg_st = []
            for tt in range(NT):
                half = tt % 2

                def s_cp(se, tt=tt, half=half):
                    S.wait(se, "ve")
                    if tt >= 2:
                        S.dwait(se, dbg_st[tt - 2])
                    S.inc(se.activation(
                        stage32[:, half * 2048:(half + 1) * 2048],
                        src_big[:, tt * 2048:(tt + 1) * 2048],
                        AF.Identity, bias=cst[:, 4:5], scale=1.0), "se", 1)
                emit("scalar", s_cp)
                se_c = S.val("se")

                def s_st(sy, tt=tt, half=half, se_c=se_c):
                    S.wait(sy, "se", se_c)
                    dbg_st.append(S.dma(
                        sy, "st", out=out_t[tt * 128:(tt + 1) * 128, :],
                        in_=stage32[:, half * 2048:(half + 1) * 2048]))
                emit("sync", s_st)

        if upto in (1, 2, 3):
            emit_debug_out(bigA)

        def fin(sy):
            for nm in ["st", "gld", "sld", "tr", "glx", "grs"]:
                S.dwait_all(sy, nm)
            S.wait(sy, "ve")
            S.wait(sy, "se")
            S.wait(sy, "te")
        emit("sync", fin)

    return nc


def make_inputs_per_core(inputs, core):
    x = np.ascontiguousarray(inputs["L0_attn_a_x"][core])
    dm = np.tril(np.ones((128, 128), dtype=np.float32))
    m = {
        "x": x, "dmaski": dm,
        "aWq": inputs["L0_attn_a_Wq"], "aWk": inputs["L0_attn_a_Wk"],
        "aWv": inputs["L0_attn_a_Wv"], "aWo": inputs["L0_attn_a_Wo"],
        "abq": inputs["L0_attn_a_bq"], "abk": inputs["L0_attn_a_bk"],
        "abv": inputs["L0_attn_a_bv"], "abo": inputs["L0_attn_a_bo"],
        "aBq": inputs["L0_attn_a_Bq"], "aBk": inputs["L0_attn_a_Bk"],
        "aBv": inputs["L0_attn_a_Bv"], "aBo": inputs["L0_attn_a_Bo"],
        "aAq": inputs["L0_attn_a_Aq"], "aAk": inputs["L0_attn_a_Ak"],
        "aAv": inputs["L0_attn_a_Av"], "aAo": inputs["L0_attn_a_Ao"],
        "bWq": inputs["L0_attn_b_Wq"], "bWk": inputs["L0_attn_b_Wk"],
        "bWv": inputs["L0_attn_b_Wv"], "bWo": inputs["L0_attn_b_Wo"],
        "bbq": inputs["L0_attn_b_bq"], "bbk": inputs["L0_attn_b_bk"],
        "bbv": inputs["L0_attn_b_bv"], "bbo": inputs["L0_attn_b_bo"],
        "fW": inputs["L0_ffn_weight"], "fb": inputs["L0_ffn_bias"],
    }
    return {k: np.ascontiguousarray(v) for k, v in m.items()}


def kernel(**inputs):
    nc = build_nc(upto=4)
    in_maps = [make_inputs_per_core(inputs, c) for c in range(8)]
    res = run_bass_kernel_spmd(nc, in_maps, core_ids=list(range(8)))
    return np.stack([res.results[c]["out"] for c in range(8)], axis=0)


# version 04
# speedup vs baseline: 1.1394x; 1.0125x over previous
import numpy as np
import ml_dtypes
import concourse.bass as bass
import concourse.mybir as mybir
from concourse.bass_utils import run_bass_kernel_spmd
from contextlib import ExitStack

F32 = mybir.dt.float32
BF16 = mybir.dt.bfloat16
I32 = mybir.dt.int32
AF = mybir.ActivationFunctionType
ALU = mybir.AluOpType
AXX = mybir.AxisListType.X

L, D, NT, ND = 2048, 2048, 16, 16
H, HD, KVH = 16, 128, 4
ISQ = float(128.0 ** -0.5)
ESH = -12.0
EPS = 1e-5
INV_D = 1.0 / 2048.0
TWO_PI_HI = float(np.float32(2 * np.pi))
TWO_PI_LO = float(np.float32(2 * np.pi - np.float64(np.float32(2 * np.pi))))
PI2_HI = float(np.float32(np.pi / 2))
PI2_LO = float(np.float32(np.pi / 2 - np.float64(np.float32(np.pi / 2))))
PI_CLAMP = 3.1415925
PI_F32 = float(np.float32(np.pi))
INV_2PI = float(np.float32(1.0 / (2 * np.pi)))
EXPSC = float(np.float32(-2.0 * np.log(10000.0) / 128.0))


class Sems:

    def __init__(self):
        self.h, self.c = {}, {}
        self.dh, self.dc, self.di = {}, {}, {}

    def add(self, name, handle):
        self.h[name], self.c[name] = handle, 0

    def add_dma(self, name, h0, h1):
        self.dh[name] = [h0, h1]
        self.dc[name] = [0, 0]
        self.di[name] = 0

    def inc(self, inst, name, n=1):
        self.c[name] += n
        inst.then_inc(self.h[name], n)
        return self.c[name]

    def wait(self, eng, name, value=None):
        v = self.c[name] if value is None else value
        if v > 0:
            eng.wait_ge(self.h[name], v)

    def val(self, name):
        return self.c[name]

    def dma(self, eng, name, **kw):
        p = self.di[name] % 2
        self.di[name] += 1
        if self.dc[name][p] > 0:
            eng.wait_ge(self.dh[name][p], self.dc[name][p])
        tr = kw.pop("transpose_call", False)
        if tr:
            inst = eng.dma_start_transpose(kw["out"], kw["in_"])
        else:
            inst = eng.dma_start(**kw)
        self.dc[name][p] += 16
        inst.then_inc(self.dh[name][p], 16)
        return (name, p, self.dc[name][p])

    def dwait(self, eng, mark):
        name, p, v = mark
        eng.wait_ge(self.dh[name][p], v)

    def dwait_all(self, eng, name):
        for p in (0, 1):
            if self.dc[name][p] > 0:
                eng.wait_ge(self.dh[name][p], self.dc[name][p])


def build_nc(upto=4):
    nc = bass.Bass("TRN2", target_bir_lowering=False)

    x_in = nc.dram_tensor("x", [L, D], F32, kind="ExternalInput").ap()
    out_t = nc.dram_tensor("out", [L, D], F32, kind="ExternalOutput").ap()

    wts = {}
    for nm, shp in [
        ("aWq", [D, D]), ("aWk", [D, D]), ("aWv", [D, D]), ("aWo", [D, D]),
        ("abq", [D]), ("abk", [D]), ("abv", [D]), ("abo", [D]),
        ("aBq", [16, D]), ("aBk", [16, D]), ("aBv", [16, D]), ("aBo", [16, D]),
        ("aAq", [D, 16]), ("aAk", [D, 16]), ("aAv", [D, 16]), ("aAo", [D, 16]),
        ("bWq", [D, D]), ("bWk", [512, D]), ("bWv", [512, D]), ("bWo", [D, D]),
        ("bbq", [D]), ("bbk", [512]), ("bbv", [512]), ("bbo", [D]),
        ("fW", [D, D]), ("fb", [D]),
    ]:
        wts[nm] = nc.dram_tensor(nm, shp, F32, kind="ExternalInput").ap()
    dmask_in = nc.dram_tensor("dmaski", [128, 128], F32, kind="ExternalInput").ap()

    scr = {}
    for nm, shp, dt in [
        ("WTqa", [128, ND * D], BF16), ("WTka", [128, ND * D], BF16),
        ("WTva", [128, ND * D], BF16), ("WToa", [128, ND * D], BF16),
        ("WTqb", [128, ND * D], BF16), ("WTob", [128, ND * D], BF16),
        ("WTkb", [128, ND * 512], BF16), ("WTvb", [128, ND * 512], BF16),
        ("WTf", [128, ND * D], BF16),
        ("ATsc", [16, 4 * D], BF16),
        ("Vspa", [NT, 128, H * 132], BF16),
        ("Vspb", [NT, 128, KVH * 132], BF16),
        ("KTbs", [KVH, 128, L], BF16),
        ("OTsc", [H, 128, L], BF16),
        ("x2sp", [NT, 128, D], BF16),
    ]:
        scr[nm] = nc.dram_tensor(nm, shp, dt, kind="Internal").ap()

    ctx = ExitStack()
    with ctx:
        def sbt(name, shape, dtype):
            return ctx.enter_context(nc.sbuf_tensor(name, shape, dtype))

        bigA = sbt("bigA", [128, 32768], BF16)
        bigB = sbt("bigB", [128, 32768], BF16)
        stage32 = sbt("stage32", [128, 4096], F32)
        xnt = sbt("xnt", [128, 4096], BF16)
        vstg = sbt("vstg", [128, 1056], BF16)
        otld = sbt("otld", [128, 4096], BF16)
        k1t = sbt("k1t", [128, 1024], I32)
        aux32 = sbt("aux32", [128, 2048], F32)
        zq = sbt("zq", [16, 2048], BF16)
        zk = sbt("zk", [16, 2048], BF16)
        zv = sbt("zv", [16, 2048], BF16)
        zob = sbt("zob", [16, 256], BF16)
        BTq = sbt("BTq", [128, ND * 16], BF16)
        BTk = sbt("BTk", [128, ND * 16], BF16)
        BTv = sbt("BTv", [128, ND * 16], BF16)
        BTo = sbt("BTo", [128, ND * 16], BF16)
        ATs = sbt("ATs", [16, 2048], BF16)
        rowsB = sbt("rowsB", [3, 2048], BF16)
        rowsA = sbt("rowsA", [3, 128], BF16)
        bcolq = sbt("bcolq", [128, 16], F32)
        bcolk = sbt("bcolk", [128, 16], F32)
        bqbc = sbt("bqbc", [128, 16], F32)
        bkbc = sbt("bkbc", [128, 4], F32)
        fbcol = sbt("fbcol", [128, 16], F32)
        wscol = sbt("wscol", [128, 16], F32)
        wsumc = sbt("wsumc", [128, 16], F32)
        betac = sbt("betac", [128, 16], F32)
        c2hc = sbt("c2hc", [128, 16], BF16)
        c2lc = sbt("c2lc", [128, 16], BF16)
        alc = sbt("alc", [128, 4], BF16)
        bstg = sbt("bstg", [16, 128], F32)
        sl = sbt("sl", [128, 16], F32)
        cst = sbt("cst", [128, 8], F32)
        dmask = sbt("dmask", [128, 128], BF16)
        identb = sbt("identb", [128, 128], BF16)
        identf = sbt("identf", [128, 128], F32)
        onesf = sbt("onesf", [1, 128], F32)
        onesr = sbt("onesr", [1, 128], BF16)
        iotp = sbt("iotp", [128, 1], F32)
        invd = sbt("invd", [128, 4], F32)
        wnat = sbt("wnat", [128, 2048], BF16)
        wtt = sbt("wtt", [128, 2048], BF16)
        Bnat = otld[0:16, 0:2048]
        brow = rowsB[0:1, :]

        ps_a = ctx.enter_context(nc.psum_tensor("ps_a", [128, 1536], F32))
        ps_b = ctx.enter_context(nc.psum_tensor("ps_b", [128, 1536], F32))
        ps_oa = ctx.enter_context(nc.psum_tensor("ps_oa", [128, 512], F32))
        ps_ob = ctx.enter_context(nc.psum_tensor("ps_ob", [128, 512], F32))
        PG = [ps_a[:, 0:512], ps_a[:, 512:1024], ps_a[:, 1024:1536], ps_b[:, 0:512]]
        KGB = [ps_b[:, 512:1024], ps_b[:, 1024:1536], ps_oa[:, 0:512], ps_ob[:, 0:512]]
        PZ = ps_b[:, 512:1024]
        STRIPLE = [ps_a, ps_b]

        S = Sems()
        for snm in ["te", "ve", "se", "gp"]:
            S.add(snm, ctx.enter_context(nc.semaphore(snm)))
        for snm in ["gld", "sld", "tr", "st", "glx", "grs"]:
            S.add_dma(snm, ctx.enter_context(nc.semaphore(snm + "0")),
                      ctx.enter_context(nc.semaphore(snm + "1")))
        block = ctx.enter_context(nc.Block())

        ET = [bigB[:, 0:8192], bigB[:, 8192:16384]]
        QTh = bigB[:, 16384:18432]
        KTh = bigB[:, 18432:20480]
        Vh = bigB[:, 20480:22592]
        OSg = bigB[:, 22592:24640]
        OTt = bigB[:, 24640:26688]
        WS = [bigB[:, 26688:28736], bigB[:, 28736:30784]]

        def b3(ap, w):
            return ap.rearrange("p (n w) -> p n w", w=w)

        def emit(engname, fn):
            getattr(block, engname)(fn)

        from concourse.hw_specs import get_activation_tables
        _tabs = list(get_activation_tables(nc.m.arch).keys())
        _NLE = _tabs.index("natural_log_exp_and_others")
        _TRIG = _tabs.index("trig_and_small")

        def preload_tbl(se, idx):
            se.add_instruction(mybir.InstLoadActFuncSet(
                name=nc.get_next_instruction_name(),
                act_func_set_id=idx, ins=[], outs=[]))

        def p0_se(se):
            preload_tbl(se, _NLE)
        emit("scalar", p0_se)

        p0_marks = []

        def p0a_g(g):
            g.iota(iotp[:, :], pattern=[[0, 1]], base=0, channel_multiplier=1,
                   allow_small_or_imprecise_dtypes=True)
            S.inc(g.iota(stage32[:, 0:128], pattern=[[1, 128]], base=0,
                         channel_multiplier=0,
                         allow_small_or_imprecise_dtypes=True), "gp", 1)
            p0_marks.append(S.dma(g, "gld", out=dmask[:, :], in_=dmask_in))
        emit("gpsimd", p0a_g)

        def p0a_v(v):
            v.memset(cst[:, 0:1], ESH)
            v.memset(cst[:, 1:2], 128.0)
            v.memset(cst[:, 2:3], EPS)
            v.memset(cst[:, 3:4], -127.5)
            v.memset(cst[:, 4:5], 0.0)
            v.memset(onesr[0:1, :], 1.0)
            v.memset(rowsA[0:3, :], 1.0)
            v.memset(vstg[:, :], 1.0)
            for mk in p0_marks:
                S.dwait(v, mk)
            S.wait(v, "gp", 1)
            v.memset(onesf[0:1, :], 1.0)
            v.tensor_scalar(identf[:, :], stage32[:, 0:128], iotp[:, 0:1],
                            None, ALU.is_equal)
            S.inc(v.tensor_scalar(identb[:, :], stage32[:, 0:128], iotp[:, 0:1],
                                  None, ALU.is_equal), "ve", 1)
        emit("vector", p0a_v)
        ve_consts = S.val("ve")

        for dst, src, w in [(bcolq, wts["abq"], 16), (bcolk, wts["abk"], 16),
                            (bqbc, wts["bbq"], 16), (bkbc, wts["bbk"], 4),
                            (fbcol, wts["fb"], 16)]:
            bm = []

            def g_bc(g, src=src, w=w, pv=S.val("ve")):
                if pv > ve_consts:
                    S.wait(g, "ve", pv)
                bm.append(S.dma(g, "gld", out=bstg[0:w, :],
                                in_=src.rearrange("(a b) -> a b", b=128)))
            emit("gpsimd", g_bc)

            def t_bc(t, w=w):
                S.dwait(t, bm[0])
                S.wait(t, "ve", ve_consts)
                S.inc(t.matmul(PZ[:, 0:16][:, 0:w] if False else PZ[:, 0:w],
                               bstg[0:w, :], identf[0:w, 0:w],
                               start=True, stop=True), "te", 1)
            emit("tensor", t_bc)
            te_bc = S.val("te")

            def v_bc(v, dst=dst, w=w, te_bc=te_bc):
                S.wait(v, "te", te_bc)
                S.inc(v.tensor_copy(dst[:, 0:w], PZ[:, 0:w]), "ve", 1)
            emit("vector", v_bc)
        ve_bcols = S.val("ve")

        at_store_marks = []
        for i, nm in enumerate(["aAq", "aAk", "aAv", "aAo"]):
            ga_mark = []

            def g_a(g, i=i, nm=nm):
                if i > 0:
                    S.dwait(g, at_store_marks[-1])
                ga_mark.append(S.dma(
                    g, "gld",
                    out=b3(wnat[:, 0:256], 16),
                    in_=wts[nm].rearrange("(ot p) r -> p ot r", p=128)))
            emit("gpsimd", g_a)

            for ot in range(16):
                def t_a(t, ot=ot, vp=S.val("ve")):
                    if ot == 0:
                        S.dwait(t, ga_mark[0])
                        S.wait(t, "ve", ve_bcols)
                    else:
                        S.wait(t, "ve", vp)
                    S.inc(t.matmul(PZ[0:16, 0:128],
                                   wnat[:, ot * 16:(ot + 1) * 16],
                                   identb[:, :], start=True, stop=True), "te", 1)
                emit("tensor", t_a)
                te_a = S.val("te")

                def v_a(v, ot=ot, te_a=te_a):
                    S.wait(v, "te", te_a)
                    S.inc(v.tensor_scalar(ATs[:, ot * 128:(ot + 1) * 128],
                                          PZ[0:16, 0:128], 0.5, None, ALU.mult),
                          "ve", 1)
                emit("vector", v_a)
            ve_a = S.val("ve")

            def s_at(sy, i=i, ve_a=ve_a):
                S.wait(sy, "ve", ve_a)
                at_store_marks.append(S.dma(
                    sy, "st", out=scr["ATsc"][:, i * D:(i + 1) * D],
                    in_=ATs[:, :]))
            emit("sync", s_at)
        at_done = list(at_store_marks)

        bld_marks, btr_marks = [], []
        for i, bt in enumerate([BTq, BTk, BTv, BTo]):
            def g_b(g, i=i):
                if i > 0:
                    S.dwait(g, btr_marks[-1])
                bld_marks.append(S.dma(
                    g, "gld", out=Bnat,
                    in_=wts[["aBq", "aBk", "aBv", "aBo"][i]]))
            emit("gpsimd", g_b)

            def s_b(sy, i=i, bt=bt):
                S.dwait(sy, bld_marks[-1])
                btr_marks.append(S.dma(sy, "tr", transpose_call=True,
                                       out=b3(bt, 16), in_=Bnat))
            emit("sync", s_b)
        bt_done = list(btr_marks)

        pw_ld, pw_tr, pw_st = [], [], []
        wprep_st = {}

        def prep_weight(wname, dramT, nrows):
            nt = nrows // 128
            for i in range(nt):
                def g_fn(g, i=i, wname=wname):
                    if pw_tr:
                        S.dwait(g, pw_tr[-1])
                    pw_ld.append(S.dma(g, "gld", out=wnat[:, :],
                                       in_=wts[wname][i * 128:(i + 1) * 128, :]))
                emit("gpsimd", g_fn)

                def s_fn(sq, i=i):
                    S.dwait(sq, pw_ld[-1])
                    if pw_st:
                        S.dwait(sq, pw_st[-1])
                    pw_tr.append(S.dma(sq, "tr", transpose_call=True,
                                       out=b3(wtt, 128), in_=wnat[:, :]))
                emit("scalar", s_fn)

                def q_fn(sq, i=i, dramT=dramT, nrows=nrows):
                    S.dwait(sq, pw_tr[-1])
                    pw_st.append(S.dma(
                        sq, "st",
                        out=dramT.rearrange("p (dc o) -> p dc o", o=nrows)[
                            :, :, i * 128:(i + 1) * 128],
                        in_=b3(wtt, 128)))
                emit("scalar", q_fn)
            wprep_st[wname] = pw_st[-1]

        prep_weight("aWv", scr["WTva"], D)
        prep_weight("aWq", scr["WTqa"], D)
        prep_weight("aWk", scr["WTka"], D)
        prep_weight("aWo", scr["WToa"], D)
        if upto >= 3:
            prep_weight("bWk", scr["WTkb"], 512)
            prep_weight("bWv", scr["WTvb"], 512)
            prep_weight("bWq", scr["WTqb"], D)
            prep_weight("bWo", scr["WTob"], D)

        def emit_ln_tile(src_ap, dst_ap, sq_ap, pre_mark=None, dst_free=None,
                         pre_ve=None, dst_free_ve=None):
            def v1(v):
                if pre_mark:
                    for mk in pre_mark:
                        S.dwait(v, mk)
                if pre_ve:
                    S.wait(v, "ve", pre_ve)
                rs = S.inc(v.reduce_sum(sl[:, 0:1], src_ap, axis=AXX), "ve", 1)
                S.wait(v, "ve", rs)
                rs = S.inc(v.scalar_tensor_tensor(sq_ap, src_ap, 1.0, src_ap,
                                                  ALU.mult, ALU.mult,
                                                  accum_out=sl[:, 1:2]), "ve", 1)
                S.wait(v, "ve", rs)
                rs = S.inc(v.tensor_scalar(sl[:, 2:3], sl[:, 0:1], INV_D, None,
                                           ALU.mult), "ve", 1)
                S.wait(v, "ve", rs)
                rs = S.inc(v.tensor_tensor(sl[:, 4:5], sl[:, 2:3], sl[:, 2:3],
                                           ALU.mult), "ve", 1)
                S.wait(v, "ve", rs)
                S.inc(v.scalar_tensor_tensor(sl[:, 5:6], sl[:, 1:2], INV_D,
                                             sl[:, 4:5], ALU.mult, ALU.subtract),
                      "ve", 1)
            emit("vector", v1)
            ve_var = S.val("ve")

            def s2(se):
                S.wait(se, "ve", ve_var)
                r = S.inc(se.activation(sl[:, 6:7], sl[:, 5:6], AF.Ln,
                                        bias=cst[:, 2:3], scale=1.0), "se", 1)
                S.wait(se, "se", r)
                S.inc(se.activation(sl[:, 7:8], sl[:, 6:7], AF.Exp,
                                    bias=cst[:, 4:5], scale=-0.5), "se", 1)
            emit("scalar", s2)
            se_sd = S.val("se")

            def v3(v):
                S.wait(v, "se", se_sd)
                S.inc(v.tensor_scalar(sl[:, 8:9], sl[:, 2:3], sl[:, 7:8], -1.0,
                                      ALU.mult, ALU.mult), "ve", 1)
            emit("vector", v3)
            ve_nm = S.val("ve")

            def s3(se):
                S.wait(se, "ve", ve_nm)
                if dst_free is not None:
                    S.dwait(se, dst_free)
                if dst_free_ve:
                    S.wait(se, "ve", dst_free_ve)
                S.inc(se.activation(dst_ap, src_ap, AF.Identity,
                                    bias=sl[:, 8:9], scale=sl[:, 7:8]), "se", 1)
            emit("scalar", s3)
            return S.val("se")

        def emit_ln_to_xnt(src_kind, xnt_dst, sq_ap, ld_halves,
                           src_ready=None, sq_free=None, dst_free_marks=None):
            prev_se = [0, 0]
            tr_marks = []
            for tt in range(NT):
                half = tt % 2
                ldt, ldw = ld_halves
                ld_buf = ldt[:, half * ldw:(half + 1) * ldw]
                src = (x_in[tt * 128:(tt + 1) * 128, :] if src_kind == "x0"
                       else scr["x2sp"][tt, :, :])

                lm = []

                def g_ld(g, ld_buf=ld_buf, src=src, tt=tt, pse=prev_se[half]):
                    if tt < 2:
                        S.wait(g, "ve", ve_consts)
                    else:
                        S.wait(g, "se", pse)
                    if src_ready is not None:
                        S.dwait(g, src_ready[tt])
                    lm.append(S.dma(g, "glx", out=ld_buf, in_=src))
                emit("gpsimd", g_ld)

                se_done = emit_ln_tile(
                    ld_buf, xnt[:, half * 2048:(half + 1) * 2048], sq_ap,
                    pre_mark=(lm + (sq_free or []) if tt < 2 else lm),
                    dst_free=(tr_marks[tt - 2] if tt >= 2 else None))
                prev_se[half] = se_done

                def s_tr(sy, tt=tt, half=half, se_done=se_done):
                    if tt == 0 and dst_free_marks:
                        for mk in dst_free_marks:
                            S.dwait(sy, mk)
                    S.wait(sy, "se", se_done)
                    tr_marks.append(S.dma(
                        sy, "tr", transpose_call=True,
                        out=b3(xnt_dst, 2048)[:, :, tt * 128:(tt + 1) * 128],
                        in_=xnt[:, half * 2048:(half + 1) * 2048]))
                emit("sync", s_tr)
            return tr_marks[-2:]

        def emit_z_all(xnT, tr_xnt_all):
            for i, (bt, zdst) in enumerate(zip([BTq, BTk, BTv], [zq, zk, zv])):
                ve_prev = S.val("ve")

                def t_fn(t, i=i, bt=bt, ve_prev=ve_prev):
                    if i == 0:
                        for mk in tr_xnt_all:
                            S.dwait(t, mk)
                        for mk in bt_done:
                            S.dwait(t, mk)
                        S.wait(t, "ve", ve_consts)
                    else:
                        S.wait(t, "ve", ve_prev)
                    last = None
                    for dc in range(ND):
                        for tc in range(4):
                            last = t.matmul(
                                PG[tc][0:16, :], b3(bt, 16)[:, dc, :],
                                xnT[:, dc * 2048 + tc * 512:dc * 2048 + (tc + 1) * 512],
                                start=(dc == 0), stop=(dc == ND - 1))
                    S.inc(last, "te", 1)
                emit("tensor", t_fn)
                te_z = S.val("te")

                def v_fn(v, zdst=zdst, te_z=te_z):
                    S.wait(v, "te", te_z)
                    last = None
                    for tc in range(4):
                        last = v.tensor_copy(zdst[:, tc * 512:(tc + 1) * 512],
                                             PG[tc][0:16, :])
                    S.inc(last, "ve", 1)
                emit("vector", v_fn)
            return S.val("ve")

        def emit_v_all(xnT, wvT_sb, nheads, vsp, tr_xnt_all, wv_ld_mark,
                       first_ve_wait, brow_gld, zvbuf, av_ready_sld):
            noc = (nheads * 128) // 512
            vst_marks = {}
            vhalf_free = {}
            for tt in range(NT):
                def t_fn(t, tt=tt):
                    if tt == 0:
                        for mk in tr_xnt_all:
                            S.dwait(t, mk)
                        S.dwait(t, wv_ld_mark)
                        S.wait(t, "ve", first_ve_wait)
                        S.dwait(t, brow_gld)
                        if av_ready_sld:
                            S.dwait(t, av_ready_sld)
                    for oc in range(noc):
                        gidx = tt * noc + oc
                        if gidx >= 4:
                            S.wait(t, "ve", vhalf_free[gidx - 4])
                    last = None
                    for dc in range(ND):
                        for oc in range(noc):
                            last = t.matmul(
                                PG[(tt * noc + oc) % 4],
                                b3(xnT, 2048)[:, dc, tt * 128:(tt + 1) * 128],
                                wvT_sb[:, dc * (nheads * 128) + oc * 512:
                                       dc * (nheads * 128) + (oc + 1) * 512],
                                start=(dc == 0), stop=False)
                    for oc in range(noc):
                        gidx = tt * noc + oc
                        if zvbuf is not None:
                            t.matmul(PG[gidx % 4],
                                     zvbuf[0:16, tt * 128:(tt + 1) * 128],
                                     ATs[0:16, oc * 512:(oc + 1) * 512],
                                     start=False, stop=False)
                        last = t.matmul(PG[gidx % 4], onesr[0:1, :],
                                        brow[0:1, oc * 512:(oc + 1) * 512],
                                        start=False, stop=True)
                        S.inc(last, "te", 1)
                emit("tensor", t_fn)
                te_last = S.val("te")

                for oc in range(noc):
                    gidx = tt * noc + oc
                    half = gidx % 2
                    te_oc = te_last - (noc - 1 - oc)

                    def v_fn(v, tt=tt, oc=oc, gidx=gidx, half=half, te_oc=te_oc):
                        S.wait(v, "te", te_oc)
                        if gidx >= 2:
                            S.dwait(v, vst_marks[gidx - 2])
                        S.inc(v.tensor_copy(
                            b3(vstg[:, half * 528:(half + 1) * 528], 132)[:, :, 0:128],
                            b3(PG[gidx % 4], 128)), "ve", 1)
                    emit("vector", v_fn)
                    vhalf_free[gidx] = S.val("ve")

                    def s_sp(sy, tt=tt, oc=oc, half=half, vm=vhalf_free[gidx],
                             gidx2=gidx):
                        S.wait(sy, "ve", vm)
                        vst_marks[gidx2] = S.dma(
                            sy, "st",
                            out=vsp[tt, :, oc * 528:(oc + 1) * 528],
                            in_=vstg[:, half * 528:(half + 1) * 528])
                    emit("sync", s_sp)
            return vst_marks

        def emit_projT(xnT, ws_buf, dst, bias_col_ap, lora_at, zbuf,
                       sld_mark, groups, state, rope=False, z_ve=0):
            def t_fn(t):
                for mk in sld_mark:
                    S.dwait(t, mk)
                if z_ve:
                    S.wait(t, "ve", z_ve)
                S.wait(t, "ve", state["proj_evac_prev"])
                S.wait(t, "se", max(state["trip_last_exp"]))
                S.wait(t, "ve", max(state["ob_last_evac"]))
                last = None
                for dc in range(ND):
                    for tc in range(4):
                        last = t.matmul(
                            groups[tc], b3(ws_buf, 128)[:, dc, :],
                            xnT[:, dc * 2048 + tc * 512:dc * 2048 + (tc + 1) * 512],
                            start=(dc == 0),
                            stop=(dc == ND - 1 and lora_at is None))
                if lora_at is not None:
                    for tc in range(4):
                        last = t.matmul(
                            groups[tc], ATs[0:16, lora_at:lora_at + 128],
                            zbuf[0:16, tc * 512:(tc + 1) * 512],
                            start=False, stop=True)
                S.inc(last, "te", 1)
            emit("tensor", t_fn)
            te_p = S.val("te")

            def v_fn(v):
                S.wait(v, "te", te_p)
                last = None
                for tc in range(4):
                    last = v.tensor_scalar(dst[:, tc * 512:(tc + 1) * 512],
                                           groups[tc], bias_col_ap, None,
                                           ALU.add)
                S.inc(last, "ve", 1)
                if rope:
                    S.wait(v, "ve", S.val("ve"))
                    S.inc(v.tensor_tensor(dst[:, :], dst[:, :], aux32[:, :],
                                          ALU.mult), "ve", 1)
            emit("vector", v_fn)
            state["proj_evac_prev"] = S.val("ve")
            if rope:
                state["rope_ve"] = S.val("ve")
            return S.val("ve")

        def emit_attention_head(qT, kT, vhbuf, etbufs, causal, osg,
                                se_ready, ve_ready, vh_mark, state,
                                kt_mark=None):
            exp_qc, mask_qc = {}, {}

            def emit_s(qc):
                kt_hi = (qc * 4 + 4) if causal else 16
                ngrp = (kt_hi + 2) // 3
                exp_marks = []
                for ktg in range(ngrp):
                    kts = list(range(3 * ktg, min(3 * ktg + 3, kt_hi)))
                    gsz = len(kts)
                    par = state["sparity"] % 2
                    state["sparity"] += 1
                    trip = STRIPLE[par]
                    need_exp = state["trip_last_exp"][par]

                    def t_s(t, kts=kts, trip=trip, need_exp=need_exp, qc=qc,
                            first=(ktg == 0 and qc == 0)):
                        if first:
                            S.wait(t, "ve", se_ready)
                            if ve_ready:
                                S.wait(t, "ve", ve_ready)
                            if kt_mark is not None:
                                S.dwait(t, kt_mark)
                        S.wait(t, "se", need_exp)
                        inst = None
                        for j, kt in enumerate(kts):
                            inst = t.matmul(
                                trip[:, j * 512:(j + 1) * 512],
                                kT[:, kt * 128:(kt + 1) * 128],
                                qT[:, qc * 512:(qc + 1) * 512],
                                start=True, stop=True)
                        S.inc(inst, "te", 1)
                    emit("tensor", t_s)
                    te_s = S.val("te")

                    etdst = etbufs[qc % 2]
                    oread = state["et_last_read"][qc % 2]

                    def s_e(se, te_s=te_s, trip=trip, gsz=gsz, etdst=etdst,
                            ktg=ktg, oread=oread):
                        S.wait(se, "te", te_s)
                        S.wait(se, "te", oread)
                        S.inc(se.activation(
                            etdst[:, (3 * ktg) * 512:(3 * ktg + gsz) * 512],
                            trip[:, 0:gsz * 512], AF.Exp,
                            bias=cst[:, 0:1], scale=ISQ), "se", 1)
                    emit("scalar", s_e)
                    exp_marks.append(S.val("se"))
                    state["trip_last_exp"][par] = S.val("se")
                exp_qc[qc] = exp_marks

                if causal:
                    mm = {}
                    for qt_l in range(4):
                        qt = qc * 4 + qt_l
                        kt = qt
                        gi = kt // 3
                        off = kt * 512 + qt_l * 128

                        def v_m(v, off=off, qc=qc, em=exp_marks[gi]):
                            S.wait(v, "se", em)
                            S.inc(v.tensor_tensor(
                                etbufs[qc % 2][:, off:off + 128],
                                etbufs[qc % 2][:, off:off + 128],
                                dmask[:, :], ALU.mult), "ve", 1)
                        emit("vector", v_m)
                        mm[qt_l] = S.val("ve")
                    mask_qc[qc] = mm

            def emit_o(qc):
                for qt_l in range(4):
                    qt = qc * 4 + qt_l
                    kts = list(range(qt + 1)) if causal else list(range(16))
                    opar = state["oparity"] % 2
                    state["oparity"] += 1
                    ob = [ps_oa, ps_ob][opar]
                    ev_prev = state["ob_last_evac"][opar]
                    mm = mask_qc.get(qc, {}).get(qt_l, 0)

                    def t_o(t, qt_l=qt_l, kts=kts, ob=ob, qc=qc, ev_prev=ev_prev,
                            em=exp_qc[qc][-1], mm=mm,
                            first=(qc == 0 and qt_l == 0)):
                        if first and vh_mark is not None:
                            S.dwait(t, vh_mark)
                        S.wait(t, "se", em)
                        if mm:
                            S.wait(t, "ve", mm)
                        S.wait(t, "ve", ev_prev)
                        inst = None
                        for idx, kt in enumerate(kts):
                            inst = t.matmul(
                                ob[:, 0:129],
                                etbufs[qc % 2][:, kt * 512 + qt_l * 128:
                                               kt * 512 + qt_l * 128 + 128],
                                b3(vhbuf, 132)[:, kt, 0:129],
                                start=(idx == 0), stop=(idx == len(kts) - 1))
                        S.inc(inst, "te", 1)
                    emit("tensor", t_o)
                    te_o = S.val("te")
                    state["et_last_read"][qc % 2] = te_o

                    def v_o(v, qt=qt, ob=ob, te_o=te_o):
                        S.wait(v, "te", te_o)
                        rc = S.inc(v.reciprocal(invd[:, qt % 4:qt % 4 + 1],
                                                ob[:, 128:129]), "ve", 1)
                        S.wait(v, "ve", rc)
                        S.inc(v.tensor_scalar(
                            osg[:, qt * 128:(qt + 1) * 128], ob[:, 0:128],
                            invd[:, qt % 4:qt % 4 + 1], None, ALU.mult), "ve", 1)
                    emit("vector", v_o)
                    state["ob_last_evac"][opar] = S.val("ve")

            for qc in range(4):
                emit_s(qc)
                if qc >= 1:
                    emit_o(qc - 1)
            emit_o(3)
            return S.val("ve")

        def emit_oproj(wtname, wt_st_mark, attn_done_st, bias_gld, lora,
                       resid_kind, resid_buf, dst_bigA, write_x2sp):
            pre_se = S.val("se")
            pre_trm = [("tr", p, S.dc["tr"][p]) for p in (0, 1)
                       if S.dc["tr"][p] > 0]
            wo_marks = []

            def s_wo(sy):
                S.dwait(sy, wt_st_mark)
                S.dwait(sy, attn_done_st)
                wo_marks.append(S.dma(sy, "sld", out=bigB[:, :], in_=scr[wtname]))
                if lora:
                    S.dwait(sy, at_done[3])
                    wo_marks.append(S.dma(sy, "sld", out=ATs[:, :],
                                          in_=scr["ATsc"][:, 3 * D:4 * D]))
            emit("sync", s_wo)

            zo_ve = {}
            prev_otld_reader = [0, 0]
            prev_y_evac = [0]
            for tt in range(NT):
                half = tt % 2

                otm_l = []

                def s_otl(sy, tt=tt, half=half, pe=prev_otld_reader[half]):
                    if tt == 0:
                        for mk in wo_marks:
                            S.dwait(sy, mk)
                    if tt >= 2:
                        S.wait(sy, "te", pe)
                    otm_l.append(S.dma(
                        sy, "sld",
                        out=otld[:, half * 2048:(half + 1) * 2048],
                        in_=scr["OTsc"].rearrange("h p t -> p h t")[
                            :, :, tt * 128:(tt + 1) * 128]))
                emit("sync", s_otl)

                res_l = []

                def g_res(g, tt=tt, half=half, py=prev_y_evac[0]):
                    if tt < 2:
                        S.wait(g, "se", pre_se)
                        for mk in pre_trm:
                            S.dwait(g, mk)
                    else:
                        S.wait(g, "ve", py)
                    src = (x_in[tt * 128:(tt + 1) * 128, :] if resid_kind == "x0"
                           else scr["x2sp"][tt, :, :])
                    res_l.append(S.dma(
                        g, "grs",
                        out=resid_buf[:, half * 2048:(half + 1) * 2048],
                        in_=src))
                emit("gpsimd", g_res)

                if lora:
                    def t_zo(t, tt=tt, half=half,
                             pz=zo_ve.get(tt - 2, 0)):
                        S.dwait(t, otm_l[-1])
                        S.wait(t, "ve", pz)
                        inst = None
                        for oc in range(16):
                            inst = t.matmul(
                                PZ[0:16, 0:128], b3(BTo, 16)[:, oc, :],
                                otld[:, half * 2048 + oc * 128:
                                     half * 2048 + (oc + 1) * 128],
                                start=(oc == 0), stop=(oc == 15))
                        S.inc(inst, "te", 1)
                    emit("tensor", t_zo)
                    te_zo = S.val("te")

                    def v_zo(v, half=half, te_zo=te_zo):
                        S.wait(v, "te", te_zo)
                        S.inc(v.tensor_copy(zob[:, half * 128:(half + 1) * 128],
                                            PZ[0:16, 0:128]), "ve", 1)
                    emit("vector", v_zo)
                    zo_ve[tt] = S.val("ve")

                def t_y(t, tt=tt, half=half,
                        zv_=zo_ve.get(tt, 0), py=prev_y_evac[0]):
                    S.dwait(t, otm_l[-1])
                    if lora:
                        S.wait(t, "ve", zv_)
                    S.wait(t, "ve", py)
                    S.dwait(t, bias_gld)
                    last = None
                    for oc in range(16):
                        for dch in range(4):
                            last = t.matmul(
                                PG[dch],
                                otld[:, half * 2048 + oc * 128:
                                     half * 2048 + (oc + 1) * 128],
                                bigB[:, oc * 2048 + dch * 512:
                                     oc * 2048 + (dch + 1) * 512],
                                start=(oc == 0), stop=False)
                    for dch in range(4):
                        if lora:
                            last = t.matmul(
                                PG[dch], zob[0:16, half * 128:(half + 1) * 128],
                                ATs[0:16, dch * 512:(dch + 1) * 512],
                                start=False, stop=False)
                        last = t.matmul(
                            PG[dch], onesr[0:1, :],
                            brow[0:1, dch * 512:(dch + 1) * 512],
                            start=False, stop=True)
                    S.inc(last, "te", 1)
                emit("tensor", t_y)
                te_y = S.val("te")
                prev_otld_reader[half] = te_y

                def v_y(v, tt=tt, half=half, te_y=te_y):
                    S.wait(v, "te", te_y)
                    S.dwait(v, res_l[-1])
                    last = None
                    for dch in range(4):
                        src2 = resid_buf[:, half * 2048 + dch * 512:
                                         half * 2048 + (dch + 1) * 512]
                        last = v.tensor_tensor(
                            dst_bigA[:, tt * 2048 + dch * 512:
                                     tt * 2048 + (dch + 1) * 512],
                            PG[dch], src2, ALU.add)
                    S.inc(last, "ve", 1)
                emit("vector", v_y)
                prev_y_evac[0] = S.val("ve")
                ve_y = S.val("ve")

                if write_x2sp:
                    def s_x2(sy, tt=tt, ve_y=ve_y):
                        S.wait(sy, "ve", ve_y)
                        S.dma(sy, "st", out=scr["x2sp"][tt, :, :],
                              in_=dst_bigA[:, tt * 2048:(tt + 1) * 2048])
                    emit("sync", s_x2)
            return S.val("ve"), S.val("te")

        def load_brow(src_name, width=2048):
            mk = []

            def g_b(g):
                S.wait(g, "te")
                mk.append(S.dma(g, "gld", out=brow[0:1, 0:width],
                                in_=wts[src_name].rearrange("(o a) -> o a", o=1)))
            emit("gpsimd", g_b)
            return mk[0]

        wprep_state = {"wtr": [], "gl_ch_acc": [], "k1_se": [], "wst": [],
                       "fm": []}

        def emit_ffn_wprep_tile(ot, first_se, wnat_free, wtt_free):
            wtr = wprep_state["wtr"]
            ch_acc = wprep_state["gl_ch_acc"]
            k1_se = wprep_state["k1_se"]
            wst = wprep_state["wst"]
            fm = []

            def s_ld(sy, ot=ot):
                if ot < 2:
                    S.wait(sy, "se", first_se)
                else:
                    S.wait(sy, "ve", k1_se[ot - 2])
                fm.append(S.dma(
                    sy, "sld",
                    out=stage32[:, (ot % 2) * 2048:(ot % 2 + 1) * 2048],
                    in_=wts["fW"][ot * 128:(ot + 1) * 128, :]))
            emit("sync", s_ld)

            def v_q(v, ot=ot):
                S.dwait(v, fm[-1])
                r = S.inc(v.reduce_max(sl[:, 9:10],
                                       stage32[:, (ot % 2) * 2048:(ot % 2 + 1) * 2048],
                                       axis=AXX, apply_absolute_value=True),
                          "ve", 1)
                S.wait(v, "ve", r)
                r = S.inc(v.tensor_scalar(wscol[:, ot:ot + 1], sl[:, 9:10],
                                          1e-8, 1.0 / 127.0, ALU.max, ALU.mult),
                          "ve", 1)
                S.wait(v, "ve", r)
                S.inc(v.reciprocal(sl[:, 11:12], wscol[:, ot:ot + 1]), "ve", 1)
            emit("vector", v_q)
            ve_iw = S.val("ve")

            for ch in range(2):
                co = (ot % 2) * 2048 + ch * 1024

                def s_k1(se, co=co, ve_iw=ve_iw, ot=ot, ch=ch,
                         pva=(ch_acc[-1] if ch_acc else 0)):
                    S.wait(se, "ve", ve_iw)
                    if pva:
                        S.wait(se, "ve", pva)
                    if ot == 0 and ch == 0 and wnat_free is not None:
                        S.dwait(se, wnat_free)
                    if wtr and ch == 0:
                        S.dwait(se, wtr[-1])
                    S.inc(se.activation(k1t[:, :], stage32[:, co:co + 1024],
                                        AF.Identity, bias=cst[:, 1:2],
                                        scale=sl[:, 11:12]), "se", 1)
                emit("scalar", s_k1)
                se_k1 = S.val("se")

                def v_r(v, co=co, se_k1=se_k1, ch=ch, ot=ot):
                    S.wait(v, "se", se_k1)
                    r = S.inc(v.scalar_tensor_tensor(
                        aux32[:, 0:1024], stage32[:, co:co + 1024],
                        sl[:, 11:12], k1t[:, :], ALU.mult, ALU.subtract),
                        "ve", 1)
                    S.wait(v, "ve", r)
                    r = S.inc(v.scalar_tensor_tensor(
                        aux32[:, 1024:2048], aux32[:, 0:1024], -127.5,
                        k1t[:, :], ALU.is_ge, ALU.add), "ve", 1)
                    S.wait(v, "ve", r)
                    S.inc(v.tensor_scalar(
                        wnat[:, ch * 1024:(ch + 1) * 1024],
                        aux32[:, 1024:2048], -128.0, 0.0, ALU.add, ALU.add,
                        accum_out=sl[:, 12 + ch:13 + ch]), "ve", 1)
                emit("vector", v_r)
                ch_acc.append(S.val("ve"))

            def v_s(v, ot=ot, m=ch_acc[-1]):
                S.wait(v, "ve", m)
                S.inc(v.tensor_tensor(wsumc[:, ot:ot + 1], sl[:, 12:13],
                                      sl[:, 13:14], ALU.add), "ve", 1)
            emit("vector", v_s)
            ve_wq = S.val("ve")

            def s_tr2(sy, ot=ot, ve_wq=ve_wq):
                S.wait(sy, "ve", ve_wq)
                if ot == 0 and wtt_free is not None:
                    S.dwait(sy, wtt_free)
                if wst:
                    S.dwait(sy, wst[-1])
                wtr.append(S.dma(sy, "tr", transpose_call=True,
                                 out=b3(wtt, 128), in_=wnat[:, :]))
            emit("sync", s_tr2)

            def q_st(sq, ot=ot):
                S.dwait(sq, wtr[-1])
                wst.append(S.dma(
                    sq, "st",
                    out=scr["WTf"].rearrange("p (dc o) -> p dc o", o=D)[
                        :, :, ot * 128:(ot + 1) * 128],
                    in_=b3(wtt, 128)))
            emit("scalar", q_st)
            k1_se.append(ve_wq)

        def emit_wprep_cols():
            def v_c2(v, pv=S.val("ve")):
                S.wait(v, "ve", pv)
                r = S.inc(v.tensor_scalar(betac[:, :], wsumc[:, :], -128.0, None,
                                          ALU.mult), "ve", 1)
                S.wait(v, "ve", r)
                r = S.inc(v.tensor_copy(c2hc[:, :], betac[:, :]), "ve", 1)
                S.wait(v, "ve", r)
                r = S.inc(v.tensor_tensor(wsumc[:, :], betac[:, :], c2hc[:, :],
                                          ALU.subtract), "ve", 1)
                S.wait(v, "ve", r)
                r = S.inc(v.tensor_copy(c2lc[:, :], wsumc[:, :]), "ve", 1)
                S.wait(v, "ve", r)
                r = S.inc(v.reciprocal(betac[:, :], wscol[:, :]), "ve", 1)
                S.wait(v, "ve", r)
                S.inc(v.tensor_tensor(betac[:, :], betac[:, :], fbcol[:, :],
                                      ALU.mult), "ve", 1)
            emit("vector", v_c2)
            return S.val("ve")

        brow_gld = load_brow("abv")
        tr_all_a = emit_ln_to_xnt("x0", bigA, otld[:, 2048:4096], (stage32, 2048))
        ve_z = emit_z_all(bigA, tr_all_a)

        wv_marks = []

        def s_wv(sy):
            S.dwait(sy, wprep_st["aWv"])
            wv_marks.append(S.dma(sy, "sld", out=bigB[:, :], in_=scr["WTva"]))
            S.dwait(sy, at_done[2])
            wv_marks.append(S.dma(sy, "sld", out=ATs[:, :],
                                  in_=scr["ATsc"][:, 2 * D:3 * D]))
        emit("sync", s_wv)
        vsp_done = emit_v_all(bigA, bigB, H, scr["Vspa"], tr_all_a,
                              wv_marks[0], ve_z, brow_gld, zv, wv_marks[1])

        state = {"sparity": 0, "oparity": 0,
                 "trip_last_exp": [0, 0], "ob_last_evac": [0, 0],
                 "et_last_read": [0, 0], "proj_evac_prev": 0}
        ln_a_se = S.val("se")
        pw_tr_last = pw_tr[-1] if pw_tr else None
        pw_st_last = pw_st[-1] if pw_st else None
        ot_st = []
        for h in range(H):
            hm = []

            def s_ws(sy, h=h, pe=state["proj_evac_prev"],
                     obm=max(state["ob_last_evac"])):
                if h == 0:
                    S.dwait(sy, wprep_st["aWk"])
                    for mk in vsp_done.values():
                        S.dwait(sy, mk)
                    S.dwait(sy, at_done[1])
                else:
                    S.wait(sy, "ve", max(pe, obm))
                hm.append(S.dma(
                    sy, "sld", out=b3(WS[0], 128),
                    in_=scr["WTqa"].rearrange("p (dc o) -> p dc o", o=D)[
                        :, :, h * 128:(h + 1) * 128]))
                hm.append(S.dma(
                    sy, "sld", out=b3(WS[1], 128),
                    in_=scr["WTka"].rearrange("p (dc o) -> p dc o", o=D)[
                        :, :, h * 128:(h + 1) * 128]))
                hm.append(S.dma(
                    sy, "sld", out=b3(Vh, 132),
                    in_=scr["Vspa"].rearrange("t p (hh w) -> p t hh w", w=132)[
                        :, :, h, :]))
                hm.append(S.dma(
                    sy, "sld", out=ATs[0:16, 0:384],
                    in_=scr["ATsc"].rearrange("r (pj o) -> r pj o", o=D)[
                        :, 0:3, h * 128:(h + 1) * 128]))
            emit("sync", s_ws)

            emit_projT(bigA, WS[0], QTh, bcolq[:, h:h + 1], 0, zq,
                       hm, PG, state, z_ve=ve_z)
            se_k = emit_projT(bigA, WS[1], KTh, bcolk[:, h:h + 1], 128, zk,
                              hm, KGB, state, z_ve=ve_z)

            ve_head = emit_attention_head(QTh, KTh, Vh, ET, False, OSg,
                                          se_k, 0, hm[2], state)

            otr = []

            def s_ot(sy, ve_head=ve_head):
                S.wait(sy, "ve", ve_head)
                if ot_st:
                    S.dwait(sy, ot_st[-1])
                otr.append(S.dma(sy, "tr", transpose_call=True,
                                 out=b3(OTt, 128), in_=OSg[:, :]))
            emit("sync", s_ot)

            def q_ot(sq, h=h):
                S.dwait(sq, otr[0])
                ot_st.append(S.dma(sq, "st", out=scr["OTsc"][h, :, :],
                                   in_=OTt[:, :]))
            emit("scalar", q_ot)
            if upto >= 2:
                emit_ffn_wprep_tile(h, ln_a_se, pw_tr_last, pw_st_last)
        attn_a_st = ot_st[-1]

        brow_gld = load_brow("abo")
        ve_x1, oproj_a_te = emit_oproj("WToa", wprep_st["aWo"], attn_a_st,
                                       brow_gld, True, "x0", stage32, bigA,
                                       False)



        def emit_rows_and_wsb(ve_cols):
            for ot in range(NT):
                def v_cp(v, ot=ot, tp=S.val("te")):
                    S.wait(v, "ve", ve_cols)
                    S.wait(v, "te", tp)
                    r = S.inc(v.tensor_copy(alc[:, 1:2], betac[:, ot:ot + 1]),
                              "ve", 1)
                    S.wait(v, "ve", r)
                    r = S.inc(v.tensor_copy(alc[:, 2:3], c2hc[:, ot:ot + 1]),
                              "ve", 1)
                    S.wait(v, "ve", r)
                    S.inc(v.tensor_copy(alc[:, 3:4], c2lc[:, ot:ot + 1]), "ve", 1)
                emit("vector", v_cp)
                ve_cp = S.val("ve")

                def t_m(t, ve_cp=ve_cp):
                    S.wait(t, "ve", ve_cp)
                    S.inc(t.matmul(PZ[0:3, 0:128], alc[:, 1:4], identb[:, :],
                                   start=True, stop=True), "te", 1)
                emit("tensor", t_m)
                te_m = S.val("te")

                def v_row(v, ot=ot, te_m=te_m):
                    S.wait(v, "te", te_m)
                    S.inc(v.tensor_copy(rowsB[0:3, ot * 128:(ot + 1) * 128],
                                        PZ[0:3, 0:128]), "ve", 1)
                emit("vector", v_row)
            for ot in range(NT):
                def t_w(t, ot=ot, vp=S.val("ve")):
                    S.wait(t, "ve", vp)
                    S.inc(t.matmul(PZ[0:1, 128:256], wscol[:, ot:ot + 1],
                                   identf[:, :], start=True, stop=True), "te", 1)
                emit("tensor", t_w)
                te_w = S.val("te")

                def v_w2(v, ot=ot, te_w=te_w):
                    S.wait(v, "te", te_w)
                    S.inc(v.tensor_copy(stage32[0:1, 2048 + ot * 128:2048 + (ot + 1) * 128],
                                        PZ[0:1, 128:256]), "ve", 1)
                emit("vector", v_w2)
            ve_wsr = S.val("ve")
            for oc in range(4):
                def t_b(t, oc=oc, vp=ve_wsr):
                    S.wait(t, "ve", vp)
                    S.inc(t.matmul(PG[oc], onesf[0:1, :],
                                   stage32[0:1, 2048 + oc * 512:2048 + (oc + 1) * 512],
                                   start=True, stop=True), "te", 1)
                emit("tensor", t_b)
                te_b = S.val("te")

                def v_b(v, oc=oc, te_b=te_b):
                    S.wait(v, "te", te_b)
                    S.inc(v.tensor_copy(aux32[:, oc * 512:(oc + 1) * 512],
                                        PG[oc]), "ve", 1)
                emit("vector", v_b)
            return S.val("ve")

        def emit_ffn(first_ve, wqt_ready, final_out):
            xq_tr = []
            mm_te = []
            xq_ve = {}
            xq_chain = []
            ep_ve = [first_ve]
            out_st = []
            for tt in range(NT):
                half = tt % 2
                src = bigA[:, tt * 2048:(tt + 1) * 2048]
                pm0 = None
                if tt < 2:
                    pm0 = [("tr", p, S.dc["tr"][p]) for p in (0, 1)
                           if S.dc["tr"][p] > 0]
                se_ln = emit_ln_tile(src, xnt[:, half * 2048:(half + 1) * 2048],
                                     wnat[:, :], pre_mark=pm0,
                                     pre_ve=(ep_ve[0] if tt < 2 else None),
                                     dst_free=(xq_tr[tt - 2] if tt >= 2 else None),
                                     dst_free_ve=(xq_ve[tt - 2] if tt >= 2
                                                  else None))

                def v_qx(v, tt=tt, half=half, se_ln=se_ln):
                    S.wait(v, "se", se_ln)
                    r = S.inc(v.reduce_max(sl[:, 9:10],
                                           xnt[:, half * 2048:(half + 1) * 2048],
                                           axis=AXX, apply_absolute_value=True),
                              "ve", 1)
                    S.wait(v, "ve", r)
                    r = S.inc(v.tensor_scalar(sl[:, 10:11], sl[:, 9:10],
                                              1.0 / 127.0, 1e-12, ALU.mult, ALU.max),
                              "ve", 1)
                    S.wait(v, "ve", r)
                    r = S.inc(v.reciprocal(sl[:, 11:12], sl[:, 10:11]), "ve", 1)
                    S.wait(v, "ve", r)
                    if mm_te:
                        S.wait(v, "te", mm_te[-1])
                    S.inc(v.tensor_copy(alc[:, 0:1], sl[:, 11:12]), "ve", 1)
                emit("vector", v_qx)
                ve_iq = S.val("ve")

                def t_al(t, ve_iq=ve_iq):
                    S.wait(t, "ve", ve_iq)
                    S.inc(t.matmul(PZ[0:1, 0:128], alc[:, 0:1], identb[:, :],
                                   start=True, stop=True), "te", 1)
                emit("tensor", t_al)
                te_al = S.val("te")

                def v_al(v, te_al=te_al):
                    S.wait(v, "te", te_al)
                    S.inc(v.tensor_copy(rowsA[0:1, :], PZ[0:1, 0:128]), "ve", 1)
                emit("vector", v_al)
                ve_al = S.val("ve")

                for ch in range(2):
                    xo = half * 2048 + ch * 1024

                    def s_k1(se, xo=xo, ve_iq=ve_iq, tt=tt, ch=ch,
                             pva=(xq_chain[-1] if xq_chain else 0)):
                        S.wait(se, "ve", ve_iq)
                        if pva:
                            S.wait(se, "ve", pva)
                        if tt > 0 and ch == 0:
                            S.dwait(se, xq_tr[-1])
                        S.inc(se.activation(k1t[:, :], xnt[:, xo:xo + 1024],
                                            AF.Identity, bias=cst[:, 1:2],
                                            scale=sl[:, 11:12]), "se", 1)
                    emit("scalar", s_k1)
                    se_k1 = S.val("se")

                    def v_r(v, xo=xo, se_k1=se_k1, ch=ch):
                        S.wait(v, "se", se_k1)
                        r = S.inc(v.scalar_tensor_tensor(
                            stage32[:, ch * 1024:(ch + 1) * 1024],
                            xnt[:, xo:xo + 1024], sl[:, 11:12], k1t[:, :],
                            ALU.mult, ALU.subtract), "ve", 1)
                        S.wait(v, "ve", r)
                        S.inc(v.scalar_tensor_tensor(
                            xnt[:, xo:xo + 1024],
                            stage32[:, ch * 1024:(ch + 1) * 1024], -127.5,
                            k1t[:, :], ALU.is_ge, ALU.add), "ve", 1)
                    emit("vector", v_r)
                    xq_chain.append(S.val("ve"))
                ve_xq = S.val("ve")
                xq_ve[tt] = ve_xq

                def s_xtr(sy, tt=tt, half=half, ve_xq=ve_xq):
                    S.wait(sy, "ve", ve_xq)
                    if len(mm_te) >= 2:
                        S.wait(sy, "te", mm_te[-2])
                    xq_tr.append(S.dma(
                        sy, "tr", transpose_call=True,
                        out=b3(otld[:, half * 2048:(half + 1) * 2048], 128),
                        in_=xnt[:, half * 2048:(half + 1) * 2048]))
                emit("sync", s_xtr)

                def t_mm(t, tt=tt, half=half, ve_al=ve_al):
                    S.dwait(t, xq_tr[-1])
                    S.wait(t, "ve", ve_al)
                    if tt == 0:
                        for mk in wqt_ready:
                            S.dwait(t, mk)
                    S.wait(t, "ve", ep_ve[0])
                    last = None
                    for dc in range(ND):
                        for oc in range(4):
                            last = t.matmul(
                                PG[oc],
                                b3(otld[:, half * 2048:(half + 1) * 2048], 128)[:, dc, :],
                                bigB[:, dc * 2048 + oc * 512:dc * 2048 + (oc + 1) * 512],
                                start=(dc == 0), stop=False)
                    for oc in range(4):
                        last = t.matmul(PG[oc], rowsA[0:3, :],
                                        rowsB[0:3, oc * 512:(oc + 1) * 512],
                                        start=False, stop=True)
                    S.inc(last, "te", 1)
                emit("tensor", t_mm)
                mm_te.append(S.val("te"))

                def v_ep(v, tt=tt, half=half, m=mm_te[-1]):
                    S.wait(v, "te", m)
                    if final_out and tt >= 1:
                        S.dwait(v, out_st[tt - 1])
                    last = None
                    for oc in range(4):
                        r = S.inc(v.scalar_tensor_tensor(
                            stage32[:, 2048 + oc * 512:2048 + (oc + 1) * 512],
                            PG[oc], sl[:, 10:11], aux32[:, oc * 512:(oc + 1) * 512],
                            ALU.mult, ALU.mult), "ve", 1)
                        S.wait(v, "ve", r)
                        last = v.tensor_tensor(
                            (stage32[:, 2048 + oc * 512:2048 + (oc + 1) * 512]
                             if final_out else
                             bigA[:, tt * 2048 + oc * 512:tt * 2048 + (oc + 1) * 512]),
                            stage32[:, 2048 + oc * 512:2048 + (oc + 1) * 512],
                            bigA[:, tt * 2048 + oc * 512:tt * 2048 + (oc + 1) * 512],
                            ALU.add)
                    S.inc(last, "ve", 1)
                emit("vector", v_ep)
                ep_ve[0] = S.val("ve")
                ve_y = S.val("ve")

                if final_out:
                    def s_o(sy, tt=tt, ve_y=ve_y):
                        S.wait(sy, "ve", ve_y)
                        out_st.append(S.dma(sy, "st",
                                            out=out_t[tt * 128:(tt + 1) * 128, :],
                                            in_=stage32[:, 2048:4096]))
                    emit("sync", s_o)
                else:
                    def s_o(sy, tt=tt, ve_y=ve_y):
                        S.wait(sy, "ve", ve_y)
                        out_st.append(S.dma(sy, "st",
                                            out=scr["x2sp"][tt, :, :],
                                            in_=bigA[:, tt * 2048:(tt + 1) * 2048]))
                    emit("sync", s_o)
            return S.val("ve"), out_st, mm_te[-1]

        if upto >= 2:
            ve_cols = emit_wprep_cols()
            ve_rows = emit_rows_and_wsb(ve_cols)
            wst = wprep_state["wst"]

            wq_marks = []

            def s_wq(sy):
                S.dwait(sy, wst[-1])
                S.dwait(sy, wst[-2])
                S.wait(sy, "te", oproj_a_te)
                wq_marks.append(S.dma(sy, "sld", out=bigB[:, :], in_=scr["WTf"]))
            emit("sync", s_wq)

            ve_x2, x2_st, ffn1_te = emit_ffn(max(ve_x1, ve_rows), wq_marks, False)

        if upto >= 3:
            def g_tr(g):
                S.wait(g, "ve", ve_x2)
                S.inc(g.iota(stage32[0:1, 0:2048], pattern=[[1, 2048]], base=0,
                             channel_multiplier=0,
                             allow_small_or_imprecise_dtypes=True), "gp", 1)
            emit("gpsimd", g_tr)
            gp_tr = S.val("gp")

            def v_dp(v):
                S.wait(v, "ve", ve_x2)
                r = S.inc(v.tensor_scalar(k1t[:, 0:1], iotp[:, 0:1], 0.015625,
                                          None, ALU.mult), "ve", 1)
                S.wait(v, "ve", r)
                r = S.inc(v.scalar_tensor_tensor(sl[:, 9:10], k1t[:, 0:1], -64.0,
                                                 iotp[:, 0:1], ALU.mult, ALU.add),
                          "ve", 1)
                S.wait(v, "ve", r)
                r = S.inc(v.tensor_scalar(sl[:, 10:11], sl[:, 9:10], 0.0, None,
                                          ALU.is_lt), "ve", 1)
                S.wait(v, "ve", r)
                S.inc(v.scalar_tensor_tensor(sl[:, 11:12], sl[:, 10:11], 64.0,
                                             sl[:, 9:10], ALU.mult, ALU.add),
                      "ve", 1)
            emit("vector", v_dp)
            ve_dp = S.val("ve")

            def s_inv(se):
                S.wait(se, "ve", ve_dp)
                S.inc(se.activation(sl[:, 12:13], sl[:, 11:12], AF.Exp,
                                    bias=cst[:, 4:5], scale=EXPSC), "se", 1)
            emit("scalar", s_inv)
            se_inv = S.val("se")

            def v_invc(v, se_inv=se_inv, tp=S.val("te")):
                S.wait(v, "se", se_inv)
                S.wait(v, "te", tp)
                S.inc(v.tensor_copy(sl[:, 13:14], sl[:, 12:13]), "ve", 1)
            emit("vector", v_invc)
            ve_invc = S.val("ve")

            def t_inv(t, ve_invc=ve_invc):
                S.wait(t, "ve", ve_invc)
                S.inc(t.matmul(PZ[0:1, 128:256], sl[:, 13:14], identf[:, :],
                               start=True, stop=True), "te", 1)
            emit("tensor", t_inv)
            te_inv = S.val("te")

            def v_invr(v, te_inv=te_inv):
                S.wait(v, "te", te_inv)
                S.inc(v.tensor_copy(stage32[0:1, 2048:2176], PZ[0:1, 128:256]),
                      "ve", 1)
            emit("vector", v_invr)
            ve_invr = S.val("ve")

            th_ve = []
            for tc in range(4):
                def t_th(t, tc=tc, ve_invr=ve_invr, gp_tr=gp_tr):
                    S.wait(t, "ve", ve_invr)
                    S.wait(t, "gp", gp_tr)
                    if tc >= 1:
                        S.wait(t, "ve", th_ve[-1])
                    S.inc(t.matmul(PG[tc], stage32[0:1, 2048:2176],
                                   stage32[0:1, tc * 512:(tc + 1) * 512],
                                   start=True, stop=True), "te", 1)
                emit("tensor", t_th)
                te_th = S.val("te")

                def v_th(v, tc=tc, te_th=te_th):
                    S.wait(v, "te", te_th)
                    S.inc(v.tensor_copy(aux32[:, tc * 512:(tc + 1) * 512],
                                        PG[tc]), "ve", 1)
                emit("vector", v_th)
                th_ve.append(S.val("ve"))
            sin_done = []
            for part in range(2):
                for ch in range(2):
                    o = ch * 1024

                    def v_u(v, part=part, ch=ch, o=o, pv=th_ve[-1],
                            sins=list(sin_done)):
                        S.wait(v, "ve", pv)
                        for m in sins:
                            S.wait(v, "se", m)
                        if part == 0:
                            r = S.inc(v.tensor_scalar(
                                stage32[:, 2048 + o:2048 + o + 1024],
                                aux32[:, o:o + 1024], 0.0, None, ALU.add), "ve", 1)
                        else:
                            r = S.inc(v.tensor_scalar(
                                stage32[:, 2048 + o:2048 + o + 1024],
                                aux32[:, o:o + 1024], PI2_HI, PI2_LO,
                                ALU.add, ALU.add), "ve", 1)
                        S.wait(v, "ve", r)
                        r = S.inc(v.tensor_scalar(
                            k1t[:, :], stage32[:, 2048 + o:2048 + o + 1024],
                            INV_2PI, None, ALU.mult), "ve", 1)
                        S.wait(v, "ve", r)
                        r = S.inc(v.scalar_tensor_tensor(
                            stage32[:, 2048 + o:2048 + o + 1024], k1t[:, :],
                            -TWO_PI_HI, stage32[:, 2048 + o:2048 + o + 1024],
                            ALU.mult, ALU.add), "ve", 1)
                        S.wait(v, "ve", r)
                        r = S.inc(v.scalar_tensor_tensor(
                            stage32[:, 2048 + o:2048 + o + 1024], k1t[:, :],
                            -TWO_PI_LO, stage32[:, 2048 + o:2048 + o + 1024],
                            ALU.mult, ALU.add), "ve", 1)
                        S.wait(v, "ve", r)
                        r = S.inc(v.tensor_scalar(
                            k1t[:, :], stage32[:, 2048 + o:2048 + o + 1024],
                            PI_F32, None, ALU.is_ge), "ve", 1)
                        S.wait(v, "ve", r)
                        r = S.inc(v.scalar_tensor_tensor(
                            stage32[:, 2048 + o:2048 + o + 1024], k1t[:, :],
                            -TWO_PI_HI, stage32[:, 2048 + o:2048 + o + 1024],
                            ALU.mult, ALU.add), "ve", 1)
                        S.wait(v, "ve", r)
                        r = S.inc(v.scalar_tensor_tensor(
                            stage32[:, 2048 + o:2048 + o + 1024], k1t[:, :],
                            -TWO_PI_LO, stage32[:, 2048 + o:2048 + o + 1024],
                            ALU.mult, ALU.add), "ve", 1)
                        S.wait(v, "ve", r)
                        r = S.inc(v.tensor_scalar(
                            k1t[:, :], stage32[:, 2048 + o:2048 + o + 1024],
                            -PI_F32, None, ALU.is_lt), "ve", 1)
                        S.wait(v, "ve", r)
                        r = S.inc(v.scalar_tensor_tensor(
                            stage32[:, 2048 + o:2048 + o + 1024], k1t[:, :],
                            TWO_PI_HI, stage32[:, 2048 + o:2048 + o + 1024],
                            ALU.mult, ALU.add), "ve", 1)
                        S.wait(v, "ve", r)
                        r = S.inc(v.scalar_tensor_tensor(
                            stage32[:, 2048 + o:2048 + o + 1024], k1t[:, :],
                            TWO_PI_LO, stage32[:, 2048 + o:2048 + o + 1024],
                            ALU.mult, ALU.add), "ve", 1)
                        S.wait(v, "ve", r)
                        S.inc(v.tensor_scalar(
                            stage32[:, 2048 + o:2048 + o + 1024],
                            stage32[:, 2048 + o:2048 + o + 1024],
                            -PI_CLAMP, PI_CLAMP, ALU.max, ALU.min), "ve", 1)
                    emit("vector", v_u)
                    ve_u = S.val("ve")

                    def s_sin(se, part=part, ch=ch, o=o, ve_u=ve_u,
                              first=(part == 0 and ch == 0)):
                        S.wait(se, "ve", ve_u)
                        if first:
                            preload_tbl(se, _TRIG)
                        S.inc(se.activation(
                            stage32[:, o:o + 1024] if part == 0 else
                            stage32[:, 2048 + o:2048 + o + 1024],
                            stage32[:, 2048 + o:2048 + o + 1024], AF.Sin,
                            bias=cst[:, 4:5], scale=1.0), "se", 1)
                    emit("scalar", s_sin)
                    sin_done.append(S.val("se"))
            def s_rest(se):
                for m in sin_done:
                    S.wait(se, "se", m)
                preload_tbl(se, _NLE)
            emit("scalar", s_rest)

            def v_ct(v, sd=sin_done[-1]):
                for m in sin_done:
                    S.wait(v, "se", m)
                S.inc(v.tensor_tensor(aux32[:, :], stage32[:, 0:2048],
                                      stage32[:, 2048:4096], ALU.add), "ve", 1)
            emit("vector", v_ct)
            ve_ct = S.val("ve")

            tr_all_b = emit_ln_to_xnt("x2sp", bigA, wnat[:, :], (otld, 2048),
                                      src_ready=x2_st, dst_free_marks=x2_st[-2:])

            kv_marks = []

            def s_kv(sy):
                S.wait(sy, "te", ffn1_te)
                S.dwait(sy, wprep_st["bWk"])
                S.dwait(sy, wprep_st["bWv"])
                kv_marks.append(S.dma(sy, "sld", out=bigB[:, 0:8192],
                                      in_=scr["WTkb"]))
                kv_marks.append(S.dma(sy, "sld", out=bigB[:, 8192:16384],
                                      in_=scr["WTvb"]))
            emit("sync", s_kv)

            brow_gld_b = load_brow("bbv", 512)

            kt_st = []
            state_b = {"sparity": 0, "oparity": 0,
                       "trip_last_exp": [0, 0], "ob_last_evac": [0, 0],
                       "et_last_read": [0, 0], "proj_evac_prev": 0}
            for g in range(KVH):
                def t_kg(t, g=g, pe=S.val("se"), pv=S.val("ve")):
                    if g == 0:
                        for mk in tr_all_b:
                            S.dwait(t, mk)
                        S.dwait(t, kv_marks[0])
                        S.wait(t, "ve", ve_x2)
                    else:
                        S.wait(t, "ve", pv)
                    last = None
                    for dc in range(ND):
                        for tc2 in range(4):
                            last = t.matmul(
                                PG[tc2],
                                bigB[:, dc * 512 + g * 128:dc * 512 + (g + 1) * 128],
                                bigA[:, dc * 2048 + tc2 * 512:dc * 2048 + (tc2 + 1) * 512],
                                start=(dc == 0), stop=(dc == ND - 1))
                    S.inc(last, "te", 1)
                emit("tensor", t_kg)
                te_kg = S.val("te")

                def s_ke(se, g=g, te_kg=te_kg):
                    S.wait(se, "te", te_kg)
                    if kt_st:
                        S.dwait(se, kt_st[-1])
                    last = None
                    for tc2 in range(4):
                        last = se.activation(KTh[:, tc2 * 512:(tc2 + 1) * 512],
                                             PG[tc2], AF.Identity,
                                             bias=bkbc[:, g:g + 1], scale=1.0)
                    S.inc(last, "se", 1)
                emit("scalar", s_ke)
                se_ke = S.val("se")

                def v_kr(v, g=g, se_ke=se_ke):
                    S.wait(v, "se", se_ke)
                    S.wait(v, "ve", ve_ct)
                    S.inc(v.tensor_tensor(KTh[:, :], KTh[:, :], aux32[:, :],
                                          ALU.mult), "ve", 1)
                emit("vector", v_kr)
                ve_kr = S.val("ve")

                def s_ks(sy, g=g, ve_kr=ve_kr):
                    S.wait(sy, "ve", ve_kr)
                    kt_st.append(S.dma(sy, "st", out=scr["KTbs"][g, :, :],
                                       in_=KTh[:, :]))
                emit("sync", s_ks)
            kt_last = S.val("ve")

            vsp_b = emit_v_all(bigA, bigB[:, 8192:16384], KVH, scr["Vspb"],
                               tr_all_b, kv_marks[1], kt_last, brow_gld_b,
                               None, None)

            oht_st = []
            for h in range(H):
                g = h // 4
                hm2 = []

                def s_wsb(sy, h=h, g=g, pe=state_b["proj_evac_prev"],
                          obm=max(state_b["ob_last_evac"])):
                    if h == 0:
                        S.dwait(sy, wprep_st["bWq"])
                        for mk in vsp_b.values():
                            S.dwait(sy, mk)
                    else:
                        S.wait(sy, "ve", max(pe, obm))
                    hm2.append(S.dma(
                        sy, "sld", out=b3(WS[0], 128),
                        in_=scr["WTqb"].rearrange("p (dc o) -> p dc o", o=D)[
                            :, :, h * 128:(h + 1) * 128]))
                    if h % 4 == 0:
                        S.dwait(sy, kt_st[g])
                        hm2.append(S.dma(sy, "sld", out=KTh[:, :],
                                         in_=scr["KTbs"][g, :, :]))
                        hm2.append(S.dma(
                            sy, "sld", out=b3(Vh, 132),
                            in_=scr["Vspb"].rearrange(
                                "t p (hh w) -> p t hh w", w=132)[:, :, g, :]))
                emit("sync", s_wsb)
                vh_m = hm2[2] if h % 4 == 0 else last_vh[0]
                kt_m = hm2[1] if h % 4 == 0 else None
                if h % 4 == 0:
                    last_vh = [hm2[2]]

                emit_projT(bigA, WS[0], QTh, bqbc[:, h:h + 1], None, None,
                           [hm2[0]], PG, state_b, rope=True, z_ve=ve_ct)
                se_q = state_b["rope_ve"]
                ve_rope = 0

                ve_head = emit_attention_head(QTh, KTh, Vh, ET, True, OSg,
                                              se_q, ve_rope, vh_m, state_b,
                                              kt_mark=kt_m)

                otr2 = []

                def s_ot2(sy, ve_head=ve_head):
                    S.wait(sy, "ve", ve_head)
                    if oht_st:
                        S.dwait(sy, oht_st[-1])
                    otr2.append(S.dma(sy, "tr", transpose_call=True,
                                      out=b3(OTt, 128), in_=OSg[:, :]))
                emit("sync", s_ot2)

                def q_ot2(sq, h=h):
                    S.dwait(sq, otr2[0])
                    oht_st.append(S.dma(sq, "st", out=scr["OTsc"][h, :, :],
                                        in_=OTt[:, :]))
                emit("scalar", q_ot2)

            brow_gld2 = load_brow("bbo")
            ve_x3, oproj_b_te = emit_oproj("WTob", wprep_st["bWo"], oht_st[-1],
                                           brow_gld2, False, "x2sp", xnt, bigA,
                                           False)

        if upto >= 4:
            ve_rows2 = emit_rows_and_wsb(ve_x3)
            wq2_marks = []

            def s_wq2(sy):
                S.wait(sy, "te", oproj_b_te)
                wq2_marks.append(S.dma(sy, "sld", out=bigB[:, :], in_=scr["WTf"]))
            emit("sync", s_wq2)

            ve_x4, out_st4, _ = emit_ffn(max(ve_x3, ve_rows2), wq2_marks, True)

        def emit_debug_out(src_big):
            dbg_st = []
            for tt in range(NT):
                half = tt % 2

                def s_cp(se, tt=tt, half=half):
                    S.wait(se, "ve")
                    if tt >= 2:
                        S.dwait(se, dbg_st[tt - 2])
                    S.inc(se.activation(
                        stage32[:, half * 2048:(half + 1) * 2048],
                        src_big[:, tt * 2048:(tt + 1) * 2048],
                        AF.Identity, bias=cst[:, 4:5], scale=1.0), "se", 1)
                emit("scalar", s_cp)
                se_c = S.val("se")

                def s_st(sy, tt=tt, half=half, se_c=se_c):
                    S.wait(sy, "se", se_c)
                    dbg_st.append(S.dma(
                        sy, "st", out=out_t[tt * 128:(tt + 1) * 128, :],
                        in_=stage32[:, half * 2048:(half + 1) * 2048]))
                emit("sync", s_st)

        if upto in (1, 2, 3):
            emit_debug_out(bigA)

        def fin(sy):
            for nm in ["st", "gld", "sld", "tr", "glx", "grs"]:
                S.dwait_all(sy, nm)
            S.wait(sy, "ve")
            S.wait(sy, "se")
            S.wait(sy, "te")
        emit("sync", fin)

    return nc


def make_inputs_per_core(inputs, core):
    x = np.ascontiguousarray(inputs["L0_attn_a_x"][core])
    dm = np.tril(np.ones((128, 128), dtype=np.float32))
    m = {
        "x": x, "dmaski": dm,
        "aWq": inputs["L0_attn_a_Wq"], "aWk": inputs["L0_attn_a_Wk"],
        "aWv": inputs["L0_attn_a_Wv"], "aWo": inputs["L0_attn_a_Wo"],
        "abq": inputs["L0_attn_a_bq"], "abk": inputs["L0_attn_a_bk"],
        "abv": inputs["L0_attn_a_bv"], "abo": inputs["L0_attn_a_bo"],
        "aBq": inputs["L0_attn_a_Bq"], "aBk": inputs["L0_attn_a_Bk"],
        "aBv": inputs["L0_attn_a_Bv"], "aBo": inputs["L0_attn_a_Bo"],
        "aAq": inputs["L0_attn_a_Aq"], "aAk": inputs["L0_attn_a_Ak"],
        "aAv": inputs["L0_attn_a_Av"], "aAo": inputs["L0_attn_a_Ao"],
        "bWq": inputs["L0_attn_b_Wq"], "bWk": inputs["L0_attn_b_Wk"],
        "bWv": inputs["L0_attn_b_Wv"], "bWo": inputs["L0_attn_b_Wo"],
        "bbq": inputs["L0_attn_b_bq"], "bbk": inputs["L0_attn_b_bk"],
        "bbv": inputs["L0_attn_b_bv"], "bbo": inputs["L0_attn_b_bo"],
        "fW": inputs["L0_ffn_weight"], "fb": inputs["L0_ffn_bias"],
    }
    return {k: np.ascontiguousarray(v) for k, v in m.items()}


def kernel(**inputs):
    nc = build_nc(upto=4)
    in_maps = [make_inputs_per_core(inputs, c) for c in range(8)]
    res = run_bass_kernel_spmd(nc, in_maps, core_ids=list(range(8)))
    return np.stack([res.results[c]["out"] for c in range(8)], axis=0)


# version 05
# speedup vs baseline: 1.1646x; 1.0220x over previous
import numpy as np
import ml_dtypes
import concourse.bass as bass
import concourse.mybir as mybir
from concourse.bass_utils import run_bass_kernel_spmd
from contextlib import ExitStack

F32 = mybir.dt.float32
BF16 = mybir.dt.bfloat16
I32 = mybir.dt.int32
AF = mybir.ActivationFunctionType
ALU = mybir.AluOpType
AXX = mybir.AxisListType.X

L, D, NT, ND = 2048, 2048, 16, 16
H, HD, KVH = 16, 128, 4
ISQ = float(128.0 ** -0.5)
ESH = -12.0
EPS = 1e-5
INV_D = 1.0 / 2048.0
TWO_PI_HI = float(np.float32(2 * np.pi))
TWO_PI_LO = float(np.float32(2 * np.pi - np.float64(np.float32(2 * np.pi))))
PI2_HI = float(np.float32(np.pi / 2))
PI2_LO = float(np.float32(np.pi / 2 - np.float64(np.float32(np.pi / 2))))
PI_CLAMP = 3.1415925
PI_F32 = float(np.float32(np.pi))
INV_2PI = float(np.float32(1.0 / (2 * np.pi)))
EXPSC = float(np.float32(-2.0 * np.log(10000.0) / 128.0))


class Sems:

    def __init__(self):
        self.h, self.c = {}, {}
        self.dh, self.dc, self.di = {}, {}, {}

    def add(self, name, handle):
        self.h[name], self.c[name] = handle, 0

    def add_dma(self, name, h0, h1):
        self.dh[name] = [h0, h1]
        self.dc[name] = [0, 0]
        self.di[name] = 0

    def inc(self, inst, name, n=1):
        self.c[name] += n
        inst.then_inc(self.h[name], n)
        return self.c[name]

    def wait(self, eng, name, value=None):
        v = self.c[name] if value is None else value
        if v > 0:
            eng.wait_ge(self.h[name], v)

    def val(self, name):
        return self.c[name]

    def dma(self, eng, name, **kw):
        p = self.di[name] % 2
        self.di[name] += 1
        if self.dc[name][p] > 0:
            eng.wait_ge(self.dh[name][p], self.dc[name][p])
        tr = kw.pop("transpose_call", False)
        if tr:
            inst = eng.dma_start_transpose(kw["out"], kw["in_"])
        else:
            inst = eng.dma_start(**kw)
        self.dc[name][p] += 16
        inst.then_inc(self.dh[name][p], 16)
        return (name, p, self.dc[name][p])

    def dwait(self, eng, mark):
        name, p, v = mark
        eng.wait_ge(self.dh[name][p], v)

    def dwait_all(self, eng, name):
        for p in (0, 1):
            if self.dc[name][p] > 0:
                eng.wait_ge(self.dh[name][p], self.dc[name][p])


def build_nc(upto=4):
    nc = bass.Bass("TRN2", target_bir_lowering=False)

    x_in = nc.dram_tensor("x", [L, D], F32, kind="ExternalInput").ap()
    out_t = nc.dram_tensor("out", [L, D], F32, kind="ExternalOutput").ap()

    wts = {}
    for nm, shp in [
        ("aWq", [D, D]), ("aWk", [D, D]), ("aWv", [D, D]), ("aWo", [D, D]),
        ("abq", [D]), ("abk", [D]), ("abv", [D]), ("abo", [D]),
        ("aBq", [16, D]), ("aBk", [16, D]), ("aBv", [16, D]), ("aBo", [16, D]),
        ("aAq", [D, 16]), ("aAk", [D, 16]), ("aAv", [D, 16]), ("aAo", [D, 16]),
        ("bWq", [D, D]), ("bWk", [512, D]), ("bWv", [512, D]), ("bWo", [D, D]),
        ("bbq", [D]), ("bbk", [512]), ("bbv", [512]), ("bbo", [D]),
        ("fW", [D, D]), ("fb", [D]),
    ]:
        wts[nm] = nc.dram_tensor(nm, shp, F32, kind="ExternalInput").ap()
    dmask_in = nc.dram_tensor("dmaski", [128, 128], F32, kind="ExternalInput").ap()

    scr = {}
    for nm, shp, dt in [
        ("WTqa", [16, 128, D], BF16), ("WTka", [16, 128, D], BF16),
        ("WTva", [16, 128, D], BF16), ("WToa", [16, 128, D], BF16),
        ("WTqb", [16, 128, D], BF16), ("WTob", [16, 128, D], BF16),
        ("WTkb", [4, 128, D], BF16), ("WTvb", [4, 128, D], BF16),
        ("WTf", [16, 128, D], BF16),
        ("ATsc", [16, 4 * D], BF16),
        ("Vspa", [NT, 128, H * 132], BF16),
        ("Vspb", [NT, 128, KVH * 132], BF16),
        ("KTbs", [KVH, 128, L], BF16),
        ("OTsc", [H, 128, L], BF16),
        ("x2sp", [NT, 128, D], BF16),
    ]:
        scr[nm] = nc.dram_tensor(nm, shp, dt, kind="Internal").ap()

    ctx = ExitStack()
    with ctx:
        def sbt(name, shape, dtype):
            return ctx.enter_context(nc.sbuf_tensor(name, shape, dtype))

        bigA = sbt("bigA", [128, 32768], BF16)
        bigB = sbt("bigB", [128, 32768], BF16)
        stage32 = sbt("stage32", [128, 4096], F32)
        xnt = sbt("xnt", [128, 4096], BF16)
        vstg = sbt("vstg", [128, 1056], BF16)
        otld = sbt("otld", [128, 4096], BF16)
        k1t = sbt("k1t", [128, 1024], I32)
        aux32 = sbt("aux32", [128, 2048], F32)
        zq = sbt("zq", [16, 2048], BF16)
        zk = sbt("zk", [16, 2048], BF16)
        zv = sbt("zv", [16, 2048], BF16)
        zob = sbt("zob", [16, 256], BF16)
        BTq = sbt("BTq", [128, ND * 16], BF16)
        BTk = sbt("BTk", [128, ND * 16], BF16)
        BTv = sbt("BTv", [128, ND * 16], BF16)
        BTo = sbt("BTo", [128, ND * 16], BF16)
        ATs = sbt("ATs", [16, 2048], BF16)
        rowsB = sbt("rowsB", [3, 2048], BF16)
        rowsA = sbt("rowsA", [3, 128], BF16)
        bcolq = sbt("bcolq", [128, 16], F32)
        bcolk = sbt("bcolk", [128, 16], F32)
        bqbc = sbt("bqbc", [128, 16], F32)
        bkbc = sbt("bkbc", [128, 4], F32)
        fbcol = sbt("fbcol", [128, 16], F32)
        wscol = sbt("wscol", [128, 16], F32)
        wsumc = sbt("wsumc", [128, 16], F32)
        betac = sbt("betac", [128, 16], F32)
        c2hc = sbt("c2hc", [128, 16], BF16)
        c2lc = sbt("c2lc", [128, 16], BF16)
        alc = sbt("alc", [128, 4], BF16)
        bstg = sbt("bstg", [16, 128], F32)
        sl = sbt("sl", [128, 16], F32)
        cst = sbt("cst", [128, 8], F32)
        dmask = sbt("dmask", [128, 128], BF16)
        identb = sbt("identb", [128, 128], BF16)
        identf = sbt("identf", [128, 128], F32)
        onesf = sbt("onesf", [1, 128], F32)
        onesr = sbt("onesr", [1, 128], BF16)
        iotp = sbt("iotp", [128, 1], F32)
        invd = sbt("invd", [128, 4], F32)
        wnat = sbt("wnat", [128, 2048], BF16)
        wtt = sbt("wtt", [128, 2048], BF16)
        Bnat = otld[0:16, 0:2048]
        brow = rowsB[0:1, :]

        ps_a = ctx.enter_context(nc.psum_tensor("ps_a", [128, 1536], F32))
        ps_b = ctx.enter_context(nc.psum_tensor("ps_b", [128, 1536], F32))
        ps_oa = ctx.enter_context(nc.psum_tensor("ps_oa", [128, 512], F32))
        ps_ob = ctx.enter_context(nc.psum_tensor("ps_ob", [128, 512], F32))
        PG = [ps_a[:, 0:512], ps_a[:, 512:1024], ps_a[:, 1024:1536], ps_b[:, 0:512]]
        KGB = [ps_b[:, 512:1024], ps_b[:, 1024:1536], ps_oa[:, 0:512], ps_ob[:, 0:512]]
        PZ = ps_b[:, 512:1024]
        STRIPLE = [ps_a, ps_b]

        S = Sems()
        for snm in ["te", "ve", "se", "gp"]:
            S.add(snm, ctx.enter_context(nc.semaphore(snm)))
        for snm in ["gld", "sld", "tr", "st", "glx", "grs"]:
            S.add_dma(snm, ctx.enter_context(nc.semaphore(snm + "0")),
                      ctx.enter_context(nc.semaphore(snm + "1")))
        block = ctx.enter_context(nc.Block())

        ET = [bigB[:, 0:8192], bigB[:, 8192:16384]]
        QTh = bigB[:, 16384:18432]
        KTh = bigB[:, 18432:20480]
        Vh = bigB[:, 20480:22592]
        OSg = bigB[:, 22592:24640]
        OTt = bigB[:, 24640:26688]
        WS = [bigB[:, 26688:28736], bigB[:, 28736:30784]]

        def b3(ap, w):
            return ap.rearrange("p (n w) -> p n w", w=w)

        def emit(engname, fn):
            getattr(block, engname)(fn)

        from concourse.hw_specs import get_activation_tables
        _tabs = list(get_activation_tables(nc.m.arch).keys())
        _NLE = _tabs.index("natural_log_exp_and_others")
        _TRIG = _tabs.index("trig_and_small")

        def preload_tbl(se, idx):
            se.add_instruction(mybir.InstLoadActFuncSet(
                name=nc.get_next_instruction_name(),
                act_func_set_id=idx, ins=[], outs=[]))

        def p0_se(se):
            preload_tbl(se, _NLE)
        emit("scalar", p0_se)

        p0_marks = []

        def p0a_g(g):
            g.iota(iotp[:, :], pattern=[[0, 1]], base=0, channel_multiplier=1,
                   allow_small_or_imprecise_dtypes=True)
            S.inc(g.iota(stage32[:, 0:128], pattern=[[1, 128]], base=0,
                         channel_multiplier=0,
                         allow_small_or_imprecise_dtypes=True), "gp", 1)
            p0_marks.append(S.dma(g, "gld", out=dmask[:, :], in_=dmask_in))
        emit("gpsimd", p0a_g)

        def p0a_v(v):
            v.memset(cst[:, 0:1], ESH)
            v.memset(cst[:, 1:2], 128.0)
            v.memset(cst[:, 2:3], EPS)
            v.memset(cst[:, 3:4], -127.5)
            v.memset(cst[:, 4:5], 0.0)
            v.memset(onesr[0:1, :], 1.0)
            v.memset(rowsA[0:3, :], 1.0)
            v.memset(vstg[:, :], 1.0)
            for mk in p0_marks:
                S.dwait(v, mk)
            S.wait(v, "gp", 1)
            v.memset(onesf[0:1, :], 1.0)
            v.tensor_scalar(identf[:, :], stage32[:, 0:128], iotp[:, 0:1],
                            None, ALU.is_equal)
            S.inc(v.tensor_scalar(identb[:, :], stage32[:, 0:128], iotp[:, 0:1],
                                  None, ALU.is_equal), "ve", 1)
        emit("vector", p0a_v)
        ve_consts = S.val("ve")

        for dst, src, w in [(bcolq, wts["abq"], 16), (bcolk, wts["abk"], 16),
                            (bqbc, wts["bbq"], 16), (bkbc, wts["bbk"], 4),
                            (fbcol, wts["fb"], 16)]:
            bm = []

            def g_bc(g, src=src, w=w, pv=S.val("ve")):
                if pv > ve_consts:
                    S.wait(g, "ve", pv)
                bm.append(S.dma(g, "gld", out=bstg[0:w, :],
                                in_=src.rearrange("(a b) -> a b", b=128)))
            emit("gpsimd", g_bc)

            def t_bc(t, w=w):
                S.dwait(t, bm[0])
                S.wait(t, "ve", ve_consts)
                S.inc(t.matmul(PZ[:, 0:16][:, 0:w] if False else PZ[:, 0:w],
                               bstg[0:w, :], identf[0:w, 0:w],
                               start=True, stop=True), "te", 1)
            emit("tensor", t_bc)
            te_bc = S.val("te")

            def v_bc(v, dst=dst, w=w, te_bc=te_bc):
                S.wait(v, "te", te_bc)
                S.inc(v.tensor_copy(dst[:, 0:w], PZ[:, 0:w]), "ve", 1)
            emit("vector", v_bc)
        ve_bcols = S.val("ve")

        at_store_marks = []
        for i, nm in enumerate(["aAq", "aAk", "aAv", "aAo"]):
            ga_mark = []

            def g_a(g, i=i, nm=nm):
                if i > 0:
                    S.dwait(g, at_store_marks[-1])
                ga_mark.append(S.dma(
                    g, "gld",
                    out=b3(wnat[:, 0:256], 16),
                    in_=wts[nm].rearrange("(ot p) r -> p ot r", p=128)))
            emit("gpsimd", g_a)

            for ot in range(16):
                def t_a(t, ot=ot, vp=S.val("ve")):
                    if ot == 0:
                        S.dwait(t, ga_mark[0])
                        S.wait(t, "ve", ve_bcols)
                    else:
                        S.wait(t, "ve", vp)
                    S.inc(t.matmul(PZ[0:16, 0:128],
                                   wnat[:, ot * 16:(ot + 1) * 16],
                                   identb[:, :], start=True, stop=True), "te", 1)
                emit("tensor", t_a)
                te_a = S.val("te")

                def v_a(v, ot=ot, te_a=te_a):
                    S.wait(v, "te", te_a)
                    S.inc(v.tensor_scalar(ATs[:, ot * 128:(ot + 1) * 128],
                                          PZ[0:16, 0:128], 0.5, None, ALU.mult),
                          "ve", 1)
                emit("vector", v_a)
            ve_a = S.val("ve")

            def s_at(sy, i=i, ve_a=ve_a):
                S.wait(sy, "ve", ve_a)
                at_store_marks.append(S.dma(
                    sy, "st", out=scr["ATsc"][:, i * D:(i + 1) * D],
                    in_=ATs[:, :]))
            emit("sync", s_at)
        at_done = list(at_store_marks)

        bld_marks, btr_marks = [], []
        for i, bt in enumerate([BTq, BTk, BTv, BTo]):
            def g_b(g, i=i):
                if i > 0:
                    S.dwait(g, btr_marks[-1])
                bld_marks.append(S.dma(
                    g, "gld", out=Bnat,
                    in_=wts[["aBq", "aBk", "aBv", "aBo"][i]]))
            emit("gpsimd", g_b)

            def s_b(sy, i=i, bt=bt):
                S.dwait(sy, bld_marks[-1])
                btr_marks.append(S.dma(sy, "tr", transpose_call=True,
                                       out=b3(bt, 16), in_=Bnat))
            emit("sync", s_b)
        bt_done = list(btr_marks)

        pw_ld, pw_tr, pw_st = [], [], []
        wprep_st = {}

        def prep_weight(wname, dramT, nrows):
            nt = nrows // 128
            for i in range(nt):
                def g_fn(g, i=i, wname=wname):
                    if pw_tr:
                        S.dwait(g, pw_tr[-1])
                    pw_ld.append(S.dma(g, "gld", out=wnat[:, :],
                                       in_=wts[wname][i * 128:(i + 1) * 128, :]))
                emit("gpsimd", g_fn)

                def s_fn(sq, i=i):
                    S.dwait(sq, pw_ld[-1])
                    if pw_st:
                        S.dwait(sq, pw_st[-1])
                    pw_tr.append(S.dma(sq, "tr", transpose_call=True,
                                       out=b3(wtt, 128), in_=wnat[:, :]))
                emit("scalar", s_fn)

                def q_fn(sq, i=i, dramT=dramT):
                    S.dwait(sq, pw_tr[-1])
                    pw_st.append(S.dma(sq, "st", out=dramT[i, :, :],
                                       in_=wtt[:, :]))
                emit("scalar", q_fn)
            wprep_st[wname] = pw_st[-1]

        prep_weight("aWv", scr["WTva"], D)
        prep_weight("aWq", scr["WTqa"], D)
        prep_weight("aWk", scr["WTka"], D)
        prep_weight("aWo", scr["WToa"], D)
        if upto >= 3:
            prep_weight("bWk", scr["WTkb"], 512)
            prep_weight("bWv", scr["WTvb"], 512)
            prep_weight("bWq", scr["WTqb"], D)
            prep_weight("bWo", scr["WTob"], D)

        def emit_ln_tile(src_ap, dst_ap, sq_ap, pre_mark=None, dst_free=None,
                         pre_ve=None, dst_free_ve=None):
            def v1(v):
                if pre_mark:
                    for mk in pre_mark:
                        S.dwait(v, mk)
                if pre_ve:
                    S.wait(v, "ve", pre_ve)
                rs = S.inc(v.reduce_sum(sl[:, 0:1], src_ap, axis=AXX), "ve", 1)
                S.wait(v, "ve", rs)
                rs = S.inc(v.scalar_tensor_tensor(sq_ap, src_ap, 1.0, src_ap,
                                                  ALU.mult, ALU.mult,
                                                  accum_out=sl[:, 1:2]), "ve", 1)
                S.wait(v, "ve", rs)
                rs = S.inc(v.tensor_scalar(sl[:, 2:3], sl[:, 0:1], INV_D, None,
                                           ALU.mult), "ve", 1)
                S.wait(v, "ve", rs)
                rs = S.inc(v.tensor_tensor(sl[:, 4:5], sl[:, 2:3], sl[:, 2:3],
                                           ALU.mult), "ve", 1)
                S.wait(v, "ve", rs)
                S.inc(v.scalar_tensor_tensor(sl[:, 5:6], sl[:, 1:2], INV_D,
                                             sl[:, 4:5], ALU.mult, ALU.subtract),
                      "ve", 1)
            emit("vector", v1)
            ve_var = S.val("ve")

            def s2(se):
                S.wait(se, "ve", ve_var)
                r = S.inc(se.activation(sl[:, 6:7], sl[:, 5:6], AF.Ln,
                                        bias=cst[:, 2:3], scale=1.0), "se", 1)
                S.wait(se, "se", r)
                S.inc(se.activation(sl[:, 7:8], sl[:, 6:7], AF.Exp,
                                    bias=cst[:, 4:5], scale=-0.5), "se", 1)
            emit("scalar", s2)
            se_sd = S.val("se")

            def v3(v):
                S.wait(v, "se", se_sd)
                S.inc(v.tensor_scalar(sl[:, 8:9], sl[:, 2:3], sl[:, 7:8], -1.0,
                                      ALU.mult, ALU.mult), "ve", 1)
            emit("vector", v3)
            ve_nm = S.val("ve")

            def s3(se):
                S.wait(se, "ve", ve_nm)
                if dst_free is not None:
                    S.dwait(se, dst_free)
                if dst_free_ve:
                    S.wait(se, "ve", dst_free_ve)
                S.inc(se.activation(dst_ap, src_ap, AF.Identity,
                                    bias=sl[:, 8:9], scale=sl[:, 7:8]), "se", 1)
            emit("scalar", s3)
            return S.val("se")

        def emit_ln_to_xnt(src_kind, xnt_dst, sq_ap, ld_halves,
                           src_ready=None, sq_free=None, dst_free_marks=None):
            prev_se = [0, 0]
            tr_marks = []
            for tt in range(NT):
                half = tt % 2
                ldt, ldw = ld_halves
                ld_buf = ldt[:, half * ldw:(half + 1) * ldw]
                src = (x_in[tt * 128:(tt + 1) * 128, :] if src_kind == "x0"
                       else scr["x2sp"][tt, :, :])

                lm = []

                def g_ld(g, ld_buf=ld_buf, src=src, tt=tt, pse=prev_se[half]):
                    if tt < 2:
                        S.wait(g, "ve", ve_consts)
                    else:
                        S.wait(g, "se", pse)
                    if src_ready is not None:
                        S.dwait(g, src_ready[tt])
                    lm.append(S.dma(g, "glx", out=ld_buf, in_=src))
                emit("gpsimd", g_ld)

                se_done = emit_ln_tile(
                    ld_buf, xnt[:, half * 2048:(half + 1) * 2048], sq_ap,
                    pre_mark=(lm + (sq_free or []) if tt < 2 else lm),
                    dst_free=(tr_marks[tt - 2] if tt >= 2 else None))
                prev_se[half] = se_done

                def s_tr(sy, tt=tt, half=half, se_done=se_done):
                    if tt == 0 and dst_free_marks:
                        for mk in dst_free_marks:
                            S.dwait(sy, mk)
                    S.wait(sy, "se", se_done)
                    tr_marks.append(S.dma(
                        sy, "tr", transpose_call=True,
                        out=b3(xnt_dst, 2048)[:, :, tt * 128:(tt + 1) * 128],
                        in_=xnt[:, half * 2048:(half + 1) * 2048]))
                emit("sync", s_tr)
            return tr_marks[-2:]

        def emit_z_all(xnT, tr_xnt_all):
            for i, (bt, zdst) in enumerate(zip([BTq, BTk, BTv], [zq, zk, zv])):
                ve_prev = S.val("ve")

                def t_fn(t, i=i, bt=bt, ve_prev=ve_prev):
                    if i == 0:
                        for mk in tr_xnt_all:
                            S.dwait(t, mk)
                        for mk in bt_done:
                            S.dwait(t, mk)
                        S.wait(t, "ve", ve_consts)
                    else:
                        S.wait(t, "ve", ve_prev)
                    last = None
                    for dc in range(ND):
                        for tc in range(4):
                            last = t.matmul(
                                PG[tc][0:16, :], b3(bt, 16)[:, dc, :],
                                xnT[:, dc * 2048 + tc * 512:dc * 2048 + (tc + 1) * 512],
                                start=(dc == 0), stop=(dc == ND - 1))
                    S.inc(last, "te", 1)
                emit("tensor", t_fn)
                te_z = S.val("te")

                def v_fn(v, zdst=zdst, te_z=te_z):
                    S.wait(v, "te", te_z)
                    last = None
                    for tc in range(4):
                        last = v.tensor_copy(zdst[:, tc * 512:(tc + 1) * 512],
                                             PG[tc][0:16, :])
                    S.inc(last, "ve", 1)
                emit("vector", v_fn)
            return S.val("ve")

        def emit_v_all(xnT, wvT_sb, nheads, vsp, tr_xnt_all, wv_ld_mark,
                       first_ve_wait, brow_gld, zvbuf, av_ready_sld):
            noc = (nheads * 128) // 512
            vst_marks = {}
            vhalf_free = {}
            for tt in range(NT):
                def t_fn(t, tt=tt):
                    if tt == 0:
                        for mk in tr_xnt_all:
                            S.dwait(t, mk)
                        S.dwait(t, wv_ld_mark)
                        S.wait(t, "ve", first_ve_wait)
                        S.dwait(t, brow_gld)
                        if av_ready_sld:
                            S.dwait(t, av_ready_sld)
                    for oc in range(noc):
                        gidx = tt * noc + oc
                        if gidx >= 4:
                            S.wait(t, "ve", vhalf_free[gidx - 4])
                    last = None
                    wv3 = b3(wvT_sb, 2048)
                    for dc in range(ND):
                        for oc in range(noc):
                            last = t.matmul(
                                PG[(tt * noc + oc) % 4],
                                b3(xnT, 2048)[:, dc, tt * 128:(tt + 1) * 128],
                                wv3[:, 4 * oc:4 * oc + 4,
                                    dc * 128:(dc + 1) * 128],
                                start=(dc == 0), stop=False)
                    for oc in range(noc):
                        gidx = tt * noc + oc
                        if zvbuf is not None:
                            t.matmul(PG[gidx % 4],
                                     zvbuf[0:16, tt * 128:(tt + 1) * 128],
                                     ATs[0:16, oc * 512:(oc + 1) * 512],
                                     start=False, stop=False)
                        last = t.matmul(PG[gidx % 4], onesr[0:1, :],
                                        brow[0:1, oc * 512:(oc + 1) * 512],
                                        start=False, stop=True)
                        S.inc(last, "te", 1)
                emit("tensor", t_fn)
                te_last = S.val("te")

                for oc in range(noc):
                    gidx = tt * noc + oc
                    half = gidx % 2
                    te_oc = te_last - (noc - 1 - oc)

                    def v_fn(v, tt=tt, oc=oc, gidx=gidx, half=half, te_oc=te_oc):
                        S.wait(v, "te", te_oc)
                        if gidx >= 2:
                            S.dwait(v, vst_marks[gidx - 2])
                        S.inc(v.tensor_copy(
                            b3(vstg[:, half * 528:(half + 1) * 528], 132)[:, :, 0:128],
                            b3(PG[gidx % 4], 128)), "ve", 1)
                    emit("vector", v_fn)
                    vhalf_free[gidx] = S.val("ve")

                    def s_sp(sy, tt=tt, oc=oc, half=half, vm=vhalf_free[gidx],
                             gidx2=gidx):
                        S.wait(sy, "ve", vm)
                        vst_marks[gidx2] = S.dma(
                            sy, "st",
                            out=vsp[tt, :, oc * 528:(oc + 1) * 528],
                            in_=vstg[:, half * 528:(half + 1) * 528])
                    emit("sync", s_sp)
            return vst_marks

        def emit_projT(xnT, ws_buf, dst, bias_col_ap, lora_at, zbuf,
                       sld_mark, groups, state, rope=False, z_ve=0):
            def t_fn(t):
                for mk in sld_mark:
                    S.dwait(t, mk)
                if z_ve:
                    S.wait(t, "ve", z_ve)
                S.wait(t, "ve", state["proj_evac_prev"])
                S.wait(t, "se", max(state["trip_last_exp"]))
                S.wait(t, "ve", max(state["ob_last_evac"]))
                last = None
                for dc in range(ND):
                    for tc in range(4):
                        last = t.matmul(
                            groups[tc], b3(ws_buf, 128)[:, dc, :],
                            xnT[:, dc * 2048 + tc * 512:dc * 2048 + (tc + 1) * 512],
                            start=(dc == 0),
                            stop=(dc == ND - 1 and lora_at is None))
                if lora_at is not None:
                    for tc in range(4):
                        last = t.matmul(
                            groups[tc], ATs[0:16, lora_at:lora_at + 128],
                            zbuf[0:16, tc * 512:(tc + 1) * 512],
                            start=False, stop=True)
                S.inc(last, "te", 1)
            emit("tensor", t_fn)
            te_p = S.val("te")

            def v_fn(v):
                S.wait(v, "te", te_p)
                last = None
                for tc in range(4):
                    last = v.tensor_scalar(dst[:, tc * 512:(tc + 1) * 512],
                                           groups[tc], bias_col_ap, None,
                                           ALU.add)
                S.inc(last, "ve", 1)
                if rope:
                    S.wait(v, "ve", S.val("ve"))
                    S.inc(v.tensor_tensor(dst[:, :], dst[:, :], aux32[:, :],
                                          ALU.mult), "ve", 1)
            emit("vector", v_fn)
            state["proj_evac_prev"] = S.val("ve")
            if rope:
                state["rope_ve"] = S.val("ve")
            return S.val("ve")

        def emit_attention_head(qT, kT, vhbuf, etbufs, causal, osg,
                                se_ready, ve_ready, vh_mark, state,
                                kt_mark=None):
            exp_qc, mask_qc = {}, {}

            def emit_s(qc):
                kt_hi = (qc * 4 + 4) if causal else 16
                ngrp = (kt_hi + 2) // 3
                exp_marks = []
                for ktg in range(ngrp):
                    kts = list(range(3 * ktg, min(3 * ktg + 3, kt_hi)))
                    gsz = len(kts)
                    par = state["sparity"] % 2
                    state["sparity"] += 1
                    trip = STRIPLE[par]
                    need_exp = state["trip_last_exp"][par]

                    def t_s(t, kts=kts, trip=trip, need_exp=need_exp, qc=qc,
                            first=(ktg == 0 and qc == 0)):
                        if first:
                            S.wait(t, "ve", se_ready)
                            if ve_ready:
                                S.wait(t, "ve", ve_ready)
                            if kt_mark is not None:
                                S.dwait(t, kt_mark)
                        S.wait(t, "se", need_exp)
                        inst = None
                        for j, kt in enumerate(kts):
                            inst = t.matmul(
                                trip[:, j * 512:(j + 1) * 512],
                                kT[:, kt * 128:(kt + 1) * 128],
                                qT[:, qc * 512:(qc + 1) * 512],
                                start=True, stop=True)
                        S.inc(inst, "te", 1)
                    emit("tensor", t_s)
                    te_s = S.val("te")

                    etdst = etbufs[qc % 2]
                    oread = state["et_last_read"][qc % 2]

                    def s_e(se, te_s=te_s, trip=trip, gsz=gsz, etdst=etdst,
                            ktg=ktg, oread=oread):
                        S.wait(se, "te", te_s)
                        S.wait(se, "te", oread)
                        S.inc(se.activation(
                            etdst[:, (3 * ktg) * 512:(3 * ktg + gsz) * 512],
                            trip[:, 0:gsz * 512], AF.Exp,
                            bias=cst[:, 0:1], scale=ISQ), "se", 1)
                    emit("scalar", s_e)
                    exp_marks.append(S.val("se"))
                    state["trip_last_exp"][par] = S.val("se")
                exp_qc[qc] = exp_marks

                if causal:
                    mm = {}
                    for qt_l in range(4):
                        qt = qc * 4 + qt_l
                        kt = qt
                        gi = kt // 3
                        off = kt * 512 + qt_l * 128

                        def v_m(v, off=off, qc=qc, em=exp_marks[gi]):
                            S.wait(v, "se", em)
                            S.inc(v.tensor_tensor(
                                etbufs[qc % 2][:, off:off + 128],
                                etbufs[qc % 2][:, off:off + 128],
                                dmask[:, :], ALU.mult), "ve", 1)
                        emit("vector", v_m)
                        mm[qt_l] = S.val("ve")
                    mask_qc[qc] = mm

            def emit_o(qc):
                for qt_l in range(4):
                    qt = qc * 4 + qt_l
                    kts = list(range(qt + 1)) if causal else list(range(16))
                    opar = state["oparity"] % 2
                    state["oparity"] += 1
                    ob = [ps_oa, ps_ob][opar]
                    ev_prev = state["ob_last_evac"][opar]
                    mm = mask_qc.get(qc, {}).get(qt_l, 0)

                    def t_o(t, qt_l=qt_l, kts=kts, ob=ob, qc=qc, ev_prev=ev_prev,
                            em=exp_qc[qc][-1], mm=mm,
                            first=(qc == 0 and qt_l == 0)):
                        if first and vh_mark is not None:
                            S.dwait(t, vh_mark)
                        S.wait(t, "se", em)
                        if mm:
                            S.wait(t, "ve", mm)
                        S.wait(t, "ve", ev_prev)
                        inst = None
                        for idx, kt in enumerate(kts):
                            inst = t.matmul(
                                ob[:, 0:129],
                                etbufs[qc % 2][:, kt * 512 + qt_l * 128:
                                               kt * 512 + qt_l * 128 + 128],
                                b3(vhbuf, 132)[:, kt, 0:129],
                                start=(idx == 0), stop=(idx == len(kts) - 1))
                        S.inc(inst, "te", 1)
                    emit("tensor", t_o)
                    te_o = S.val("te")
                    state["et_last_read"][qc % 2] = te_o

                    def v_o(v, qt=qt, ob=ob, te_o=te_o):
                        S.wait(v, "te", te_o)
                        rc = S.inc(v.reciprocal(invd[:, qt % 4:qt % 4 + 1],
                                                ob[:, 128:129]), "ve", 1)
                        S.wait(v, "ve", rc)
                        S.inc(v.tensor_scalar(
                            osg[:, qt * 128:(qt + 1) * 128], ob[:, 0:128],
                            invd[:, qt % 4:qt % 4 + 1], None, ALU.mult), "ve", 1)
                    emit("vector", v_o)
                    state["ob_last_evac"][opar] = S.val("ve")

            for qc in range(4):
                emit_s(qc)
                if qc >= 1:
                    emit_o(qc - 1)
            emit_o(3)
            return S.val("ve")

        def emit_oproj(wtname, wt_st_mark, attn_done_st, bias_gld, lora,
                       resid_kind, resid_buf, dst_bigA, write_x2sp):
            pre_se = S.val("se")
            pre_trm = [("tr", p, S.dc["tr"][p]) for p in (0, 1)
                       if S.dc["tr"][p] > 0]
            wo_marks = []

            def s_wo(sy):
                S.dwait(sy, wt_st_mark)
                S.dwait(sy, attn_done_st)
                wo_marks.append(S.dma(sy, "sld", out=bigB[:, :],
                                      in_=scr[wtname].rearrange("t p w -> p t w")))
                if lora:
                    S.dwait(sy, at_done[3])
                    wo_marks.append(S.dma(sy, "sld", out=ATs[:, :],
                                          in_=scr["ATsc"][:, 3 * D:4 * D]))
            emit("sync", s_wo)

            zo_ve = {}
            prev_otld_reader = [0, 0]
            prev_y_evac = [0]
            for tt in range(NT):
                half = tt % 2

                otm_l = []

                def s_otl(sy, tt=tt, half=half, pe=prev_otld_reader[half]):
                    if tt == 0:
                        for mk in wo_marks:
                            S.dwait(sy, mk)
                    if tt >= 2:
                        S.wait(sy, "te", pe)
                    otm_l.append(S.dma(
                        sy, "sld",
                        out=otld[:, half * 2048:(half + 1) * 2048],
                        in_=scr["OTsc"].rearrange("h p t -> p h t")[
                            :, :, tt * 128:(tt + 1) * 128]))
                emit("sync", s_otl)

                res_l = []

                def g_res(g, tt=tt, half=half, py=prev_y_evac[0]):
                    if tt < 2:
                        S.wait(g, "se", pre_se)
                        for mk in pre_trm:
                            S.dwait(g, mk)
                    else:
                        S.wait(g, "ve", py)
                    src = (x_in[tt * 128:(tt + 1) * 128, :] if resid_kind == "x0"
                           else scr["x2sp"][tt, :, :])
                    res_l.append(S.dma(
                        g, "grs",
                        out=resid_buf[:, half * 2048:(half + 1) * 2048],
                        in_=src))
                emit("gpsimd", g_res)

                if lora:
                    def t_zo(t, tt=tt, half=half,
                             pz=zo_ve.get(tt - 2, 0)):
                        S.dwait(t, otm_l[-1])
                        S.wait(t, "ve", pz)
                        inst = None
                        for oc in range(16):
                            inst = t.matmul(
                                PZ[0:16, 0:128], b3(BTo, 16)[:, oc, :],
                                otld[:, half * 2048 + oc * 128:
                                     half * 2048 + (oc + 1) * 128],
                                start=(oc == 0), stop=(oc == 15))
                        S.inc(inst, "te", 1)
                    emit("tensor", t_zo)
                    te_zo = S.val("te")

                    def v_zo(v, half=half, te_zo=te_zo):
                        S.wait(v, "te", te_zo)
                        S.inc(v.tensor_copy(zob[:, half * 128:(half + 1) * 128],
                                            PZ[0:16, 0:128]), "ve", 1)
                    emit("vector", v_zo)
                    zo_ve[tt] = S.val("ve")

                def t_y(t, tt=tt, half=half,
                        zv_=zo_ve.get(tt, 0), py=prev_y_evac[0]):
                    S.dwait(t, otm_l[-1])
                    if lora:
                        S.wait(t, "ve", zv_)
                    S.wait(t, "ve", py)
                    S.dwait(t, bias_gld)
                    last = None
                    wo3 = b3(bigB, 2048)
                    for oc in range(16):
                        for dch in range(4):
                            last = t.matmul(
                                PG[dch],
                                otld[:, half * 2048 + oc * 128:
                                     half * 2048 + (oc + 1) * 128],
                                wo3[:, 4 * dch:4 * dch + 4,
                                    oc * 128:(oc + 1) * 128],
                                start=(oc == 0), stop=False)
                    for dch in range(4):
                        if lora:
                            last = t.matmul(
                                PG[dch], zob[0:16, half * 128:(half + 1) * 128],
                                ATs[0:16, dch * 512:(dch + 1) * 512],
                                start=False, stop=False)
                        last = t.matmul(
                            PG[dch], onesr[0:1, :],
                            brow[0:1, dch * 512:(dch + 1) * 512],
                            start=False, stop=True)
                    S.inc(last, "te", 1)
                emit("tensor", t_y)
                te_y = S.val("te")
                prev_otld_reader[half] = te_y

                def v_y(v, tt=tt, half=half, te_y=te_y):
                    S.wait(v, "te", te_y)
                    S.dwait(v, res_l[-1])
                    last = None
                    for dch in range(4):
                        src2 = resid_buf[:, half * 2048 + dch * 512:
                                         half * 2048 + (dch + 1) * 512]
                        last = v.tensor_tensor(
                            dst_bigA[:, tt * 2048 + dch * 512:
                                     tt * 2048 + (dch + 1) * 512],
                            PG[dch], src2, ALU.add)
                    S.inc(last, "ve", 1)
                emit("vector", v_y)
                prev_y_evac[0] = S.val("ve")
                ve_y = S.val("ve")

                if write_x2sp:
                    def s_x2(sy, tt=tt, ve_y=ve_y):
                        S.wait(sy, "ve", ve_y)
                        S.dma(sy, "st", out=scr["x2sp"][tt, :, :],
                              in_=dst_bigA[:, tt * 2048:(tt + 1) * 2048])
                    emit("sync", s_x2)
            return S.val("ve"), S.val("te")

        def load_brow(src_name, width=2048):
            mk = []

            def g_b(g):
                S.wait(g, "te")
                mk.append(S.dma(g, "gld", out=brow[0:1, 0:width],
                                in_=wts[src_name].rearrange("(o a) -> o a", o=1)))
            emit("gpsimd", g_b)
            return mk[0]

        wprep_state = {"wtr": [], "gl_ch_acc": [], "k1_se": [], "wst": [],
                       "fm": []}

        def emit_ffn_wprep_tile(ot, first_se, wnat_free, wtt_free):
            wtr = wprep_state["wtr"]
            ch_acc = wprep_state["gl_ch_acc"]
            k1_se = wprep_state["k1_se"]
            wst = wprep_state["wst"]
            fm = []

            def s_ld(sy, ot=ot):
                if ot < 2:
                    S.wait(sy, "se", first_se)
                else:
                    S.wait(sy, "ve", k1_se[ot - 2])
                fm.append(S.dma(
                    sy, "sld",
                    out=stage32[:, (ot % 2) * 2048:(ot % 2 + 1) * 2048],
                    in_=wts["fW"][ot * 128:(ot + 1) * 128, :]))
            emit("sync", s_ld)

            def v_q(v, ot=ot):
                S.dwait(v, fm[-1])
                r = S.inc(v.reduce_max(sl[:, 9:10],
                                       stage32[:, (ot % 2) * 2048:(ot % 2 + 1) * 2048],
                                       axis=AXX, apply_absolute_value=True),
                          "ve", 1)
                S.wait(v, "ve", r)
                r = S.inc(v.tensor_scalar(wscol[:, ot:ot + 1], sl[:, 9:10],
                                          1e-8, 1.0 / 127.0, ALU.max, ALU.mult),
                          "ve", 1)
                S.wait(v, "ve", r)
                S.inc(v.reciprocal(sl[:, 11:12], wscol[:, ot:ot + 1]), "ve", 1)
            emit("vector", v_q)
            ve_iw = S.val("ve")

            for ch in range(2):
                co = (ot % 2) * 2048 + ch * 1024

                def s_k1(se, co=co, ve_iw=ve_iw, ot=ot, ch=ch,
                         pva=(ch_acc[-1] if ch_acc else 0)):
                    S.wait(se, "ve", ve_iw)
                    if pva:
                        S.wait(se, "ve", pva)
                    if ot == 0 and ch == 0 and wnat_free is not None:
                        S.dwait(se, wnat_free)
                    if wtr and ch == 0:
                        S.dwait(se, wtr[-1])
                    S.inc(se.activation(k1t[:, :], stage32[:, co:co + 1024],
                                        AF.Identity, bias=cst[:, 1:2],
                                        scale=sl[:, 11:12]), "se", 1)
                emit("scalar", s_k1)
                se_k1 = S.val("se")

                def v_r(v, co=co, se_k1=se_k1, ch=ch, ot=ot):
                    S.wait(v, "se", se_k1)
                    r = S.inc(v.scalar_tensor_tensor(
                        aux32[:, 0:1024], stage32[:, co:co + 1024],
                        sl[:, 11:12], k1t[:, :], ALU.mult, ALU.subtract),
                        "ve", 1)
                    S.wait(v, "ve", r)
                    r = S.inc(v.scalar_tensor_tensor(
                        aux32[:, 1024:2048], aux32[:, 0:1024], -127.5,
                        k1t[:, :], ALU.is_ge, ALU.add), "ve", 1)
                    S.wait(v, "ve", r)
                    S.inc(v.tensor_scalar(
                        wnat[:, ch * 1024:(ch + 1) * 1024],
                        aux32[:, 1024:2048], -128.0, 0.0, ALU.add, ALU.add,
                        accum_out=sl[:, 12 + ch:13 + ch]), "ve", 1)
                emit("vector", v_r)
                ch_acc.append(S.val("ve"))

            def v_s(v, ot=ot, m=ch_acc[-1]):
                S.wait(v, "ve", m)
                S.inc(v.tensor_tensor(wsumc[:, ot:ot + 1], sl[:, 12:13],
                                      sl[:, 13:14], ALU.add), "ve", 1)
            emit("vector", v_s)
            ve_wq = S.val("ve")

            def s_tr2(sy, ot=ot, ve_wq=ve_wq):
                S.wait(sy, "ve", ve_wq)
                if ot == 0 and wtt_free is not None:
                    S.dwait(sy, wtt_free)
                if wst:
                    S.dwait(sy, wst[-1])
                wtr.append(S.dma(sy, "tr", transpose_call=True,
                                 out=b3(wtt, 128), in_=wnat[:, :]))
            emit("sync", s_tr2)

            def q_st(sq, ot=ot):
                S.dwait(sq, wtr[-1])
                wst.append(S.dma(sq, "st", out=scr["WTf"][ot, :, :],
                                 in_=wtt[:, :]))
            emit("scalar", q_st)
            k1_se.append(ve_wq)

        def emit_wprep_cols():
            def v_c2(v, pv=S.val("ve")):
                S.wait(v, "ve", pv)
                r = S.inc(v.tensor_scalar(betac[:, :], wsumc[:, :], -128.0, None,
                                          ALU.mult), "ve", 1)
                S.wait(v, "ve", r)
                r = S.inc(v.tensor_copy(c2hc[:, :], betac[:, :]), "ve", 1)
                S.wait(v, "ve", r)
                r = S.inc(v.tensor_tensor(wsumc[:, :], betac[:, :], c2hc[:, :],
                                          ALU.subtract), "ve", 1)
                S.wait(v, "ve", r)
                r = S.inc(v.tensor_copy(c2lc[:, :], wsumc[:, :]), "ve", 1)
                S.wait(v, "ve", r)
                r = S.inc(v.reciprocal(betac[:, :], wscol[:, :]), "ve", 1)
                S.wait(v, "ve", r)
                S.inc(v.tensor_tensor(betac[:, :], betac[:, :], fbcol[:, :],
                                      ALU.mult), "ve", 1)
            emit("vector", v_c2)
            return S.val("ve")

        brow_gld = load_brow("abv")
        tr_all_a = emit_ln_to_xnt("x0", bigA, otld[:, 2048:4096], (stage32, 2048))
        ve_z = emit_z_all(bigA, tr_all_a)

        wv_marks = []

        def s_wv(sy):
            S.dwait(sy, wprep_st["aWv"])
            wv_marks.append(S.dma(sy, "sld", out=bigB[:, :],
                                  in_=scr["WTva"].rearrange("t p w -> p t w")))
            S.dwait(sy, at_done[2])
            wv_marks.append(S.dma(sy, "sld", out=ATs[:, :],
                                  in_=scr["ATsc"][:, 2 * D:3 * D]))
        emit("sync", s_wv)
        vsp_done = emit_v_all(bigA, bigB, H, scr["Vspa"], tr_all_a,
                              wv_marks[0], ve_z, brow_gld, zv, wv_marks[1])

        state = {"sparity": 0, "oparity": 0,
                 "trip_last_exp": [0, 0], "ob_last_evac": [0, 0],
                 "et_last_read": [0, 0], "proj_evac_prev": 0}
        ln_a_se = S.val("se")
        pw_tr_last = pw_tr[-1] if pw_tr else None
        pw_st_last = pw_st[-1] if pw_st else None
        ot_st = []
        for h in range(H):
            hm = []

            def s_ws(sy, h=h, pe=state["proj_evac_prev"],
                     obm=max(state["ob_last_evac"])):
                if h == 0:
                    S.dwait(sy, wprep_st["aWk"])
                    for mk in vsp_done.values():
                        S.dwait(sy, mk)
                    S.dwait(sy, at_done[1])
                else:
                    S.wait(sy, "ve", max(pe, obm))
                hm.append(S.dma(sy, "sld", out=WS[0], in_=scr["WTqa"][h, :, :]))
                hm.append(S.dma(sy, "sld", out=WS[1], in_=scr["WTka"][h, :, :]))
                hm.append(S.dma(
                    sy, "sld", out=b3(Vh, 132),
                    in_=scr["Vspa"].rearrange("t p (hh w) -> p t hh w", w=132)[
                        :, :, h, :]))
                hm.append(S.dma(
                    sy, "sld", out=ATs[0:16, 0:384],
                    in_=scr["ATsc"].rearrange("r (pj o) -> r pj o", o=D)[
                        :, 0:3, h * 128:(h + 1) * 128]))
            emit("sync", s_ws)

            emit_projT(bigA, WS[0], QTh, bcolq[:, h:h + 1], 0, zq,
                       hm, PG, state, z_ve=ve_z)
            se_k = emit_projT(bigA, WS[1], KTh, bcolk[:, h:h + 1], 128, zk,
                              hm, KGB, state, z_ve=ve_z)

            ve_head = emit_attention_head(QTh, KTh, Vh, ET, False, OSg,
                                          se_k, 0, hm[2], state)

            otr = []

            def s_ot(sy, ve_head=ve_head):
                S.wait(sy, "ve", ve_head)
                if ot_st:
                    S.dwait(sy, ot_st[-1])
                otr.append(S.dma(sy, "tr", transpose_call=True,
                                 out=b3(OTt, 128), in_=OSg[:, :]))
            emit("sync", s_ot)

            def q_ot(sq, h=h):
                S.dwait(sq, otr[0])
                ot_st.append(S.dma(sq, "st", out=scr["OTsc"][h, :, :],
                                   in_=OTt[:, :]))
            emit("scalar", q_ot)
            if upto >= 2:
                emit_ffn_wprep_tile(h, ln_a_se, pw_tr_last, pw_st_last)
        attn_a_st = ot_st[-1]

        brow_gld = load_brow("abo")
        ve_x1, oproj_a_te = emit_oproj("WToa", wprep_st["aWo"], attn_a_st,
                                       brow_gld, True, "x0", stage32, bigA,
                                       False)



        def emit_rows_and_wsb(ve_cols):
            for ot in range(NT):
                def v_cp(v, ot=ot, tp=S.val("te")):
                    S.wait(v, "ve", ve_cols)
                    S.wait(v, "te", tp)
                    r = S.inc(v.tensor_copy(alc[:, 1:2], betac[:, ot:ot + 1]),
                              "ve", 1)
                    S.wait(v, "ve", r)
                    r = S.inc(v.tensor_copy(alc[:, 2:3], c2hc[:, ot:ot + 1]),
                              "ve", 1)
                    S.wait(v, "ve", r)
                    S.inc(v.tensor_copy(alc[:, 3:4], c2lc[:, ot:ot + 1]), "ve", 1)
                emit("vector", v_cp)
                ve_cp = S.val("ve")

                def t_m(t, ve_cp=ve_cp):
                    S.wait(t, "ve", ve_cp)
                    S.inc(t.matmul(PZ[0:3, 0:128], alc[:, 1:4], identb[:, :],
                                   start=True, stop=True), "te", 1)
                emit("tensor", t_m)
                te_m = S.val("te")

                def v_row(v, ot=ot, te_m=te_m):
                    S.wait(v, "te", te_m)
                    S.inc(v.tensor_copy(rowsB[0:3, ot * 128:(ot + 1) * 128],
                                        PZ[0:3, 0:128]), "ve", 1)
                emit("vector", v_row)
            for ot in range(NT):
                def t_w(t, ot=ot, vp=S.val("ve")):
                    S.wait(t, "ve", vp)
                    S.inc(t.matmul(PZ[0:1, 128:256], wscol[:, ot:ot + 1],
                                   identf[:, :], start=True, stop=True), "te", 1)
                emit("tensor", t_w)
                te_w = S.val("te")

                def v_w2(v, ot=ot, te_w=te_w):
                    S.wait(v, "te", te_w)
                    S.inc(v.tensor_copy(stage32[0:1, 2048 + ot * 128:2048 + (ot + 1) * 128],
                                        PZ[0:1, 128:256]), "ve", 1)
                emit("vector", v_w2)
            ve_wsr = S.val("ve")
            for oc in range(4):
                def t_b(t, oc=oc, vp=ve_wsr):
                    S.wait(t, "ve", vp)
                    S.inc(t.matmul(PG[oc], onesf[0:1, :],
                                   stage32[0:1, 2048 + oc * 512:2048 + (oc + 1) * 512],
                                   start=True, stop=True), "te", 1)
                emit("tensor", t_b)
                te_b = S.val("te")

                def v_b(v, oc=oc, te_b=te_b):
                    S.wait(v, "te", te_b)
                    S.inc(v.tensor_copy(aux32[:, oc * 512:(oc + 1) * 512],
                                        PG[oc]), "ve", 1)
                emit("vector", v_b)
            return S.val("ve")

        def emit_ffn(first_ve, wqt_ready, final_out):
            xq_tr = []
            mm_te = []
            xq_ve = {}
            xq_chain = []
            ep_ve = [first_ve]
            out_st = []
            for tt in range(NT):
                half = tt % 2
                src = bigA[:, tt * 2048:(tt + 1) * 2048]
                pm0 = None
                if tt < 2:
                    pm0 = [("tr", p, S.dc["tr"][p]) for p in (0, 1)
                           if S.dc["tr"][p] > 0]
                se_ln = emit_ln_tile(src, xnt[:, half * 2048:(half + 1) * 2048],
                                     wnat[:, :], pre_mark=pm0,
                                     pre_ve=(ep_ve[0] if tt < 2 else None),
                                     dst_free=(xq_tr[tt - 2] if tt >= 2 else None),
                                     dst_free_ve=(xq_ve[tt - 2] if tt >= 2
                                                  else None))

                def v_qx(v, tt=tt, half=half, se_ln=se_ln):
                    S.wait(v, "se", se_ln)
                    r = S.inc(v.reduce_max(sl[:, 9:10],
                                           xnt[:, half * 2048:(half + 1) * 2048],
                                           axis=AXX, apply_absolute_value=True),
                              "ve", 1)
                    S.wait(v, "ve", r)
                    r = S.inc(v.tensor_scalar(sl[:, 10:11], sl[:, 9:10],
                                              1.0 / 127.0, 1e-12, ALU.mult, ALU.max),
                              "ve", 1)
                    S.wait(v, "ve", r)
                    r = S.inc(v.reciprocal(sl[:, 11:12], sl[:, 10:11]), "ve", 1)
                    S.wait(v, "ve", r)
                    if mm_te:
                        S.wait(v, "te", mm_te[-1])
                    S.inc(v.tensor_copy(alc[:, 0:1], sl[:, 11:12]), "ve", 1)
                emit("vector", v_qx)
                ve_iq = S.val("ve")

                def t_al(t, ve_iq=ve_iq):
                    S.wait(t, "ve", ve_iq)
                    S.inc(t.matmul(PZ[0:1, 0:128], alc[:, 0:1], identb[:, :],
                                   start=True, stop=True), "te", 1)
                emit("tensor", t_al)
                te_al = S.val("te")

                def v_al(v, te_al=te_al):
                    S.wait(v, "te", te_al)
                    S.inc(v.tensor_copy(rowsA[0:1, :], PZ[0:1, 0:128]), "ve", 1)
                emit("vector", v_al)
                ve_al = S.val("ve")

                for ch in range(2):
                    xo = half * 2048 + ch * 1024

                    def s_k1(se, xo=xo, ve_iq=ve_iq, tt=tt, ch=ch,
                             pva=(xq_chain[-1] if xq_chain else 0)):
                        S.wait(se, "ve", ve_iq)
                        if pva:
                            S.wait(se, "ve", pva)
                        if tt > 0 and ch == 0:
                            S.dwait(se, xq_tr[-1])
                        S.inc(se.activation(k1t[:, :], xnt[:, xo:xo + 1024],
                                            AF.Identity, bias=cst[:, 1:2],
                                            scale=sl[:, 11:12]), "se", 1)
                    emit("scalar", s_k1)
                    se_k1 = S.val("se")

                    def v_r(v, xo=xo, se_k1=se_k1, ch=ch):
                        S.wait(v, "se", se_k1)
                        r = S.inc(v.scalar_tensor_tensor(
                            stage32[:, ch * 1024:(ch + 1) * 1024],
                            xnt[:, xo:xo + 1024], sl[:, 11:12], k1t[:, :],
                            ALU.mult, ALU.subtract), "ve", 1)
                        S.wait(v, "ve", r)
                        S.inc(v.scalar_tensor_tensor(
                            xnt[:, xo:xo + 1024],
                            stage32[:, ch * 1024:(ch + 1) * 1024], -127.5,
                            k1t[:, :], ALU.is_ge, ALU.add), "ve", 1)
                    emit("vector", v_r)
                    xq_chain.append(S.val("ve"))
                ve_xq = S.val("ve")
                xq_ve[tt] = ve_xq

                def s_xtr(sy, tt=tt, half=half, ve_xq=ve_xq):
                    S.wait(sy, "ve", ve_xq)
                    if len(mm_te) >= 2:
                        S.wait(sy, "te", mm_te[-2])
                    xq_tr.append(S.dma(
                        sy, "tr", transpose_call=True,
                        out=b3(otld[:, half * 2048:(half + 1) * 2048], 128),
                        in_=xnt[:, half * 2048:(half + 1) * 2048]))
                emit("sync", s_xtr)

                def t_mm(t, tt=tt, half=half, ve_al=ve_al):
                    S.dwait(t, xq_tr[-1])
                    S.wait(t, "ve", ve_al)
                    if tt == 0:
                        for mk in wqt_ready:
                            S.dwait(t, mk)
                    S.wait(t, "ve", ep_ve[0])
                    last = None
                    wq3 = b3(bigB, 2048)
                    for dc in range(ND):
                        for oc in range(4):
                            last = t.matmul(
                                PG[oc],
                                b3(otld[:, half * 2048:(half + 1) * 2048], 128)[:, dc, :],
                                wq3[:, 4 * oc:4 * oc + 4,
                                    dc * 128:(dc + 1) * 128],
                                start=(dc == 0), stop=False)
                    for oc in range(4):
                        last = t.matmul(PG[oc], rowsA[0:3, :],
                                        rowsB[0:3, oc * 512:(oc + 1) * 512],
                                        start=False, stop=True)
                    S.inc(last, "te", 1)
                emit("tensor", t_mm)
                mm_te.append(S.val("te"))

                def v_ep(v, tt=tt, half=half, m=mm_te[-1]):
                    S.wait(v, "te", m)
                    if final_out and tt >= 1:
                        S.dwait(v, out_st[tt - 1])
                    last = None
                    for oc in range(4):
                        r = S.inc(v.scalar_tensor_tensor(
                            stage32[:, 2048 + oc * 512:2048 + (oc + 1) * 512],
                            PG[oc], sl[:, 10:11], aux32[:, oc * 512:(oc + 1) * 512],
                            ALU.mult, ALU.mult), "ve", 1)
                        S.wait(v, "ve", r)
                        last = v.tensor_tensor(
                            (stage32[:, 2048 + oc * 512:2048 + (oc + 1) * 512]
                             if final_out else
                             bigA[:, tt * 2048 + oc * 512:tt * 2048 + (oc + 1) * 512]),
                            stage32[:, 2048 + oc * 512:2048 + (oc + 1) * 512],
                            bigA[:, tt * 2048 + oc * 512:tt * 2048 + (oc + 1) * 512],
                            ALU.add)
                    S.inc(last, "ve", 1)
                emit("vector", v_ep)
                ep_ve[0] = S.val("ve")
                ve_y = S.val("ve")

                if final_out:
                    def s_o(sy, tt=tt, ve_y=ve_y):
                        S.wait(sy, "ve", ve_y)
                        out_st.append(S.dma(sy, "st",
                                            out=out_t[tt * 128:(tt + 1) * 128, :],
                                            in_=stage32[:, 2048:4096]))
                    emit("sync", s_o)
                else:
                    def s_o(sy, tt=tt, ve_y=ve_y):
                        S.wait(sy, "ve", ve_y)
                        out_st.append(S.dma(sy, "st",
                                            out=scr["x2sp"][tt, :, :],
                                            in_=bigA[:, tt * 2048:(tt + 1) * 2048]))
                    emit("sync", s_o)
            return S.val("ve"), out_st, mm_te[-1]

        if upto >= 2:
            ve_cols = emit_wprep_cols()
            ve_rows = emit_rows_and_wsb(ve_cols)
            wst = wprep_state["wst"]

            wq_marks = []

            def s_wq(sy):
                S.dwait(sy, wst[-1])
                S.dwait(sy, wst[-2])
                S.wait(sy, "te", oproj_a_te)
                wq_marks.append(S.dma(sy, "sld", out=bigB[:, :],
                                      in_=scr["WTf"].rearrange("t p w -> p t w")))
            emit("sync", s_wq)

            ve_x2, x2_st, ffn1_te = emit_ffn(max(ve_x1, ve_rows), wq_marks, False)

        if upto >= 3:
            def g_tr(g):
                S.wait(g, "ve", ve_x2)
                S.inc(g.iota(stage32[0:1, 0:2048], pattern=[[1, 2048]], base=0,
                             channel_multiplier=0,
                             allow_small_or_imprecise_dtypes=True), "gp", 1)
            emit("gpsimd", g_tr)
            gp_tr = S.val("gp")

            def v_dp(v):
                S.wait(v, "ve", ve_x2)
                r = S.inc(v.tensor_scalar(k1t[:, 0:1], iotp[:, 0:1], 0.015625,
                                          None, ALU.mult), "ve", 1)
                S.wait(v, "ve", r)
                r = S.inc(v.scalar_tensor_tensor(sl[:, 9:10], k1t[:, 0:1], -64.0,
                                                 iotp[:, 0:1], ALU.mult, ALU.add),
                          "ve", 1)
                S.wait(v, "ve", r)
                r = S.inc(v.tensor_scalar(sl[:, 10:11], sl[:, 9:10], 0.0, None,
                                          ALU.is_lt), "ve", 1)
                S.wait(v, "ve", r)
                S.inc(v.scalar_tensor_tensor(sl[:, 11:12], sl[:, 10:11], 64.0,
                                             sl[:, 9:10], ALU.mult, ALU.add),
                      "ve", 1)
            emit("vector", v_dp)
            ve_dp = S.val("ve")

            def s_inv(se):
                S.wait(se, "ve", ve_dp)
                S.inc(se.activation(sl[:, 12:13], sl[:, 11:12], AF.Exp,
                                    bias=cst[:, 4:5], scale=EXPSC), "se", 1)
            emit("scalar", s_inv)
            se_inv = S.val("se")

            def v_invc(v, se_inv=se_inv, tp=S.val("te")):
                S.wait(v, "se", se_inv)
                S.wait(v, "te", tp)
                S.inc(v.tensor_copy(sl[:, 13:14], sl[:, 12:13]), "ve", 1)
            emit("vector", v_invc)
            ve_invc = S.val("ve")

            def t_inv(t, ve_invc=ve_invc):
                S.wait(t, "ve", ve_invc)
                S.inc(t.matmul(PZ[0:1, 128:256], sl[:, 13:14], identf[:, :],
                               start=True, stop=True), "te", 1)
            emit("tensor", t_inv)
            te_inv = S.val("te")

            def v_invr(v, te_inv=te_inv):
                S.wait(v, "te", te_inv)
                S.inc(v.tensor_copy(stage32[0:1, 2048:2176], PZ[0:1, 128:256]),
                      "ve", 1)
            emit("vector", v_invr)
            ve_invr = S.val("ve")

            th_ve = []
            for tc in range(4):
                def t_th(t, tc=tc, ve_invr=ve_invr, gp_tr=gp_tr):
                    S.wait(t, "ve", ve_invr)
                    S.wait(t, "gp", gp_tr)
                    if tc >= 1:
                        S.wait(t, "ve", th_ve[-1])
                    S.inc(t.matmul(PG[tc], stage32[0:1, 2048:2176],
                                   stage32[0:1, tc * 512:(tc + 1) * 512],
                                   start=True, stop=True), "te", 1)
                emit("tensor", t_th)
                te_th = S.val("te")

                def v_th(v, tc=tc, te_th=te_th):
                    S.wait(v, "te", te_th)
                    S.inc(v.tensor_copy(aux32[:, tc * 512:(tc + 1) * 512],
                                        PG[tc]), "ve", 1)
                emit("vector", v_th)
                th_ve.append(S.val("ve"))
            sin_done = []
            for part in range(2):
                for ch in range(2):
                    o = ch * 1024

                    def v_u(v, part=part, ch=ch, o=o, pv=th_ve[-1],
                            sins=list(sin_done)):
                        S.wait(v, "ve", pv)
                        for m in sins:
                            S.wait(v, "se", m)
                        if part == 0:
                            r = S.inc(v.tensor_scalar(
                                stage32[:, 2048 + o:2048 + o + 1024],
                                aux32[:, o:o + 1024], 0.0, None, ALU.add), "ve", 1)
                        else:
                            r = S.inc(v.tensor_scalar(
                                stage32[:, 2048 + o:2048 + o + 1024],
                                aux32[:, o:o + 1024], PI2_HI, PI2_LO,
                                ALU.add, ALU.add), "ve", 1)
                        S.wait(v, "ve", r)
                        r = S.inc(v.tensor_scalar(
                            k1t[:, :], stage32[:, 2048 + o:2048 + o + 1024],
                            INV_2PI, None, ALU.mult), "ve", 1)
                        S.wait(v, "ve", r)
                        r = S.inc(v.scalar_tensor_tensor(
                            stage32[:, 2048 + o:2048 + o + 1024], k1t[:, :],
                            -TWO_PI_HI, stage32[:, 2048 + o:2048 + o + 1024],
                            ALU.mult, ALU.add), "ve", 1)
                        S.wait(v, "ve", r)
                        r = S.inc(v.scalar_tensor_tensor(
                            stage32[:, 2048 + o:2048 + o + 1024], k1t[:, :],
                            -TWO_PI_LO, stage32[:, 2048 + o:2048 + o + 1024],
                            ALU.mult, ALU.add), "ve", 1)
                        S.wait(v, "ve", r)
                        r = S.inc(v.tensor_scalar(
                            k1t[:, :], stage32[:, 2048 + o:2048 + o + 1024],
                            PI_F32, None, ALU.is_ge), "ve", 1)
                        S.wait(v, "ve", r)
                        r = S.inc(v.scalar_tensor_tensor(
                            stage32[:, 2048 + o:2048 + o + 1024], k1t[:, :],
                            -TWO_PI_HI, stage32[:, 2048 + o:2048 + o + 1024],
                            ALU.mult, ALU.add), "ve", 1)
                        S.wait(v, "ve", r)
                        r = S.inc(v.scalar_tensor_tensor(
                            stage32[:, 2048 + o:2048 + o + 1024], k1t[:, :],
                            -TWO_PI_LO, stage32[:, 2048 + o:2048 + o + 1024],
                            ALU.mult, ALU.add), "ve", 1)
                        S.wait(v, "ve", r)
                        r = S.inc(v.tensor_scalar(
                            k1t[:, :], stage32[:, 2048 + o:2048 + o + 1024],
                            -PI_F32, None, ALU.is_lt), "ve", 1)
                        S.wait(v, "ve", r)
                        r = S.inc(v.scalar_tensor_tensor(
                            stage32[:, 2048 + o:2048 + o + 1024], k1t[:, :],
                            TWO_PI_HI, stage32[:, 2048 + o:2048 + o + 1024],
                            ALU.mult, ALU.add), "ve", 1)
                        S.wait(v, "ve", r)
                        r = S.inc(v.scalar_tensor_tensor(
                            stage32[:, 2048 + o:2048 + o + 1024], k1t[:, :],
                            TWO_PI_LO, stage32[:, 2048 + o:2048 + o + 1024],
                            ALU.mult, ALU.add), "ve", 1)
                        S.wait(v, "ve", r)
                        S.inc(v.tensor_scalar(
                            stage32[:, 2048 + o:2048 + o + 1024],
                            stage32[:, 2048 + o:2048 + o + 1024],
                            -PI_CLAMP, PI_CLAMP, ALU.max, ALU.min), "ve", 1)
                    emit("vector", v_u)
                    ve_u = S.val("ve")

                    def s_sin(se, part=part, ch=ch, o=o, ve_u=ve_u,
                              first=(part == 0 and ch == 0)):
                        S.wait(se, "ve", ve_u)
                        if first:
                            preload_tbl(se, _TRIG)
                        S.inc(se.activation(
                            stage32[:, o:o + 1024] if part == 0 else
                            stage32[:, 2048 + o:2048 + o + 1024],
                            stage32[:, 2048 + o:2048 + o + 1024], AF.Sin,
                            bias=cst[:, 4:5], scale=1.0), "se", 1)
                    emit("scalar", s_sin)
                    sin_done.append(S.val("se"))
            def s_rest(se):
                for m in sin_done:
                    S.wait(se, "se", m)
                preload_tbl(se, _NLE)
            emit("scalar", s_rest)

            def v_ct(v, sd=sin_done[-1]):
                for m in sin_done:
                    S.wait(v, "se", m)
                S.inc(v.tensor_tensor(aux32[:, :], stage32[:, 0:2048],
                                      stage32[:, 2048:4096], ALU.add), "ve", 1)
            emit("vector", v_ct)
            ve_ct = S.val("ve")

            tr_all_b = emit_ln_to_xnt("x2sp", bigA, wnat[:, :], (otld, 2048),
                                      src_ready=x2_st, dst_free_marks=x2_st[-2:])

            kv_marks = []

            def s_kv(sy):
                S.wait(sy, "te", ffn1_te)
                S.dwait(sy, wprep_st["bWk"])
                S.dwait(sy, wprep_st["bWv"])
                kv_marks.append(S.dma(sy, "sld", out=bigB[:, 0:8192],
                                      in_=scr["WTkb"].rearrange("t p w -> p t w")))
                kv_marks.append(S.dma(sy, "sld", out=bigB[:, 8192:16384],
                                      in_=scr["WTvb"].rearrange("t p w -> p t w")))
            emit("sync", s_kv)

            brow_gld_b = load_brow("bbv", 512)

            kt_st = []
            state_b = {"sparity": 0, "oparity": 0,
                       "trip_last_exp": [0, 0], "ob_last_evac": [0, 0],
                       "et_last_read": [0, 0], "proj_evac_prev": 0}
            for g in range(KVH):
                def t_kg(t, g=g, pe=S.val("se"), pv=S.val("ve")):
                    if g == 0:
                        for mk in tr_all_b:
                            S.dwait(t, mk)
                        S.dwait(t, kv_marks[0])
                        S.wait(t, "ve", ve_x2)
                    else:
                        S.wait(t, "ve", pv)
                    last = None
                    for dc in range(ND):
                        for tc2 in range(4):
                            last = t.matmul(
                                PG[tc2],
                                bigB[:, g * 2048 + dc * 128:g * 2048 + (dc + 1) * 128],
                                bigA[:, dc * 2048 + tc2 * 512:dc * 2048 + (tc2 + 1) * 512],
                                start=(dc == 0), stop=(dc == ND - 1))
                    S.inc(last, "te", 1)
                emit("tensor", t_kg)
                te_kg = S.val("te")

                def s_ke(se, g=g, te_kg=te_kg):
                    S.wait(se, "te", te_kg)
                    if kt_st:
                        S.dwait(se, kt_st[-1])
                    last = None
                    for tc2 in range(4):
                        last = se.activation(KTh[:, tc2 * 512:(tc2 + 1) * 512],
                                             PG[tc2], AF.Identity,
                                             bias=bkbc[:, g:g + 1], scale=1.0)
                    S.inc(last, "se", 1)
                emit("scalar", s_ke)
                se_ke = S.val("se")

                def v_kr(v, g=g, se_ke=se_ke):
                    S.wait(v, "se", se_ke)
                    S.wait(v, "ve", ve_ct)
                    S.inc(v.tensor_tensor(KTh[:, :], KTh[:, :], aux32[:, :],
                                          ALU.mult), "ve", 1)
                emit("vector", v_kr)
                ve_kr = S.val("ve")

                def s_ks(sy, g=g, ve_kr=ve_kr):
                    S.wait(sy, "ve", ve_kr)
                    kt_st.append(S.dma(sy, "st", out=scr["KTbs"][g, :, :],
                                       in_=KTh[:, :]))
                emit("sync", s_ks)
            kt_last = S.val("ve")

            vsp_b = emit_v_all(bigA, bigB[:, 8192:16384], KVH, scr["Vspb"],
                               tr_all_b, kv_marks[1], kt_last, brow_gld_b,
                               None, None)

            oht_st = []
            for h in range(H):
                g = h // 4
                hm2 = []

                def s_wsb(sy, h=h, g=g, pe=state_b["proj_evac_prev"],
                          obm=max(state_b["ob_last_evac"])):
                    if h == 0:
                        S.dwait(sy, wprep_st["bWq"])
                        for mk in vsp_b.values():
                            S.dwait(sy, mk)
                    else:
                        S.wait(sy, "ve", max(pe, obm))
                    hm2.append(S.dma(sy, "sld", out=WS[0],
                                     in_=scr["WTqb"][h, :, :]))
                    if h % 4 == 0:
                        S.dwait(sy, kt_st[g])
                        hm2.append(S.dma(sy, "sld", out=KTh[:, :],
                                         in_=scr["KTbs"][g, :, :]))
                        hm2.append(S.dma(
                            sy, "sld", out=b3(Vh, 132),
                            in_=scr["Vspb"].rearrange(
                                "t p (hh w) -> p t hh w", w=132)[:, :, g, :]))
                emit("sync", s_wsb)
                vh_m = hm2[2] if h % 4 == 0 else last_vh[0]
                kt_m = hm2[1] if h % 4 == 0 else None
                if h % 4 == 0:
                    last_vh = [hm2[2]]

                emit_projT(bigA, WS[0], QTh, bqbc[:, h:h + 1], None, None,
                           [hm2[0]], PG, state_b, rope=True, z_ve=ve_ct)
                se_q = state_b["rope_ve"]
                ve_rope = 0

                ve_head = emit_attention_head(QTh, KTh, Vh, ET, True, OSg,
                                              se_q, ve_rope, vh_m, state_b,
                                              kt_mark=kt_m)

                otr2 = []

                def s_ot2(sy, ve_head=ve_head):
                    S.wait(sy, "ve", ve_head)
                    if oht_st:
                        S.dwait(sy, oht_st[-1])
                    otr2.append(S.dma(sy, "tr", transpose_call=True,
                                      out=b3(OTt, 128), in_=OSg[:, :]))
                emit("sync", s_ot2)

                def q_ot2(sq, h=h):
                    S.dwait(sq, otr2[0])
                    oht_st.append(S.dma(sq, "st", out=scr["OTsc"][h, :, :],
                                        in_=OTt[:, :]))
                emit("scalar", q_ot2)

            brow_gld2 = load_brow("bbo")
            ve_x3, oproj_b_te = emit_oproj("WTob", wprep_st["bWo"], oht_st[-1],
                                           brow_gld2, False, "x2sp", xnt, bigA,
                                           False)

        if upto >= 4:
            ve_rows2 = emit_rows_and_wsb(ve_x3)
            wq2_marks = []

            def s_wq2(sy):
                S.wait(sy, "te", oproj_b_te)
                wq2_marks.append(S.dma(sy, "sld", out=bigB[:, :],
                                       in_=scr["WTf"].rearrange("t p w -> p t w")))
            emit("sync", s_wq2)

            ve_x4, out_st4, _ = emit_ffn(max(ve_x3, ve_rows2), wq2_marks, True)

        def emit_debug_out(src_big):
            dbg_st = []
            for tt in range(NT):
                half = tt % 2

                def s_cp(se, tt=tt, half=half):
                    S.wait(se, "ve")
                    if tt >= 2:
                        S.dwait(se, dbg_st[tt - 2])
                    S.inc(se.activation(
                        stage32[:, half * 2048:(half + 1) * 2048],
                        src_big[:, tt * 2048:(tt + 1) * 2048],
                        AF.Identity, bias=cst[:, 4:5], scale=1.0), "se", 1)
                emit("scalar", s_cp)
                se_c = S.val("se")

                def s_st(sy, tt=tt, half=half, se_c=se_c):
                    S.wait(sy, "se", se_c)
                    dbg_st.append(S.dma(
                        sy, "st", out=out_t[tt * 128:(tt + 1) * 128, :],
                        in_=stage32[:, half * 2048:(half + 1) * 2048]))
                emit("sync", s_st)

        if upto in (1, 2, 3):
            emit_debug_out(bigA)

        def fin(sy):
            for nm in ["st", "gld", "sld", "tr", "glx", "grs"]:
                S.dwait_all(sy, nm)
            S.wait(sy, "ve")
            S.wait(sy, "se")
            S.wait(sy, "te")
        emit("sync", fin)

    return nc


def make_inputs_per_core(inputs, core):
    x = np.ascontiguousarray(inputs["L0_attn_a_x"][core])
    dm = np.tril(np.ones((128, 128), dtype=np.float32))
    m = {
        "x": x, "dmaski": dm,
        "aWq": inputs["L0_attn_a_Wq"], "aWk": inputs["L0_attn_a_Wk"],
        "aWv": inputs["L0_attn_a_Wv"], "aWo": inputs["L0_attn_a_Wo"],
        "abq": inputs["L0_attn_a_bq"], "abk": inputs["L0_attn_a_bk"],
        "abv": inputs["L0_attn_a_bv"], "abo": inputs["L0_attn_a_bo"],
        "aBq": inputs["L0_attn_a_Bq"], "aBk": inputs["L0_attn_a_Bk"],
        "aBv": inputs["L0_attn_a_Bv"], "aBo": inputs["L0_attn_a_Bo"],
        "aAq": inputs["L0_attn_a_Aq"], "aAk": inputs["L0_attn_a_Ak"],
        "aAv": inputs["L0_attn_a_Av"], "aAo": inputs["L0_attn_a_Ao"],
        "bWq": inputs["L0_attn_b_Wq"], "bWk": inputs["L0_attn_b_Wk"],
        "bWv": inputs["L0_attn_b_Wv"], "bWo": inputs["L0_attn_b_Wo"],
        "bbq": inputs["L0_attn_b_bq"], "bbk": inputs["L0_attn_b_bk"],
        "bbv": inputs["L0_attn_b_bv"], "bbo": inputs["L0_attn_b_bo"],
        "fW": inputs["L0_ffn_weight"], "fb": inputs["L0_ffn_bias"],
    }
    return {k: np.ascontiguousarray(v) for k, v in m.items()}


def kernel(**inputs):
    nc = build_nc(upto=4)
    in_maps = [make_inputs_per_core(inputs, c) for c in range(8)]
    res = run_bass_kernel_spmd(nc, in_maps, core_ids=list(range(8)))
    return np.stack([res.results[c]["out"] for c in range(8)], axis=0)
